# version 16
# baseline (speedup 1.0000x reference)
import numpy as np
import ml_dtypes
import concourse.bass as bass
import concourse.mybir as mybir
from concourse.bass_utils import run_bass_kernel_spmd

F32 = mybir.dt.float32
BF16 = mybir.dt.bfloat16
I32 = mybir.dt.int32
U32 = mybir.dt.uint32
AF = mybir.ActivationFunctionType
ALU = mybir.AluOpType
AX = mybir.AxisListType
BF = ml_dtypes.bfloat16
IOOA = bass.IndirectOffsetOnAxis

P = 128
D = 1024
DC = 8
NB = 2
SEQ = 2048
NTOK = 4096
NT = 32
NE = 8
CAP = 2048
GCAP = 16384
FF = 4096
FC = 32
NH = 16
BIGIDX = 1.0e7

PHASES = ("ffnA", "attn", "ffnC")
DEBUG_TAPS = False

ENGINES = ("tensor", "vector", "scalar", "gpsimd", "sync")
ESEM = {"tensor": "pe", "vector": "dve", "scalar": "act", "gpsimd": "pool", "sync": "sp"}
QSEM = {"sync": "qsp", "scalar": "qact", "gpsimd": "qpool"}


class Prog:

    QROT = {"qsp": 24, "qact": 6, "qpool": 24}

    def __init__(self, nc):
        self.nc = nc
        self.ops = []
        self.eng_ops = {e: [] for e in ENGINES}
        self.last_writer = {}
        self.readers = {}
        names = set(ESEM.values())
        for q, k in self.QROT.items():
            for i in range(k):
                names.add(f"{q}{i}")
        self.sems = {s: nc.alloc_semaphore(s) for s in names}

    def op(self, eng, fn, reads=(), writes=(), dma=False, tag=""):
        o = dict(eng=eng, fn=fn, dma=dma, tag=tag, deps=[], need_inc=False,
                 seq=len(self.ops))
        for r in reads:
            w = self.last_writer.get(r)
            if w is not None:
                o["deps"].append(w)
        for w_ in writes:
            lw = self.last_writer.get(w_)
            if lw is not None:
                o["deps"].append(lw)
            for rd in self.readers.get(w_, ()):
                o["deps"].append(rd)
        for w_ in writes:
            self.last_writer[w_] = o
            self.readers[w_] = []
        for r in reads:
            self.readers.setdefault(r, []).append(o)
        self.ops.append(o)
        self.eng_ops[eng].append(o)
        return o

    def barrier(self, tag="bar"):
        tails = [ops[-1] for ops in self.eng_ops.values() if ops]
        for eng in ENGINES:
            o = self.op(eng, None, tag=tag)
            o["deps"] = list(tails)

    def finalize(self):
        for o in self.ops:
            dd, seen = [], set()
            for d in o["deps"]:
                if d is o:
                    continue
                if (d["eng"] == "tensor" and o["eng"] == "tensor"
                        and not d["dma"] and not o["dma"]):
                    continue
                if id(d) in seen:
                    continue
                seen.add(id(d))
                dd.append(d)
                d["need_inc"] = True
            o["deps"] = dd
        cnt = {s: 0 for s in self.sems}
        qidx = {q: 0 for q in self.QROT}
        for o in self.ops:
            if o["fn"] is None:
                o["sem"] = None
                o["incval"] = None
                continue
            if o["dma"]:
                q = QSEM[o["eng"]]
                i = qidx[q]
                qidx[q] += 1
                s = f"{q}{i % self.QROT[q]}"
                cnt[s] += 16
                o["sem"], o["wait_target"], o["incval"] = s, cnt[s], 16
            else:
                s = ESEM[o["eng"]]
                if o["need_inc"]:
                    cnt[s] += 1
                    o["sem"], o["wait_target"], o["incval"] = s, cnt[s], 1
                else:
                    o["sem"], o["incval"] = None, None
        self.final_counts = dict(cnt)
        for o in self.ops:
            w = {}
            stack = list(o["deps"])
            depth = 0
            while stack and depth < 100000:
                depth += 1
                d = stack.pop()
                if d.get("sem") is None:
                    stack.extend(d["deps"])
                    continue
                prev = w.get(d["sem"], 0)
                if d["wait_target"] > prev:
                    w[d["sem"]] = d["wait_target"]
            o["waits"] = w
        for eng in ENGINES:
            have = {}
            for o in self.eng_ops[eng]:
                pruned = {}
                for s, v in o.get("waits", {}).items():
                    if have.get(s, 0) < v:
                        pruned[s] = v
                        have[s] = v
                o["waits"] = pruned
        return self

    def emit(self):
        prog = self
        fc = prog.final_counts

        def mk(eng_name):
            def body(eng):
                for o in prog.eng_ops[eng_name]:
                    for s, v in sorted(o.get("waits", {}).items()):
                        eng.wait_ge(prog.sems[s], v)
                    if o["fn"] is None:
                        continue
                    inst = o["fn"](eng)
                    if o["incval"]:
                        inst.then_inc(prog.sems[o["sem"]], o["incval"])
                if eng_name == "sync":
                    for s, v in sorted(fc.items()):
                        if v and (s[0] == "q"):
                            eng.wait_ge(prog.sems[s], v)
            return body

        with self.nc.Block() as blk:
            blk.tensor(mk("tensor"))
            blk.vector(mk("vector"))
            blk.scalar(mk("scalar"))
            blk.gpsimd(mk("gpsimd"))
            blk.sync(mk("sync"))


def build_program(phases=PHASES, debug=None):
    global DEBUG_TAPS
    if debug is not None:
        DEBUG_TAPS = debug
    nc = bass.Bass("TRN2", target_bir_lowering=False)

    x_in = nc.dram_tensor("x", [NTOK, D], F32, kind="ExternalInput").ap()
    wq_d = nc.dram_tensor("wq", [D, D], F32, kind="ExternalInput").ap()
    wk_d = nc.dram_tensor("wk", [D, D], F32, kind="ExternalInput").ap()
    wv_d = nc.dram_tensor("wv", [D, D], F32, kind="ExternalInput").ap()
    wo_d = nc.dram_tensor("wo", [D, D], F32, kind="ExternalInput").ap()
    aw1_d = nc.dram_tensor("aw1", [D, 512], F32, kind="ExternalInput").ap()
    ab1_d = nc.dram_tensor("ab1", [1, 512], F32, kind="ExternalInput").ap()
    aw2_d = nc.dram_tensor("aw2", [512, D], F32, kind="ExternalInput").ap()
    ab2_d = nc.dram_tensor("ab2", [1, D], F32, kind="ExternalInput").ap()
    wg_d = nc.dram_tensor("wg", [D, NE], F32, kind="ExternalInput").ap()
    fw1_d = nc.dram_tensor("fw1", [NE, D, FF], F32, kind="ExternalInput").ap()
    fb1_d = nc.dram_tensor("fb1", [NE, FF], F32, kind="ExternalInput").ap()
    fw2_d = nc.dram_tensor("fw2", [NE, FF, D], F32, kind="ExternalInput").ap()
    fb2_d = nc.dram_tensor("fb2", [NE, D], F32, kind="ExternalInput").ap()
    selrank_d = nc.dram_tensor("selrank", [NE, 2], F32, kind="ExternalInput").ap()
    out_d = nc.dram_tensor("out", [NTOK, D], F32, kind="ExternalOutput").ap()
    dbg = {}
    if DEBUG_TAPS:
        for nm, sh, dt in [("idxf", [P, 2 * NT], F32), ("posl", [P, 2 * NT], F32),
                           ("posg", [P, 2 * NT], F32), ("keep", [P, 2 * NT], F32),
                           ("wk", [P, 2 * NT], F32), ("S8", [P, NT], F32),
                           ("m12", [P, 2 * NT], F32), ("idxi", [P, 2 * NT], I32),
                           ("masks", [P, 16 * NT], F32)]:
            dbg[nm] = nc.dram_tensor("dbg_" + nm, sh, dt, kind="ExternalOutput").ap()
        dbg["xe"] = nc.dram_tensor("dbg_xe", [NE * CAP, D], BF16, kind="ExternalOutput").ap()
        dbg["xet0"] = nc.dram_tensor("dbg_xet0", [P, DC * 1024], BF16, kind="ExternalOutput").ap()
        dbg["cp"] = nc.dram_tensor("dbg_cp", [P, 2048], BF16, kind="ExternalOutput").ap()
        dbg["kc"] = nc.dram_tensor("dbg_kc", [P, 2048], BF16, kind="ExternalOutput").ap()
        dbg["vc"] = nc.dram_tensor("dbg_vc", [P, 2048], BF16, kind="ExternalOutput").ap()
        dbg["h1c"] = nc.dram_tensor("dbg_h1c", [P, 4096], BF16, kind="ExternalOutput").ap()
        dbg["h0"] = nc.dram_tensor("dbg_h0", [P, FC * 1024], BF16, kind="ExternalOutput").ap()
        dbg["ye"] = nc.dram_tensor("dbg_ye", [NE * CAP, D], F32, kind="ExternalOutput").ap()

    il = nc.inline_tensor
    ident_b = il(np.eye(P, dtype=BF), "c_identb_d").ap()
    ident_f = il(np.eye(P, dtype=np.float32), "c_identf_d").ap()
    ut128f_d = il(np.triu(np.ones((P, P), dtype=np.float32), 1), "c_ut128_d").ap()
    ut32f_d = il(np.triu(np.ones((32, 32), dtype=np.float32), 1), "c_ut32_d").ap()
    iota8_d = il(np.broadcast_to(np.arange(NE, dtype=np.float32), (P, NE)).copy(),
                 "c_iota8_d").ap()
    onesrow_f_d = il(np.ones((1, P), dtype=np.float32), "c_onesrowf_d").ap()
    onesrow_b_d = il(np.ones((1, P), dtype=BF), "c_onesrowb_d").ap()
    onescol_f_d = il(np.ones((P, 1), dtype=np.float32), "c_onescolf_d").ap()
    onescol32_d = il(np.ones((32, 1), dtype=np.float32), "c_onescol32_d").ap()
    qq, kk = np.meshgrid(np.arange(P), np.arange(P), indexing="ij")
    diag_np = np.where(kk <= qq, 0.0, -1e30).astype(np.float32)
    diagm_d = il(diag_np, "c_diagm_d").ap()
    diagmT_d = il(np.ascontiguousarray(diag_np.T), "c_diagmT_d").ap()

    xA = nc.dram_tensor("xA", [NTOK, D], F32, kind="Internal").ap()
    xB = nc.dram_tensor("xB", [NTOK, D], F32, kind="Internal").ap()
    xe_dram = nc.dram_tensor("xe_dram", [NE * CAP, D], BF16, kind="Internal").ap()
    ye_dram = nc.dram_tensor("ye_dram", [NE * CAP, D], F32, kind="Internal").ap()
    cc_in = nc.dram_tensor("cc_in", [1, 16], F32, kind="Internal").ap()
    cc_out = nc.dram_tensor("cc_out", [NE, 16], F32, kind="Internal").ap()
    wqb = nc.dram_tensor("wqb", [D, D], BF16, kind="Internal").ap()
    wkb = nc.dram_tensor("wkb", [D, D], BF16, kind="Internal").ap()
    wvb = nc.dram_tensor("wvb", [D, D], BF16, kind="Internal").ap()
    wob = nc.dram_tensor("wob", [D, D], BF16, kind="Internal").ap()
    aw1b = nc.dram_tensor("aw1b", [D, 512], BF16, kind="Internal").ap()
    aw2b = nc.dram_tensor("aw2b", [512, D], BF16, kind="Internal").ap()
    fw1b = nc.dram_tensor("fw1b", [NE, D, FF], BF16, kind="Internal").ap()
    fw2b = nc.dram_tensor("fw2b", [NE, FF, D], BF16, kind="Internal").ap()

    pr = Prog(nc)
    sb = nc.alloc_sbuf_tensor

    c_identb = sb("c_identb", [P, P], BF16)
    c_identf = sb("c_identf", [P, P], F32)
    c_ut128f = sb("c_ut128f", [P, P], F32)
    c_ut32f = sb("c_ut32f", [32, 32], F32)
    c_iota8 = sb("c_iota8", [P, NE], F32)
    c_onesrow_f = sb("c_onesrow_f", [1, P], F32)
    c_onesrow_b = sb("c_onesrow_b", [1, P], BF16)
    c_onescol_f = sb("c_onescol_f", [P, 1], F32)
    c_onescol32_f = sb("c_onescol32_f", [32, 1], F32)
    c_diagm = sb("c_diagm", [P, P], F32)
    c_diagmT = sb("c_diagmT", [P, P], F32)
    c_selrank = sb("c_selrank", [NE, 2], F32)
    s_wg = sb("s_wg", [P, DC * NE], F32)
    s_fb1t = sb("s_fb1t", [P, NE * FC], F32)
    s_ab1t = sb("s_ab1t", [P, 4], F32)
    s_ab2t = sb("s_ab2t", [P, 8], F32)

    s_h = sb("s_h", [P, 32768], BF16)
    s_w2 = sb("s_w2", [P, 32768], BF16)
    s_xet = sb("s_xet", [P, 8192], BF16)
    s_xtok = sb("s_xtok", [P, 4096], BF16)
    s_w1w = [sb(f"s_w1w{i}", [P, 4096], BF16) for i in range(2)]

    s_scrA = sb("s_scrA", [P, 1024], F32)
    s_scrB = sb("s_scrB", [P, 1024], F32)
    s_xt = [sb(f"s_xt{i}", [P, D], F32) for i in range(2)]
    s_yest = [sb(f"s_yest{i}", [P, D], F32) for i in range(2)]

    s_rows = sb("s_rows", [1, 1024], F32)
    v_rep = s_rows[0:1, 0:512]
    v_toffrow = s_rows[0:1, 512:1024]
    v_rrow = s_rows[:].bitcast(BF16)

    s_mean = sb("s_mean", [P, NT], F32)
    s_ssq = sb("s_ssq", [P, NT], F32)
    s_rstd = sb("s_rstd", [P, NT], F32)
    s_negm1 = sb("s_negm1", [P, NT], F32)
    s_S8 = sb("s_S8", [P, NT], F32)
    s_m12 = sb("s_m12", [P, 2 * NT], F32)
    s_idxf = sb("s_idxf", [P, 2 * NT], F32)
    s_masksf = sb("s_masksf", [P, 16 * NT], F32)
    s_posl = sb("s_posl", [P, 2 * NT], F32)
    s_posg = sb("s_posg", [P, 2 * NT], F32)
    s_keep = sb("s_keep", [P, 2 * NT], F32)
    s_wk_ = sb("s_wk", [P, 2 * NT], F32)
    s_idxi = sb("s_idxi", [P, 2 * NT], I32)
    s_ta = sb("s_ta", [P, 2 * NT], F32)
    s_tb = sb("s_tb", [P, 2 * NT], F32)
    s_l8p = [sb(f"s_l8p{i}", [P, NE], F32) for i in range(2)]
    s_mx8p = [sb(f"s_mx8p{i}", [P, NE], F32) for i in range(2)]
    s_ix8p = [sb(f"s_ix8p{i}", [P, NE], U32) for i in range(2)]
    s_junk8 = sb("s_junk8", [P, NE], F32)
    s_t32 = sb("s_t32", [32, 16], F32)
    s_toff = sb("s_toff", [32, 16], F32)
    s_lt = sb("s_lt", [1, 16], F32)
    s_ag = sb("s_ag", [NE, 16], F32)
    s_s1 = sb("s_s1", [1, 16], F32)
    s_s2 = sb("s_s2", [1, 16], F32)
    s_goff = sb("s_goff", [1, 16], F32)
    s_loff = sb("s_loff", [1, 16], F32)
    s_aS = sb("s_aS", [P, NH], F32)
    s_aR = sb("s_aR", [P, NH], BF16)
    s_rb16 = sb("s_rb16", [16, P], BF16)
    s_bb = sb("s_bb", [64, P], BF16)

    pA = nc.alloc_psum_tensor("pA", [P, 1024], F32)
    pB = nc.alloc_psum_tensor("pB", [P, 1024], F32)
    pY = nc.alloc_psum_tensor("pY", [P, 1024], F32)
    pC = nc.alloc_psum_tensor("pC", [P, 1024], F32)
    pAb = pA[:].bitcast(BF16)
    pBb = pB[:].bitcast(BF16)
    pCb = pC[:].bitcast(BF16)

    v_fb1raw = s_scrA[0:32, :]
    v_abraw = s_scrA[0:8, 0:128]
    v_ab1raw = s_scrA[0:4, 128:256]
    v_fb2row = s_scrA[0:1, 0:1024]

    bcreg = {}

    def init_bcreg(g):
        r = g.alloc_register("bcreg")
        g.reg_mov(r, NE * CAP - 1)
        bcreg["r"] = r
        return g.engine_nop() if hasattr(g, "engine_nop") else g.nop()

    pr.op("gpsimd", init_bcreg, tag="bcreg")

    def dma(eng, dst, src, reads=(), writes=(), tag="dma"):
        pr.op(eng, (lambda d, s: lambda e: e.dma_start(d, s))(dst, src),
              reads=reads, writes=writes, dma=True, tag=tag)

    def mm(dst, lhsT, rhs, start, stop, reads, writes, tag="mm"):
        pr.op("tensor",
              (lambda d, l, r, st, sp:
               lambda t: t.matmul(d, l, r, start=st, stop=sp))
              (dst, lhsT, rhs, start, stop),
              reads=reads, writes=writes, tag=tag)

    def tp(dst, src, ident, reads, writes, tag="tp"):
        pr.op("tensor",
              (lambda d, s, i: lambda t: t.transpose(d, s, i))(dst, src, ident),
              reads=reads, writes=writes, tag=tag)

    def vop(fn, reads, writes, tag="v"):
        pr.op("vector", fn, reads=reads, writes=writes, tag=tag)

    def sop(fn, reads, writes, tag="s"):
        pr.op("scalar", fn, reads=reads, writes=writes, tag=tag)

    def prelude():
        for dst, src in [
            (c_identb, ident_b), (c_identf, ident_f), (c_ut128f, ut128f_d),
            (c_ut32f, ut32f_d), (c_iota8, iota8_d), (c_onesrow_f, onesrow_f_d),
            (c_onesrow_b, onesrow_b_d), (c_onescol_f, onescol_f_d),
            (c_onescol32_f, onescol32_d), (c_diagm, diagm_d),
            (c_diagmT, diagmT_d), (c_selrank, selrank_d),
        ]:
            dma("sync", dst[:], src, writes=[dst.name], tag="const")
        dma("sync", s_wg[:].rearrange("p (c e) -> p c e", c=DC),
            wg_d.rearrange("(c p) e -> p c e", p=P), writes=["s_wg"], tag="wg")
        dma("sync", v_fb1raw.rearrange("c (ee p) -> c ee p", ee=NE),
            fb1_d.rearrange("ee (c p) -> c ee p", p=P),
            writes=["scrA"], tag="fb1")
        for e in range(NE):
            tp(pC[0:P, 0:32], v_fb1raw[:, e * P:(e + 1) * P],
               c_identf[0:32, 0:32],
               reads=["scrA", "c_identf", "pC.0"], writes=["pC.0"], tag="fb1t")
            vop((lambda e: lambda v: v.tensor_copy(
                s_fb1t[:, e * FC:(e + 1) * FC], pC[0:P, 0:32]))(e),
                reads=["pC.0"], writes=["s_fb1t"], tag="fb1e")
        dma("sync", v_ab1raw, ab1_d.rearrange("o (c p) -> (o c) p", p=P),
            reads=["scrA"], writes=["scrA"], tag="ab1")
        tp(pC[0:P, 32:36], v_ab1raw, c_identf[0:4, 0:4],
           reads=["scrA", "c_identf", "pC.0"], writes=["pC.0"], tag="ab1t")
        vop(lambda v: v.tensor_copy(s_ab1t[:], pC[0:P, 32:36]),
            reads=["pC.0"], writes=["s_ab1t"], tag="ab1e")
        dma("sync", v_abraw, ab2_d.rearrange("o (c p) -> (o c) p", p=P),
            reads=["scrA"], writes=["scrA"], tag="ab2")
        tp(pC[0:P, 36:44], v_abraw, c_identf[0:8, 0:8],
           reads=["scrA", "c_identf", "pC.0"], writes=["pC.0"], tag="ab2t")
        vop(lambda v: v.tensor_copy(s_ab2t[:], pC[0:P, 36:44]),
            reads=["pC.0"], writes=["s_ab2t"], tag="ab2e")

    def cache_weights():
        for nm, dd, ss in (("wq", wqb, wq_d), ("wk", wkb, wk_d),
                           ("wv", wvb, wv_d), ("wo", wob, wo_d),
                           ("aw1", aw1b, aw1_d), ("aw2", aw2b, aw2_d)):
            dma("gpsimd", dd, ss, writes=[("wbc", nm)], tag="wcc")

    def xstage(t, deep=True):
        if not deep:
            return s_xt[t % 2][:], [("xtb", t % 2)]
        if t % 4 < 2:
            return s_xt[t % 4][:], [("xtb", t % 4)]
        j = t % 4 - 2
        xw = s_w1w[0][:].bitcast(F32)
        return (xw[:, j * 1024:(j + 1) * 1024],
                [("w1b", 0, 4 * j + i) for i in range(4)])

    def ln_pass(C, src_hbm, n_tiles, skey=None, soff=0, deep=True):
        def hk(t):
            return [("hbm", skey, soff + t)] if skey else []
        for t in range(n_tiles):
            xt, rx = xstage(t, deep)
            dma("sync" if t % 2 == 0 else "scalar",
                xt, src_hbm[t * P:(t + 1) * P, :],
                reads=rx + hk(t), writes=rx, tag="ldx")
            vop((lambda t, xt: lambda v: v.tensor_reduce(
                s_mean[:, t:t + 1], xt, AX.X, ALU.add))(t, xt),
                reads=rx, writes=[(C, "mean", t)], tag="mean")
            sop((lambda t, xt: lambda s: s.activation(
                s_scrA[:], xt, AF.Square, bias=0.0, scale=1.0 / 32.0,
                accum_out=s_ssq[:, t:t + 1]))(t, xt),
                reads=rx + ["scrA"], writes=[(C, "ssq", t), "scrA"],
                tag="ssq")
        allm = [(C, "mean", t) for t in range(n_tiles)]
        allq = [(C, "ssq", t) for t in range(n_tiles)]
        vop(lambda v: v.tensor_scalar(
            s_mean[:, 0:n_tiles], s_mean[:, 0:n_tiles], 1.0 / D, None, ALU.mult),
            reads=allm, writes=[(C, "meanN")], tag="meanN")
        vop(lambda v: v.tensor_mul(
            s_rstd[:, 0:n_tiles], s_mean[:, 0:n_tiles], s_mean[:, 0:n_tiles]),
            reads=[(C, "meanN")], writes=[(C, "m2")], tag="m2")
        vop(lambda v: v.tensor_sub(
            s_rstd[:, 0:n_tiles], s_rstd[:, 0:n_tiles], s_ssq[:, 0:n_tiles]),
            reads=[(C, "m2")] + allq, writes=[(C, "negvar0")], tag="negvar")
        vop(lambda v: v.tensor_scalar(
            s_rstd[:, 0:n_tiles], s_rstd[:, 0:n_tiles], -1e-5, None, ALU.add),
            reads=[(C, "negvar0")], writes=[(C, "negvar")], tag="nveps")
        sop(lambda s: s.activation(
            s_rstd[:, 0:n_tiles], s_rstd[:, 0:n_tiles], AF.Ln,
            bias=0.0, scale=-1.0),
            reads=[(C, "negvar")], writes=[(C, "lnvar")], tag="lnvar")
        sop(lambda s: s.activation(
            s_rstd[:, 0:n_tiles], s_rstd[:, 0:n_tiles], AF.Exp,
            bias=0.0, scale=-0.5),
            reads=[(C, "lnvar")], writes=[(C, "rstd")], tag="rstd")

    def ffn(call, src_hbm, dst_hbm, skey=None, dkey=None, fuse=False):
        C = f"f{call}"

        def hk(t):
            return [("hbm", skey, t)] if skey else []

        def pass2_tile(t, stg, rsk):
            xt, rx = stg
            vop((lambda t, xt: lambda v: v.tensor_scalar(
                s_scrA[:], xt, s_mean[:, t:t + 1], s_rstd[:, t:t + 1],
                ALU.subtract, ALU.mult))(t, xt),
                reads=rx + [rsk, "scrA"],
                writes=["scrA", (C, "zf", t)], tag="znorm")
            sop((lambda t: lambda s: s.activation(
                s_h[:, t * D:(t + 1) * D], s_scrA[:], AF.Identity,
                bias=0.0, scale=1.0))(t),
                reads=[(C, "zf", t), "scrA", ("hb", t)], writes=[("hb", t)],
                tag="zcast")
            pz = pA if t % 2 == 0 else pB
            pzk = ["pA.0", "pA.1"] if t % 2 == 0 else ["pB.0", "pB.1"]
            for c in range(DC):
                tp(pz[:, c * P:(c + 1) * P], s_scrA[:, c * P:(c + 1) * P],
                   c_identf[:],
                   reads=[(C, "zf", t), "scrA", "c_identf"] + pzk,
                   writes=pzk, tag="ztp")
            vop((lambda pz: lambda v: v.tensor_copy(s_scrB[:], pz[:]))(pz),
                reads=pzk + ["scrB"], writes=["scrB", (C, "zt", t)], tag="zte")
            for c in range(DC):
                mm(pC[:, 0:NE], s_scrB[:, c * P:(c + 1) * P],
                   s_wg[:, c * NE:(c + 1) * NE], c == 0, c == DC - 1,
                   reads=[(C, "zt", t), "scrB", "s_wg", "pC.0"],
                   writes=["pC.0"], tag="gate")
            l8 = s_l8p[t % 2]
            mx8 = s_mx8p[t % 2]
            ix8 = s_ix8p[t % 2]
            rl8, rmx, rix = ("l8b", t % 2), ("mx8b", t % 2), ("ix8b", t % 2)
            vop((lambda l8: lambda v: v.tensor_copy(l8[:], pC[:, 0:NE]))(l8),
                reads=["pC.0", rl8], writes=[rl8, (C, "l8", t)], tag="l8e")
            vop((lambda l8, mx8: lambda v: v.max(mx8[:], l8[:]))(l8, mx8),
                reads=[(C, "l8", t), rmx], writes=[rmx, (C, "mx8", t)], tag="max8")
            vop((lambda l8, mx8, ix8: lambda v: v.max_index(ix8[:], mx8[:], l8[:]))
                (l8, mx8, ix8),
                reads=[(C, "l8", t), (C, "mx8", t), rix],
                writes=[rix, (C, "ix8", t)], tag="idx8")
            vop((lambda t, mx8: lambda v: v.tensor_copy(
                s_m12[:, 2 * t:2 * t + 2], mx8[:, 0:2]))(t, mx8),
                reads=[(C, "mx8", t)], writes=[(C, "m12", t)], tag="m12")
            vop((lambda t, mx8: lambda v: v.tensor_scalar(
                s_negm1[:, t:t + 1], mx8[:, 0:1], -1.0, None, ALU.mult))(t, mx8),
                reads=[(C, "mx8", t)], writes=[(C, "negm1", t)], tag="negm1")
            sop((lambda t, l8: lambda s: s.activation(
                s_junk8[:], l8[:], AF.Exp, bias=s_negm1[:, t:t + 1], scale=1.0,
                accum_out=s_S8[:, t:t + 1]))(t, l8),
                reads=[(C, "l8", t), (C, "negm1", t)], writes=[(C, "S8", t)],
                tag="gexp")
            vop((lambda t, ix8: lambda v: v.tensor_copy(
                s_idxf[:, 2 * t:2 * t + 2], ix8[:, 0:2]))(t, ix8),
                reads=[(C, "ix8", t)], writes=[(C, "idxf", t)], tag="idxf")
            for j in range(2):
                vop((lambda t, j: lambda v: v.tensor_scalar(
                    s_masksf[:, 16 * t + 8 * j:16 * t + 8 * j + 8],
                    c_iota8[:], s_idxf[:, 2 * t + j:2 * t + j + 1], None,
                    ALU.is_equal))(t, j),
                    reads=[(C, "idxf", t), "c_iota8"],
                    writes=[(C, "mask", t, j)], tag="mask")

        staged = {}
        for g in range(NT // 2):
            for a in (2 * g, 2 * g + 1):
                xt, rx = xstage(a)
                staged[a] = (xt, rx)
                dma("sync" if a % 2 == 0 else "scalar",
                    xt, src_hbm[a * P:(a + 1) * P, :],
                    reads=rx + hk(a), writes=rx, tag="ldx2")
                zbj = s_xtok[:, (a % 2) * 1024:(a % 2 + 1) * 1024]
                rzb = ("xtokR", a % 2)
                sop((lambda xt, zbj, a: lambda s: s.activation(
                    zbj, xt, AF.Identity, bias=0.0, scale=1.0 / D,
                    accum_out=s_mean[:, a:a + 1]))(xt, zbj, a),
                    reads=rx + [rzb], writes=[rzb, (C, "fm", a)], tag="pmean")
                sop((lambda xt, zbj, a: lambda s: s.activation(
                    zbj, xt, AF.Square, bias=0.0, scale=1.0 / 32.0,
                    accum_out=s_ssq[:, a:a + 1]))(xt, zbj, a),
                    reads=rx + [rzb], writes=[rzb, (C, "fq", a)], tag="pssq")
            c0, c1 = 2 * g, 2 * g + 2
            vop((lambda c0, c1: lambda v: v.tensor_mul(
                s_rstd[:, c0:c1], s_mean[:, c0:c1], s_mean[:, c0:c1]))(c0, c1),
                reads=[(C, "fm", c0), (C, "fm", c0 + 1)],
                writes=[(C, "fv", g)], tag="pm2")
            vop((lambda c0, c1: lambda v: v.tensor_sub(
                s_rstd[:, c0:c1], s_rstd[:, c0:c1], s_ssq[:, c0:c1]))(c0, c1),
                reads=[(C, "fq", c0), (C, "fq", c0 + 1), (C, "fv", g)],
                writes=[(C, "fv", g)], tag="pnv")
            vop((lambda c0, c1: lambda v: v.tensor_scalar(
                s_rstd[:, c0:c1], s_rstd[:, c0:c1], -1e-5, None, ALU.add))
                (c0, c1),
                reads=[(C, "fv", g)], writes=[(C, "fv", g)], tag="pnve")
            sop((lambda c0, c1: lambda s: s.activation(
                s_rstd[:, c0:c1], s_rstd[:, c0:c1], AF.Ln, bias=0.0,
                scale=-1.0))(c0, c1),
                reads=[(C, "fv", g)], writes=[(C, "fv", g)], tag="plnv")
            sop((lambda c0, c1: lambda s: s.activation(
                s_rstd[:, c0:c1], s_rstd[:, c0:c1], AF.Exp, bias=0.0,
                scale=-0.5))(c0, c1),
                reads=[(C, "fv", g)], writes=[(C, "frs2", g)], tag="prstd")
            for t in (2 * g, 2 * g + 1):
                pass2_tile(t, staged[t], (C, "frs2", g))


        allmask = [(C, "mask", t, j) for t in range(NT) for j in range(2)]
        allidxf = [(C, "idxf", t) for t in range(NT)]
        allm12 = [(C, "m12", t) for t in range(NT)]
        allS8 = [(C, "S8", t) for t in range(NT)]

        mm(pB[:, 0:512], c_ut128f[:], s_masksf[:], True, False,
           reads=["c_ut128f", "pB.0"] + allmask, writes=["pB.0"], tag="utg")
        mm(pA[:, 0:512], c_ut128f[:], s_masksf[:], True, False,
           reads=["c_ut128f", "pA.0"] + allmask, writes=["pA.0"], tag="utl")
        mm(pY[0:1, 0:512], c_onescol_f[:], s_masksf[:], True, True,
           reads=["c_onescol_f", "pY.0"] + allmask, writes=["pY.0"], tag="tot")
        vop(lambda v: v.tensor_copy(v_rep, pY[0:1, 0:512]),
            reads=["pY.0", "rows"], writes=["rows", (C, "trow")], tag="trowe")
        dma("gpsimd", s_t32[:], v_rep,
            reads=[(C, "trow")], writes=[(C, "t32")], tag="t32")
        mm(pC[0:32, 0:16], c_ut32f[:], s_t32[:], True, True,
           reads=["c_ut32f", (C, "t32"), "pC.0"], writes=["pC.0"], tag="ut32")
        vop(lambda v: v.tensor_copy(s_toff[:], pC[0:32, 0:16]),
            reads=["pC.0"], writes=[(C, "toff")], tag="toffe")
        mm(pC[0:1, 16:32], c_onescol32_f[:], s_t32[:], True, True,
           reads=["c_onescol32_f", (C, "t32"), "pC.0"], writes=["pC.0"], tag="ltm")
        vop(lambda v: v.tensor_copy(s_lt[:], pC[0:1, 16:32]),
            reads=["pC.0"], writes=[(C, "lt")], tag="lte")
        dma("gpsimd", v_toffrow, s_toff[:],
            reads=[(C, "toff"), "rows"], writes=["rows", (C, "toffrow")],
            tag="toffrow")
        dma("gpsimd", cc_in, s_lt[:], reads=[(C, "lt"), "ccio"],
            writes=["ccio", (C, "ccin")], tag="ccw")
        pr.op("gpsimd", lambda g: g.collective_compute(
            "AllGather", ALU.bypass, replica_groups=[list(range(NE))],
            ins=[cc_in.opt()], outs=[cc_out.opt()]),
            reads=[(C, "ccin"), "ccio"], writes=["ccio", (C, "ccout")], tag="cc")
        vop(lambda v: v.memset(s_loff[:], 0.0),
            reads=[], writes=[(C, "loff0")], tag="loffz")
        vop(lambda v: v.tensor_copy(s_loff[0:1, 8:16], s_lt[0:1, 0:8]),
            reads=[(C, "loff0"), (C, "lt")], writes=[(C, "loff")], tag="loffc")
        for t in range(NT):
            vop((lambda t: lambda v: v.tensor_add(
                v_rep[0:1, 16 * t:16 * t + 16],
                v_toffrow[0:1, 16 * t:16 * t + 16], s_loff[0:1, :]))(t),
                reads=[(C, "toffrow"), (C, "loff"), "rows"],
                writes=["rows", (C, "lrow", t)], tag="lrow")
        mm(pA[:, 0:512], c_onesrow_f[:], v_rep, False, True,
           reads=["c_onesrow_f", "pA.0"] + [(C, "lrow", t) for t in range(NT)],
           writes=["pA.0"], tag="lk1")
        vop(lambda v: v.tensor_mul(s_scrA[:, 0:512], pA[:, 0:512], s_masksf[:]),
            reads=["pA.0", "scrA"] + allmask, writes=["scrA", (C, "prodl")],
            tag="prodl")
        vop(lambda v: v.tensor_reduce(
            s_posl[:], s_scrA[:, 0:512].rearrange("p (g e) -> p g e", e=8),
            AX.X, ALU.add),
            reads=[(C, "prodl"), "scrA"], writes=[(C, "posl")], tag="redl")
        vop(lambda v: v.tensor_scalar(
            s_keep[:], s_posl[:], float(CAP), None, ALU.is_lt),
            reads=[(C, "posl")], writes=[(C, "kl")], tag="kl")
        m1v = s_m12[:].rearrange("p (t j) -> p t j", j=2)[:, :, 0]
        m2v = s_m12[:].rearrange("p (t j) -> p t j", j=2)[:, :, 1]
        vop(lambda v: v.tensor_sub(s_ta[:, 0:NT], m2v, m1v),
            reads=allm12, writes=[(C, "d21")], tag="d21")
        sop(lambda s: s.activation(s_ta[:, 0:NT], s_ta[:, 0:NT], AF.Exp),
            reads=[(C, "d21")], writes=[(C, "e21")], tag="e21")
        vop(lambda v: v.reciprocal(s_tb[:, 0:NT], s_S8[:]),
            reads=allS8, writes=[(C, "g1")], tag="g1")
        vop(lambda v: v.tensor_mul(s_ta[:, NT:2 * NT], s_ta[:, 0:NT],
                                   s_tb[:, 0:NT]),
            reads=[(C, "e21"), (C, "g1")], writes=[(C, "g2")], tag="g2")
        vop(lambda v: v.tensor_add(s_ta[:, 0:NT], s_ta[:, NT:2 * NT],
                                   s_tb[:, 0:NT]),
            reads=[(C, "g1"), (C, "g2")], writes=[(C, "gsum")], tag="gsum")
        vop(lambda v: v.tensor_scalar(
            s_ta[:, 0:NT], s_ta[:, 0:NT], 1e-9, None, ALU.add),
            reads=[(C, "gsum")], writes=[(C, "den")], tag="den")
        vop(lambda v: v.reciprocal(s_ta[:, 0:NT], s_ta[:, 0:NT]),
            reads=[(C, "den")], writes=[(C, "rden")], tag="rden")
        w1v = s_wk_[:].rearrange("p (t j) -> p t j", j=2)[:, :, 0]
        w2v = s_wk_[:].rearrange("p (t j) -> p t j", j=2)[:, :, 1]
        vop(lambda v: v.tensor_mul(w1v, s_tb[:, 0:NT], s_ta[:, 0:NT]),
            reads=[(C, "g1"), (C, "rden")], writes=[(C, "w1")], tag="w1")
        vop(lambda v: v.tensor_mul(w2v, s_ta[:, NT:2 * NT], s_ta[:, 0:NT]),
            reads=[(C, "g2"), (C, "rden")], writes=[(C, "w2")], tag="w2")
        vop(lambda v: v.scalar_tensor_tensor(
            s_ta[:], s_idxf[:], float(CAP), s_posl[:], ALU.mult, ALU.add),
            reads=allidxf + [(C, "posl")], writes=[(C, "lidx")], tag="lidx")
        vop(lambda v: v.tensor_scalar(
            s_tb[:], s_keep[:], -BIGIDX, BIGIDX, ALU.mult, ALU.add),
            reads=[(C, "kl")], writes=[(C, "sentm")], tag="sentm")
        vop(lambda v: v.tensor_add(s_tb[:], s_tb[:], s_ta[:]),
            reads=[(C, "sentm"), (C, "lidx")], writes=[(C, "lidxF")],
            tag="lidxF")
        vop(lambda v: v.tensor_copy(s_idxi[:], s_tb[:]),
            reads=[(C, "lidxF")], writes=[(C, "idxi")], tag="idxi")

        for t in range(NT):
            for j in range(2):
                col = 2 * t + j
                pr.op("gpsimd", (lambda t, col: lambda g: g.indirect_dma_start(
                    xe_dram, IOOA(s_idxi[:, col:col + 1], 0),
                    s_h[:, t * D:(t + 1) * D], None,
                    bounds_check=bcreg["r"], oob_is_err=False))(t, col),
                    reads=[(C, "idxi"), ("hb", t)],
                    writes=[(C, "scat", t, j)], dma=True, tag="scat")
        pr.barrier(tag=f"{C}_scatdone")

        def w1load(e, sh, w, c):
            wb = w % 2
            dst = s_w1w[wb][:, c * 512:(c + 1) * 512]
            if call == "A" and sh == 0:
                dma("gpsimd", dst,
                    fw1_d[e, c * P:(c + 1) * P, w * 512:(w + 1) * 512],
                    reads=[("w1b", wb, c)],
                    writes=[("w1b", wb, c), (C, "w1", e, sh, w, c)], tag="w1l")
                dma("sync", fw1b[e, c * P:(c + 1) * P, w * 512:(w + 1) * 512],
                    dst,
                    reads=[("w1b", wb, c), (C, "w1", e, sh, w, c)],
                    writes=[("w1c", e, w, c)], tag="w1s")
            else:
                dma("gpsimd", dst,
                    fw1b[e, c * P:(c + 1) * P, w * 512:(w + 1) * 512],
                    reads=[("w1b", wb, c), ("w1c", e, w, c)],
                    writes=[("w1b", wb, c), (C, "w1", e, sh, w, c)], tag="w1l")

        for e in range(NE):
            for f in range(FC):
                if call == "A":
                    dma("gpsimd", s_w2[:, f * 1024:(f + 1) * 1024],
                        fw2_d[e, f * P:(f + 1) * P, :],
                        reads=[("w2B", f)], writes=[("w2B", f)], tag="w2l")
                    dma("scalar", fw2b[e, f * P:(f + 1) * P, :],
                        s_w2[:, f * 1024:(f + 1) * 1024],
                        reads=[("w2B", f)],
                        writes=[("w2c", e, f)], tag="w2s")
                else:
                    dma("sync", s_w2[:, f * 1024:(f + 1) * 1024],
                        fw2b[e, f * P:(f + 1) * P, :],
                        reads=[("w2B", f), ("w2c", e, f)], writes=[("w2B", f)],
                        tag="w2l")
            dma("sync", v_fb2row, fb2_d[e:e + 1, :],
                reads=["scrA"], writes=["scrA", (C, "b2r", e)], tag="b2l")
            for hf in range(2):
                mm(pC[:, 0:512], c_onesrow_f[:],
                   v_fb2row[0:1, hf * 512:(hf + 1) * 512], True, True,
                   reads=["c_onesrow_f", (C, "b2r", e), "scrA", "pC.0"],
                   writes=["pC.0"], tag="b2k1")
                vop((lambda hf: lambda v: v.tensor_copy(
                    s_scrB[:, hf * 512:(hf + 1) * 512], pC[:, 0:512]))(hf),
                    reads=["pC.0", "scrB"], writes=["scrB", (C, "b2bc", e, hf)],
                    tag="b2e")
            for sh in range(2):
                for w in range(2):
                    for c in range(DC):
                        w1load(e, sh, w, c)
                for g2 in range(4):
                    bp = (g2 % 2) * 2
                    rows0 = e * CAP + sh * 1024 + g2 * 256
                    dma("scalar",
                        s_xtok[:, bp * 1024:(bp + 2) * 1024]
                        .rearrange("p (tt d) -> p tt d", tt=2),
                        xe_dram[rows0:rows0 + 256, :]
                        .rearrange("(tt p) d -> p tt d", p=P),
                        reads=[("xtokR", bp), ("xtokR", bp + 1)],
                        writes=[("xtokR", bp), ("xtokR", bp + 1),
                                (C, "xtok", e, sh, g2)],
                        tag="xel")
                    for c in range(DC):
                        kgrp = c + g2
                        pdst = pAb if kgrp % 2 == 0 else pBb
                        pk = "pA.0" if kgrp % 2 == 0 else "pB.0"
                        for tt in range(2):
                            tp(pdst[:, tt * P:(tt + 1) * P],
                               s_xtok[:, (bp + tt) * D + c * P:(bp + tt) * D + (c + 1) * P],
                               c_identb[:],
                               reads=[(C, "xtok", e, sh, g2), ("xtokR", bp + tt),
                                      "c_identb", pk],
                               writes=[pk], tag="xtp")
                        vop((lambda c, g2, pdst: lambda v: v.tensor_copy(
                            s_xet[:, c * 1024 + g2 * 256:c * 1024 + g2 * 256 + 256],
                            pdst[:, 0:256]))(c, g2, pdst),
                            reads=[pk, ("xetb", c)],
                            writes=[("xetb", c), (C, "xet", e, sh, c, g2)],
                            tag="xte")
                xetall = [(C, "xet", e, sh, c, g2)
                          for c in range(DC) for g2 in range(4)]
                if DEBUG_TAPS and call == "A" and e == 0 and sh == 0:
                    dma("sync", dbg["xet0"], s_xet[:], reads=list(xetall),
                        writes=[("dbgw", "xet0")], tag="dbgxet")
                for w in range(8):
                    wb = w % 2
                    if w >= 2:
                        for c in range(DC):
                            w1load(e, sh, w, c)
                    for fc4 in range(4):
                        f = w * 4 + fc4
                        ps, pk0, pk1 = ((pB, "pB.0", "pB.1") if (f % 2 == 0)
                                        else (pA, "pA.0", "pA.1"))
                        for d in range(DC):
                            lhs = s_w1w[wb][:, d * 512 + fc4 * P:
                                            d * 512 + (fc4 + 1) * P]
                            w1keys = [(C, "w1", e, sh, w, c) for c in range(DC)]
                            mm(ps[:, 0:512], lhs,
                               s_xet[:, d * 1024:d * 1024 + 512],
                               d == 0, d == DC - 1,
                               reads=w1keys + [("w1b", wb, d),
                                      ("xetb", d), pk0] + xetall,
                               writes=[pk0], tag="mm1")
                            mm(ps[:, 512:1024], lhs,
                               s_xet[:, d * 1024 + 512:d * 1024 + 1024],
                               d == 0, d == DC - 1,
                               reads=w1keys + [("w1b", wb, d),
                                      ("xetb", d), pk1] + xetall,
                               writes=[pk1], tag="mm1")
                        sop((lambda ps, e, f: lambda s: s.activation(
                            s_h[:, f * 1024:(f + 1) * 1024], ps[:], AF.Gelu,
                            bias=s_fb1t[:, e * FC + f:e * FC + f + 1],
                            scale=1.0))(ps, e, f),
                            reads=[pk0, pk1, "s_fb1t", ("hb", f)],
                            writes=[("hb", f), (C, "h", e, sh, f)], tag="gelu")
                if DEBUG_TAPS and call == "A" and e == 0 and sh == 0:
                    dma("sync", dbg["h0"], s_h[:],
                        reads=[(C, "h", e, sh, f) for f in range(FC)],
                        writes=[("dbgw", "h0")], tag="dbgh0")
                for st in range(8):
                    py, yk0, yk1 = ((pY, "pY.0", "pY.1") if st % 2 == 0
                                    else (pC, "pC.0", "pC.1"))
                    for f in range(FC):
                        lhs = s_h[:, f * 1024 + st * P:f * 1024 + (st + 1) * P]
                        mm(py[:, 0:512], lhs, s_w2[:, f * 1024:f * 1024 + 512],
                           f == 0, f == FC - 1,
                           reads=[(C, "h", e, sh, f), ("hb", f), ("w2B", f),
                                  yk0],
                           writes=[yk0], tag="mm2")
                        mm(py[:, 512:1024], lhs,
                           s_w2[:, f * 1024 + 512:f * 1024 + 1024],
                           f == 0, f == FC - 1,
                           reads=[(C, "h", e, sh, f), ("hb", f), ("w2B", f),
                                  yk1],
                           writes=[yk1], tag="mm2")
                    yst = s_yest[st % 2]
                    ry = ("yestb", st % 2)
                    vop((lambda yst, py: lambda v: v.scalar_tensor_tensor(
                        yst[:], py[:], 1.0, s_scrB[:], ALU.mult, ALU.add))(yst, py),
                        reads=[yk0, yk1, (C, "b2bc", e, 0), (C, "b2bc", e, 1),
                               "scrB", ry],
                        writes=[ry, (C, "yest", e, sh, st)], tag="yee")
                    rows0 = e * CAP + sh * 1024 + st * P
                    dma("sync", ye_dram[rows0:rows0 + P, :], yst[:],
                        reads=[(C, "yest", e, sh, st), ry], writes=[ry, (C, "yed", e, sh, st)],
                        tag="yew")
        allye = [(C, "yed", e, sh, st)
                 for e in range(NE) for sh in range(2) for st in range(8)]
        dma("gpsimd", s_ag[:], cc_out, reads=[(C, "ccout")],
            writes=[(C, "ag")], tag="ccr")
        mm(pC[0:1, 32:48], c_selrank[:, 0:1], s_ag[:], True, True,
           reads=["c_selrank", (C, "ag"), "pC.0"], writes=["pC.0"], tag="s1m")
        vop(lambda v: v.tensor_copy(s_s1[:], pC[0:1, 32:48]),
            reads=["pC.0"], writes=[(C, "s1")], tag="s1e")
        mm(pC[0:1, 48:64], c_selrank[:, 1:2], s_ag[:], True, True,
           reads=["c_selrank", (C, "ag"), "pC.0"], writes=["pC.0"], tag="s2m")
        vop(lambda v: v.tensor_copy(s_s2[:], pC[0:1, 48:64]),
            reads=["pC.0"], writes=[(C, "s2")], tag="s2e")
        vop(lambda v: v.tensor_copy(s_goff[:], s_s1[:]),
            reads=[(C, "s1")], writes=[(C, "goff0")], tag="goffc")
        vop(lambda v: v.tensor_add(
            s_goff[0:1, 8:16], s_s1[0:1, 8:16], s_s2[0:1, 0:8]),
            reads=[(C, "goff0"), (C, "s1"), (C, "s2")], writes=[(C, "goff")],
            tag="goffa")
        vop(lambda v: v.tensor_sub(s_goff[:], s_goff[:], s_loff[:]),
            reads=[(C, "goff"), (C, "loff")], writes=[(C, "gdelta")],
            tag="gdelta")
        for t in range(NT):
            vop((lambda t: lambda v: v.tensor_copy(
                v_rep[0:1, 16 * t:16 * t + 16], s_goff[0:1, :]))(t),
                reads=[(C, "gdelta"), "rows"],
                writes=["rows", (C, "grow", t)], tag="grow")
        mm(pB[:, 0:512], c_onesrow_f[:], v_rep, True, True,
           reads=["c_onesrow_f", "pB.0"] + [(C, "grow", t) for t in range(NT)],
           writes=["pB.0"], tag="gk1")
        vop(lambda v: v.tensor_mul(s_scrA[:, 0:512], pB[:, 0:512], s_masksf[:]),
            reads=["pB.0", "scrA"] + allmask,
            writes=["scrA", (C, "prodg")], tag="prodg")
        vop(lambda v: v.tensor_reduce(
            s_posg[:], s_scrA[:, 0:512].rearrange("p (g e) -> p g e", e=8),
            AX.X, ALU.add),
            reads=[(C, "prodg"), "scrA"], writes=[(C, "posg0")], tag="redg")
        vop(lambda v: v.tensor_add(s_posg[:], s_posg[:], s_posl[:]),
            reads=[(C, "posg0"), (C, "posl")], writes=[(C, "posg")],
            tag="posg")
        vop(lambda v: v.tensor_scalar(
            s_ta[:], s_posg[:], float(GCAP), None, ALU.is_lt),
            reads=[(C, "posg")], writes=[(C, "kg")], tag="kg")
        vop(lambda v: v.tensor_mul(s_keep[:], s_keep[:], s_ta[:]),
            reads=[(C, "kg"), (C, "kl")], writes=[(C, "keep")], tag="keep")
        vop(lambda v: v.tensor_mul(s_wk_[:], s_wk_[:], s_keep[:]),
            reads=[(C, "w1"), (C, "w2"), (C, "keep")], writes=[(C, "wkk")],
            tag="wkk")
        vop(lambda v: v.tensor_scalar(s_wk_[:], s_wk_[:], 0.5, None, ALU.mult),
            reads=[(C, "wkk")], writes=[(C, "wk")], tag="wkf")
        if DEBUG_TAPS and call == "A":
            for nm, buf, rk in [("idxf", s_idxf, [(C, "idxf", t) for t in range(NT)]),
                                ("posl", s_posl, [(C, "posl")]),
                                ("posg", s_posg, [(C, "posg")]),
                                ("keep", s_keep, [(C, "keep")]),
                                ("wk", s_wk_, [(C, "wk")]),
                                ("S8", s_S8, [(C, "S8", t) for t in range(NT)]),
                                ("m12", s_m12, [(C, "m12", t) for t in range(NT)]),
                                ("idxi", s_idxi, [(C, "idxi")]),
                                ("masks", s_masksf,
                                 [(C, "mask", t, j) for t in range(NT) for j in range(2)])]:
                dma("sync", dbg[nm], buf[:], reads=rk, writes=[("dbgw", nm)],
                    tag="dbg")


        if DEBUG_TAPS and call == "A":
            dma("sync", dbg["xe"], xe_dram, reads=[(C, "scat", t, j) for t in range(NT) for j in range(2)],
                writes=[("dbgw", "xe")], tag="dbgxe")
            dma("sync", dbg["ye"], ye_dram, reads=allye, writes=[("dbgw", "ye")],
                tag="dbgye")

        def rot(t):
            if t % 4 < 2:
                return s_yest[t % 4][:], ("yestb", t % 4)
            return s_xt[t % 4 - 2][:], ("xtb", t % 4 - 2)

        def combine_tile(t):
            pr.op("gpsimd", (lambda t: lambda g: g.indirect_dma_start(
                s_scrB[:], None, ye_dram, IOOA(s_idxi[:, 2 * t:2 * t + 1], 0),
                bounds_check=bcreg["r"], oob_is_err=False))(t),
                reads=[(C, "idxi"), "scrB"] + allye,
                writes=["scrB", (C, "gA", t)], dma=True, tag="gat1")
            pr.op("gpsimd", (lambda t: lambda g: g.indirect_dma_start(
                s_scrA[:], None, ye_dram, IOOA(s_idxi[:, 2 * t + 1:2 * t + 2], 0),
                bounds_check=bcreg["r"], oob_is_err=False))(t),
                reads=[(C, "idxi"), "scrA"] + allye,
                writes=["scrA", (C, "gB", t)], dma=True, tag="gat2")
            xw = s_w1w[(t // 2) % 2][:].bitcast(F32)
            xt = xw[:, (t % 2) * 1024:(t % 2 + 1) * 1024]
            rx4 = [("w1b", (t // 2) % 2, 4 * (t % 2) + i) for i in range(4)]
            dma("sync", xt, src_hbm[t * P:(t + 1) * P, :],
                reads=rx4 + hk(t), writes=rx4, tag="ldx3")
            vop((lambda t, xt: lambda v: v.scalar_tensor_tensor(
                s_scrB[:], s_scrB[:], s_wk_[:, 2 * t:2 * t + 1], xt,
                ALU.mult, ALU.add))(t, xt),
                reads=[(C, "gA", t), (C, "wk"), "scrB"] + rx4,
                writes=["scrB", (C, "acc1", t)] + rx4, tag="acc1")
            yst, ry = rot(t) if fuse else (s_yest[t % 2][:], ("yestb", t % 2))
            vop((lambda t, yst: lambda v: v.scalar_tensor_tensor(
                yst, s_scrA[:], s_wk_[:, 2 * t + 1:2 * t + 2], s_scrB[:],
                ALU.mult, ALU.add))(t, yst),
                reads=[(C, "gB", t), (C, "acc1", t), (C, "wk"), "scrB",
                       "scrA", ry],
                writes=[ry, (C, "ost", t)], tag="acc2")
            dma("sync", dst_hbm[t * P:(t + 1) * P, :], yst,
                reads=[(C, "ost", t), ry],
                writes=[ry, (C, "dst", t)] +
                ([("hbm", dkey, t)] if dkey else []), tag="stx")
            if fuse:
                zbj = s_xtok[:, (t % 2) * 1024:(t % 2 + 1) * 1024]
                rzb = ("xtokR", t % 2)
                sop((lambda yst, zbj, t: lambda s: s.activation(
                    zbj, yst, AF.Identity, bias=0.0, scale=1.0 / D,
                    accum_out=s_mean[:, t:t + 1]))(yst, zbj, t),
                    reads=[ry, (C, "ost", t), rzb],
                    writes=[rzb, ("fm", t)], tag="fmean")
                sop((lambda yst, zbj, t: lambda s: s.activation(
                    zbj, yst, AF.Square, bias=0.0, scale=1.0 / 32.0,
                    accum_out=s_ssq[:, t:t + 1]))(yst, zbj, t),
                    reads=[ry, (C, "ost", t), rzb],
                    writes=[rzb, ("fq", t)], tag="fssq")

        def fuse_rstd(g):
            c0, c1 = 2 * g, 2 * g + 2
            vop((lambda c0, c1: lambda v: v.tensor_mul(
                s_rstd[:, c0:c1], s_mean[:, c0:c1], s_mean[:, c0:c1]))(c0, c1),
                reads=[("fm", c0), ("fm", c0 + 1)], writes=[("fv", g)],
                tag="fm2")
            vop((lambda c0, c1: lambda v: v.tensor_sub(
                s_rstd[:, c0:c1], s_rstd[:, c0:c1], s_ssq[:, c0:c1]))(c0, c1),
                reads=[("fq", c0), ("fq", c0 + 1), ("fv", g)],
                writes=[("fv", g)], tag="fnv")
            vop((lambda c0, c1: lambda v: v.tensor_scalar(
                s_rstd[:, c0:c1], s_rstd[:, c0:c1], -1e-5, None, ALU.add))
                (c0, c1),
                reads=[("fv", g)], writes=[("fv", g)], tag="fnve")
            sop((lambda c0, c1: lambda s: s.activation(
                s_rstd[:, c0:c1], s_rstd[:, c0:c1], AF.Ln, bias=0.0,
                scale=-1.0))(c0, c1),
                reads=[("fv", g)], writes=[("fv", g)], tag="flnv")
            sop((lambda c0, c1: lambda s: s.activation(
                s_rstd[:, c0:c1], s_rstd[:, c0:c1], AF.Exp, bias=0.0,
                scale=-0.5))(c0, c1),
                reads=[("fv", g)], writes=[("frs", g)], tag="frstd")

        def fuse_zprep(t):
            A0 = "a0"
            yst, ry = rot(t)
            zbj = s_xtok[:, (t % 2) * 1024:(t % 2 + 1) * 1024]
            rzb = ("xtokR", t % 2)
            vop((lambda yst, zbj, t: lambda v: v.tensor_scalar(
                zbj, yst, s_mean[:, t:t + 1], s_rstd[:, t:t + 1],
                ALU.subtract, ALU.mult))(yst, zbj, t),
                reads=[ry, (C, "ost", t), ("frs", t // 2), rzb],
                writes=[rzb, (A0, "zb", t)], tag="faznorm")
            pzb = pAb if t % 2 == 0 else pBb
            pzk = "pA.0" if t % 2 == 0 else "pB.0"
            for c in range(DC):
                tp(pzb[:, c * P:(c + 1) * P], zbj[:, c * P:(c + 1) * P],
                   c_identb[:],
                   reads=[(A0, "zb", t), rzb, "c_identb", pzk],
                   writes=[pzk], tag="faztp")
            zt_view = s_h[:, 0:16384].rearrange(
                "p (c q) -> p c q", c=DC)[:, :, t * P:(t + 1) * P]
            pt_view = pzb[:, 0:1024].rearrange("p (c q) -> p c q", c=DC)
            hbks = [("hb", 2 * c + t // 8) for c in range(DC)]
            vop((lambda zv, pv: lambda v: v.tensor_copy(zv, pv))
                (zt_view, pt_view),
                reads=[pzk] + hbks,
                writes=hbks + [(A0, "zt", t)], tag="fazte")

        if fuse:
            for g in range(NT // 2):
                combine_tile(2 * g)
                combine_tile(2 * g + 1)
                fuse_rstd(g)
                if 2 * g < 16:
                    fuse_zprep(2 * g)
                    fuse_zprep(2 * g + 1)
        else:
            for t in range(NT):
                combine_tile(t)

    def attn(src_hbm, dst_hbm, skey=None, dkey=None, fused=False):
        a_zt = s_h[:, 0:16384]
        a_at = a_zt
        a_q = s_h[:, 16384:32768]
        a_ki = s_w2[:, 0:8192]
        a_kc = s_w2[:, 8192:10240]
        a_pt = [s_w2[:, 10240:11264], s_w2[:, 11264:12288]]
        a_ju = s_w2[:, 12288:13312]
        a_wq = s_w2[:, 0:8192]
        a_wk = s_w2[:, 8192:16384]
        a_wv = s_w2[:, 16384:24576]
        a_wo = s_w2[:, 24576:32768]
        a_vi = s_xet[:, 0:8192]
        a_zb = [s_xtok[:, 0:1024], s_xtok[:, 1024:2048]]
        a_cp = s_h[:, 0:2048]
        a_vc = s_xtok[:, 2048:4096]
        a_h1 = s_xtok[:, 0:4096]
        a_w1c = s_w1w[0]
        a_w2c = s_w1w[1]
        w2f = s_w2[:].bitcast(F32)
        hf_ = s_h[:].bitcast(F32)

        def projT_sl(oc, lo, hi):
            if oc < 6:
                return w2f[:, oc * SEQ + lo:oc * SEQ + hi]
            return hf_[:, 8192 + (oc - 6) * SEQ + lo:8192 + (oc - 6) * SEQ + hi]

        def projT_keys(oc):
            if oc < 6:
                return [("w2B", 4 * oc + i) for i in range(4)]
            return [("hb", 16 + 4 * (oc - 6) + i) for i in range(4)]

        wqK = [("w2B", i) for i in range(0, 8)]
        wkK = [("w2B", i) for i in range(8, 16)]
        wvK = [("w2B", i) for i in range(16, 24)]
        woK = [("w2B", i) for i in range(24, 32)]
        kiK = wqK
        kcK = [("w2B", 8), ("w2B", 9)]
        ptK = [("w2B", 10), ("w2B", 11)]

        for c in range(DC):
            dma("gpsimd", a_w1c[:, c * 512:(c + 1) * 512],
                aw1b[c * P:(c + 1) * P, :],
                reads=[("w1b", 0, c), ("wbc", "aw1")],
                writes=[("w1b", 0, c)], tag="aw1l")
        for f1 in range(4):
            dma("scalar", a_w2c[:, f1 * 1024:(f1 + 1) * 1024],
                aw2b[f1 * P:(f1 + 1) * P, :],
                reads=[("w1b", 1, 2 * f1), ("w1b", 1, 2 * f1 + 1),
                       ("wbc", "aw2")],
                writes=[("w1b", 1, 2 * f1), ("w1b", 1, 2 * f1 + 1)], tag="aw2l")
        w1cK = [("w1b", 0, c) for c in range(DC)]
        w2cK = [("w1b", 1, i) for i in range(8)]

        for b in range(NB):
            C = f"a{b}"
            bsrc = src_hbm[b * SEQ:(b + 1) * SEQ, :]
            bdst = dst_hbm[b * SEQ:(b + 1) * SEQ, :]
            NTB = 16
            for wdr, base, wnm in ((wqb, 0, "wq"), (wkb, 8, "wk"),
                                   (wvb, 16, "wv"), (wob, 24, "wo")):
                for c in range(DC):
                    dma(("sync", "scalar", "gpsimd")[c % 3],
                        s_w2[:, (base + c) * 1024:(base + c + 1) * 1024],
                        wdr[c * P:(c + 1) * P, :],
                        reads=[("w2B", base + c), ("wbc", wnm)],
                        writes=[("w2B", base + c)], tag="awl")
            if not fused:
                ln_pass(C, bsrc, NTB, skey, soff=b * 16, deep=False)

            for t in ([] if (fused and b == 0) else range(NTB)):
                g = b * 16 + t
                xt = s_xt[t % 2]
                rx = ("xtb", t % 2)
                zb = a_zb[t % 2]
                rzb = ("xtokR", t % 2)
                dma("sync", xt[:], bsrc[t * P:(t + 1) * P, :],
                    reads=[rx] + ([("hbm", skey, b * 16 + t)] if skey else []),
                    writes=[rx], tag="aldx")
                mcol = g if fused else t
                rstdkey = ("frs", g // 2) if fused else (C, "rstd")
                vop((lambda mcol, xt, zb: lambda v: v.tensor_scalar(
                    zb, xt[:], s_mean[:, mcol:mcol + 1],
                    s_rstd[:, mcol:mcol + 1],
                    ALU.subtract, ALU.mult))(mcol, xt, zb),
                    reads=[rx, rstdkey, rzb],
                    writes=[rzb, (C, "zb", t)], tag="aznorm")
                pzb = pAb if t % 2 == 0 else pBb
                pzk = "pA.0" if t % 2 == 0 else "pB.0"
                for c in range(DC):
                    tp(pzb[:, c * P:(c + 1) * P], zb[:, c * P:(c + 1) * P],
                       c_identb[:],
                       reads=[(C, "zb", t), ("xtokR", t % 2), "c_identb", pzk],
                       writes=[pzk], tag="aztp")
                zt_view = a_zt.rearrange("p (c q) -> p c q", c=DC)[:, :, t * P:(t + 1) * P]
                pt_view = pzb[:, 0:1024].rearrange("p (c q) -> p c q", c=DC)
                hbks = [("hb", 2 * c + t // 8) for c in range(DC)]
                vop((lambda zv, pv: lambda v: v.tensor_copy(zv, pv))
                    (zt_view, pt_view),
                    reads=[pzk] + hbks,
                    writes=hbks + [(C, "zt", t)], tag="azte")
            ztall = [(C, "zt", t) for t in range(NTB)]
            ztinit = [(C, "zt", t) for t in range(8)]

            for qc in range(DC):
                for hf2 in range(2):
                    ps, k0, k1 = ((pA, "pA.0", "pA.1") if hf2 == 0
                                  else (pB, "pB.0", "pB.1"))
                    for d in range(DC):
                        lhs = a_wq[:, d * D + qc * P:d * D + (qc + 1) * P]
                        mm(ps[:, 0:512], lhs,
                           a_zt[:, d * SEQ + hf2 * 1024:d * SEQ + hf2 * 1024 + 512],
                           d == 0, False,
                           reads=[("w2B", d)] + ztall + [k0],
                           writes=[k0], tag="qm")
                        mm(ps[:, 512:1024], lhs,
                           a_zt[:, d * SEQ + hf2 * 1024 + 512:d * SEQ + (hf2 + 1) * 1024],
                           d == 0, d == DC - 1,
                           reads=[("w2B", d)] + ztall + [k1],
                           writes=[k1], tag="qm")
                    sop((lambda qc, hf2, ps: lambda s: s.activation(
                        a_q[:, qc * SEQ + hf2 * 1024:qc * SEQ + (hf2 + 1) * 1024],
                        ps[:], AF.Copy, bias=0.0, scale=0.125))(qc, hf2, ps),
                        reads=[k0, k1, ("hb", 16 + 2 * qc + hf2)],
                        writes=[("hb", 16 + 2 * qc + hf2), (C, "q", qc, hf2)],
                        tag="qe")
            qall = [(C, "q", qc, hf2) for qc in range(DC) for hf2 in range(2)]

            for kc_ in range(DC):
                for d in range(DC):
                    lhs = a_wk[:, d * D + kc_ * P:d * D + (kc_ + 1) * P]
                    mm(pA[:, 0:512], lhs, a_zt[:, d * SEQ:d * SEQ + 512],
                       d == 0, False, reads=[("w2B", 8 + d)] + ztinit + ["pA.0"],
                       writes=["pA.0"], tag="kim")
                    mm(pA[:, 512:1024], lhs,
                       a_zt[:, d * SEQ + 512:d * SEQ + 1024],
                       d == 0, d == DC - 1,
                       reads=[("w2B", 8 + d)] + ztinit + ["pA.1"],
                       writes=["pA.1"], tag="kim")
                sop((lambda kc_: lambda s: s.activation(
                    a_ki[:, kc_ * 1024:(kc_ + 1) * 1024], pA[:], AF.Copy))(kc_),
                    reads=["pA.0", "pA.1", ("w2B", kc_)],
                    writes=[("w2B", kc_), (C, "ki", kc_)], tag="kie")
            kiall = [(C, "ki", kc_) for kc_ in range(DC)]

            for tt in range(8):
                for d in range(DC):
                    lhs = a_zt[:, d * SEQ + tt * P:d * SEQ + (tt + 1) * P]
                    mm(pY[:, 0:512], lhs, a_wv[:, d * D:d * D + 512],
                       d == 0, False,
                       reads=[("w2B", 16 + d), (C, "zt", tt), "pY.0"],
                       writes=["pY.0"], tag="vim")
                    mm(pY[:, 512:1024], lhs, a_wv[:, d * D + 512:d * D + 1024],
                       d == 0, d == DC - 1,
                       reads=[("w2B", 16 + d), (C, "zt", tt), "pY.1"],
                       writes=["pY.1"], tag="vim")
                sop((lambda tt: lambda s: s.activation(
                    a_vi[:, tt * 1024:(tt + 1) * 1024], pY[:], AF.Copy))(tt),
                    reads=["pY.0", "pY.1", ("xetb", tt)],
                    writes=[("xetb", tt), (C, "vi", tt)], tag="vie")
            for f1 in range(4):
                for d in range(DC):
                    lhs = a_w1c[:, d * 512 + f1 * P:d * 512 + (f1 + 1) * P]
                    mm(pA[:, 0:512], lhs, a_zt[:, d * SEQ:d * SEQ + 512],
                       d == 0, False, reads=w1cK + ztinit + ["pA.0"],
                       writes=["pA.0"], tag="cmp1")
                    mm(pA[:, 512:1024], lhs,
                       a_zt[:, d * SEQ + 512:d * SEQ + 1024],
                       d == 0, d == DC - 1, reads=w1cK + ztinit + ["pA.1"],
                       writes=["pA.1"], tag="cmp1")
                sop((lambda f1: lambda s: s.activation(
                    a_h1[:, f1 * 1024:(f1 + 1) * 1024], pA[:], AF.Relu,
                    bias=s_ab1t[:, f1:f1 + 1], scale=1.0))(f1),
                    reads=["pA.0", "pA.1", "s_ab1t"] + [("xtokR", i) for i in range(4)],
                    writes=[("xtokR", i) for i in range(4)] + [(C, "h1", f1)],
                    tag="crelu")
            h1all = [(C, "h1", f1) for f1 in range(4)]
            for dc in range(DC):
                for f1 in range(4):
                    lhs = a_w2c[:, f1 * 1024 + dc * P:f1 * 1024 + (dc + 1) * P]
                    mm(pB[:, 0:512], lhs, a_h1[:, f1 * 1024:f1 * 1024 + 512],
                       f1 == 0, False,
                       reads=w2cK + h1all + [("xtokR", i) for i in range(4)] +
                             ["pB.0"],
                       writes=["pB.0"], tag="cmp2")
                    mm(pB[:, 512:1024], lhs,
                       a_h1[:, f1 * 1024 + 512:f1 * 1024 + 1024],
                       f1 == 0, f1 == 3,
                       reads=w2cK + h1all + [("xtokR", i) for i in range(4)] +
                             ["pB.1"],
                       writes=["pB.1"], tag="cmp2")
                sop((lambda dc: lambda s: s.activation(
                    s_scrB[:], pB[:], AF.Identity, bias=s_ab2t[:, dc:dc + 1],
                    scale=1.0))(dc),
                    reads=["pB.0", "pB.1", "s_ab2t", "scrB"],
                    writes=["scrB", (C, "cst", dc)], tag="cbias")
                cv4 = s_scrB[:].rearrange("p (g four) -> p g four", four=4)
                vop((lambda cv4: lambda v: v.tensor_add(
                    s_scrA[:, 0:256], cv4[:, :, 0], cv4[:, :, 1]))(cv4),
                    reads=[(C, "cst", dc), "scrA"],
                    writes=["scrA", (C, "cp1", dc)], tag="cp1")
                vop((lambda cv4: lambda v: v.tensor_add(
                    s_scrA[:, 256:512], cv4[:, :, 2], cv4[:, :, 3]))(cv4),
                    reads=[(C, "cst", dc), "scrA", (C, "cp1", dc)],
                    writes=["scrA", (C, "cp2", dc)], tag="cp2")
                vop((lambda: lambda v: v.tensor_add(
                    s_scrA[:, 0:256], s_scrA[:, 0:256], s_scrA[:, 256:512]))(),
                    reads=[(C, "cp1", dc), (C, "cp2", dc), "scrA"],
                    writes=["scrA", (C, "cp3", dc)], tag="cp3")
                sop((lambda dc: lambda s: s.activation(
                    a_cp[:, dc * 256:(dc + 1) * 256], s_scrA[:, 0:256],
                    AF.Copy, bias=0.0, scale=0.25))(dc),
                    reads=[(C, "cp3", dc), ("hb", dc // 4)],
                    writes=[("hb", dc // 4), (C, "cp", dc)], tag="cpool")
            cpall = [(C, "cp", dc) for dc in range(DC)]
            if DEBUG_TAPS and b == 0:
                dma("sync", dbg["cp"], a_cp, reads=cpall, writes=[("dbgw", "cp")],
                    tag="dbgcp")
                dma("sync", dbg["h1c"], a_h1, reads=h1all,
                    writes=[("dbgw", "h1c")], tag="dbgh1")

            for kc_ in range(DC):
                ps, psk = (pA, "pA") if kc_ < 4 else (pY, "pY")
                off = (kc_ % 4) * 256
                pk = psk + (".0" if off < 512 else ".1")
                for d in range(DC):
                    lhs = a_wk[:, d * D + kc_ * P:d * D + (kc_ + 1) * P]
                    mm(ps[:, off:off + 256], lhs, a_cp[:, d * 256:(d + 1) * 256],
                       d == 0, d == DC - 1,
                       reads=[("w2B", 8 + d), ("hb", 0), ("hb", 1)] +
                             cpall + [pk],
                       writes=[pk], tag="kcm")
            for kc_ in range(DC):
                ps, psk = (pA, "pA") if kc_ < 4 else (pY, "pY")
                off = (kc_ % 4) * 256
                pk = psk + (".0" if off < 512 else ".1")
                sop((lambda kc_, ps, off: lambda s: s.activation(
                    a_kc[:, kc_ * 256:(kc_ + 1) * 256], ps[:, off:off + 256],
                    AF.Copy))(kc_, ps, off),
                    reads=[pk, ("w2B", 8 + kc_ // 4)],
                    writes=[("w2B", 8 + kc_ // 4), (C, "kc", kc_)], tag="kce")
            for tt in range(2):
                for d in range(DC):
                    lhs = a_cp[:, d * 256 + tt * P:d * 256 + (tt + 1) * P]
                    mm(pY[:, 0:512], lhs, a_wv[:, d * D:d * D + 512],
                       d == 0, False,
                       reads=[("w2B", 16 + d), ("hb", 0), ("hb", 1)] +
                             cpall + ["pY.0"],
                       writes=["pY.0"], tag="vcm")
                    mm(pY[:, 512:1024], lhs, a_wv[:, d * D + 512:d * D + 1024],
                       d == 0, d == DC - 1,
                       reads=[("w2B", 16 + d), ("hb", 0), ("hb", 1)] +
                             cpall + ["pY.1"],
                       writes=["pY.1"], tag="vcm")
                sop((lambda tt: lambda s: s.activation(
                    a_vc[:, tt * 1024:(tt + 1) * 1024], pY[:], AF.Copy))(tt),
                    reads=["pY.0", "pY.1", ("xtokR", 2 + tt)],
                    writes=[("xtokR", 2 + tt), (C, "vc", tt)], tag="vce")
            viall = [(C, "vi", tt) for tt in range(8)]
            if DEBUG_TAPS and b == 0:
                dma("sync", dbg["vc"], a_vc,
                    reads=[(C, "vc", 0), (C, "vc", 1)],
                    writes=[("dbgw", "vc")], tag="dbgvc")
            vcall = [(C, "vc", tt) for tt in range(2)]

            if DEBUG_TAPS and b == 0:
                dma("sync", dbg["kc"], a_kc,
                    reads=[(C, "kc", i) for i in range(DC)],
                    writes=[("dbgw", "kc")], tag="dbgkc")
            kcall = [(C, "kc", kc_) for kc_ in range(DC)]
            def _bk(k, lo, hi):
                ks = []
                if lo < 512:
                    ks.append(k + ".0")
                if hi > 512:
                    ks.append(k + ".1")
                return ks

            for qi in range(16):
                causal = qi < 8
                kts = (qi + 1) if causal else 2
                kW = kts * P
                qoff = qi * P if causal else 1024 + (qi - 8) * P
                pstat, pstk = (pA, "pA") if (qi % 2 == 0) else (pY, "pY")
                pother, potk = (pY, "pY") if (qi % 2 == 0) else (pA, "pA")
                for h in range(NH):
                    psh, pshk = (pstat, pstk) if h % 2 == 0 else (pother, potk)
                    qsl = a_q[(h % 2) * 64:(h % 2) * 64 + 64,
                              (h // 2) * SEQ + qoff:(h // 2) * SEQ + qoff + P]
                    nblk = (kW + 511) // 512
                    for blk in range(nblk):
                        w0 = blk * 512
                        w1 = min(kW, w0 + 512)
                        if causal:
                            rhs = a_ki[(h % 2) * 64:(h % 2) * 64 + 64,
                                       (h // 2) * 1024 + w0:(h // 2) * 1024 + w1]
                            kkeys = kiall
                        else:
                            rhs = a_kc[(h % 2) * 64:(h % 2) * 64 + 64,
                                       (h // 2) * 256 + w0:(h // 2) * 256 + w1]
                            kkeys = kcall
                        mm(psh[:, w0:w1], qsl, rhs, True, True,
                           reads=qall + kkeys + _bk(pshk, w0, w1),
                           writes=_bk(pshk, w0, w1), tag="qk1")
                    if causal:
                        vop((lambda psh, qi: lambda v: v.tensor_add(
                            psh[:, qi * P:(qi + 1) * P],
                            psh[:, qi * P:(qi + 1) * P], c_diagm[:]))
                            (psh, qi),
                            reads=_bk(pshk, qi * P, (qi + 1) * P) + ["c_diagm"],
                            writes=_bk(pshk, qi * P, (qi + 1) * P), tag="dmask")
                    sop((lambda psh, kW, h: lambda s: s.activation(
                        a_ju[:, 0:kW], psh[:, 0:kW], AF.Exp, bias=0.0,
                        scale=1.0, accum_out=s_aS[:, h:h + 1]))(psh, kW, h),
                        reads=_bk(pshk, 0, kW) + [("w2B", 12)],
                        writes=_bk(pshk, 0, kW) + [("w2B", 12), (C, "S", qi, h)],
                        tag="sexp")
                vop(lambda v: v.reciprocal(s_scrA[:, 0:NH], s_aS[:]),
                    reads=[(C, "S", qi, h) for h in range(NH)] + ["scrA"],
                    writes=["scrA", (C, "recS", qi)], tag="recS")
                vop(lambda v: v.tensor_copy(s_aR[:], s_scrA[:, 0:NH]),
                    reads=[(C, "recS", qi), "scrA", "s_aR"],
                    writes=["s_aR", (C, "aRb", qi)], tag="aRb")
                tp(pCb[0:16, 0:P], s_aR[:], c_identb[:],
                   reads=[(C, "aRb", qi), "c_identb", "pC.0"], writes=["pC.0"],
                   tag="rtp")
                vop(lambda v: v.tensor_copy(s_rb16[:], pCb[0:16, 0:P]),
                    reads=["pC.0", "s_rb16"], writes=["s_rb16", (C, "rb16", qi)],
                    tag="rbe")
                dma("gpsimd", v_rrow, s_rb16[:],
                    reads=[(C, "rb16", qi), "rows"],
                    writes=["rows", (C, "rrow", qi)], tag="rrow")
                def stage1(h):
                    pq, pqk = (pB, "pB") if h % 2 == 0 else (pother, potk)
                    qsl = a_q[(h % 2) * 64:(h % 2) * 64 + 64,
                              (h // 2) * SEQ + qoff:(h // 2) * SEQ + qoff + P]
                    ptb = a_pt[h % 2]
                    ptbK = ("w2B", 10 + h % 2)
                    for kt in range(kts):
                        if causal:
                            ksl = a_ki[(h % 2) * 64:(h % 2) * 64 + 64,
                                       (h // 2) * 1024 + kt * P:(h // 2) * 1024 + (kt + 1) * P]
                            kkeys = kiall
                        else:
                            ksl = a_kc[(h % 2) * 64:(h % 2) * 64 + 64,
                                       (h // 2) * 256 + kt * P:(h // 2) * 256 + (kt + 1) * P]
                            kkeys = kcall
                        bk2 = pqk + (".0" if kt < 4 else ".1")
                        mm(pq[:, kt * P:(kt + 1) * P], ksl, qsl, True, True,
                           reads=qall + kkeys + [bk2], writes=[bk2], tag="qk2")
                    if causal:
                        bk2 = pqk + (".0" if qi < 4 else ".1")
                        vop((lambda pq, qi: lambda v: v.tensor_add(
                            pq[:, qi * P:(qi + 1) * P],
                            pq[:, qi * P:(qi + 1) * P], c_diagmT[:]))(pq, qi),
                            reads=[bk2, "c_diagmT"], writes=[bk2], tag="dmaskT")
                    bkst = _bk(pqk, 0, kW)
                    sop((lambda pq, ptb, kW: lambda s: s.activation(
                        ptb[:, 0:kW], pq[:, 0:kW], AF.Exp))(pq, ptb, kW),
                        reads=bkst + [ptbK],
                        writes=bkst + [ptbK, (C, "apt", qi, h)], tag="ptexp")

                def stage2(h):
                    pvsl = pC[0:64, 512:640] if h % 2 == 0 else pC[0:64, 0:128]
                    pvk = "pC.1" if h % 2 == 0 else "pC.0"
                    bbsl = pstat[0:64, 0:P] if h % 2 == 0 \
                        else pstat[0:64, 512:512 + P]
                    bbk = pstk + (".0" if h % 2 == 0 else ".1")
                    ptb = a_pt[h % 2]
                    for kt in range(kts):
                        if causal:
                            vsl = a_vi[:, kt * 1024 + h * 64:kt * 1024 + h * 64 + 64]
                            vkeys = viall
                        else:
                            vsl = a_vc[:, kt * 1024 + h * 64:kt * 1024 + h * 64 + 64]
                            vkeys = vcall
                        mm(pvsl, vsl, ptb[:, kt * P:(kt + 1) * P],
                           kt == 0, kt == kts - 1,
                           reads=vkeys + [(C, "apt", qi, h), pvk],
                           writes=[pvk], tag="pv")
                    mm(bbsl, c_onesrow_b[0:1, 0:64],
                       v_rrow[0:1, h * P:(h + 1) * P], True, True,
                       reads=["c_onesrow_b", (C, "rrow", qi), bbk],
                       writes=[bbk], tag="bk1")
                    vop((lambda bbsl: lambda v: v.tensor_copy(
                        s_bb[:], bbsl))(bbsl),
                        reads=[bbk, "s_bb"],
                        writes=["s_bb", (C, "bb", qi, h)], tag="bbe")
                    atblk = ("hb", ((h // 2) * 2048 + qoff) // 1024)
                    osl = a_at[(h % 2) * 64:(h % 2) * 64 + 64,
                               (h // 2) * SEQ + qoff:(h // 2) * SEQ + qoff + P]
                    vop((lambda osl, pvsl: lambda v: v.tensor_mul(
                        osl, pvsl, s_bb[:]))(osl, pvsl),
                        reads=[(C, "bb", qi, h), pvk, atblk],
                        writes=[atblk, pvk, (C, "at", qi, h)], tag="ate")

                stage1(0)
                for h in range(1, NH):
                    stage1(h)
                    stage2(h - 1)
                stage2(NH - 1)
            atall = [(C, "at", qi, h) for qi in range(16) for h in range(NH)]

            for oc in range(DC):
                for hf2 in range(2):
                    for dv in range(DC):
                        lhs = a_wo[:, dv * D + oc * P:dv * D + (oc + 1) * P]
                        atbk = ("hb", 2 * dv + hf2)
                        mm(pA[:, 0:512], lhs,
                           a_at[:, dv * SEQ + hf2 * 1024:dv * SEQ + hf2 * 1024 + 512],
                           dv == 0, False,
                           reads=[("w2B", 24 + dv), atbk] + atall + ["pA.0"],
                           writes=["pA.0"], tag="proj")
                        mm(pA[:, 512:1024], lhs,
                           a_at[:, dv * SEQ + hf2 * 1024 + 512:dv * SEQ + (hf2 + 1) * 1024],
                           dv == 0, dv == DC - 1,
                           reads=[("w2B", 24 + dv), atbk] + atall + ["pA.1"],
                           writes=["pA.1"], tag="proj")
                    prj = projT_sl(oc, hf2 * 1024, (hf2 + 1) * 1024)
                    vop((lambda prj: lambda v: v.tensor_copy(prj, pA[:]))(prj),
                        reads=["pA.0", "pA.1"] + projT_keys(oc),
                        writes=projT_keys(oc) + [(C, "prj", oc, hf2)],
                        tag="prje")
            prjall = [(C, "prj", oc, hf2) for oc in range(DC) for hf2 in range(2)]

            for t in range(NTB):
                po, pok = (pB, "pB") if t % 2 == 0 else (pY, "pY")
                pokk = [pok + ".0", pok + ".1"]
                for oc in range(DC):
                    rkey = (("w2B", 4 * oc + t // 4) if oc < 6
                            else ("hb", 16 + 4 * (oc - 6) + t // 4))
                    tp(po[:, oc * P:(oc + 1) * P],
                       projT_sl(oc, t * P, (t + 1) * P), c_identf[:],
                       reads=prjall + ["c_identf", rkey] + pokk, writes=pokk,
                       tag="otp")
                xr = w2f[:, 12288 + (t % 4) * 1024:12288 + (t % 4 + 1) * 1024]
                rx4 = [("w2B", 24 + 2 * (t % 4)), ("w2B", 25 + 2 * (t % 4))]
                dma("sync", xr, bsrc[t * P:(t + 1) * P, :],
                    reads=rx4 + ([("hbm", skey, b * 16 + t)] if skey else []),
                    writes=rx4, tag="aldx2")
                yst = s_yest[t % 2]
                ry = ("yestb", t % 2)
                vop((lambda xr, yst, po: lambda v: v.tensor_add(
                    yst[:], po[:], xr))(xr, yst, po),
                    reads=[ry] + pokk + rx4,
                    writes=[ry, (C, "ost", t)] + pokk + rx4,
                    tag="ores")
                dma("sync", bdst[t * P:(t + 1) * P, :], yst[:],
                    reads=[(C, "ost", t), ry],
                    writes=[ry, (C, "dst", t)] +
                    ([("hbm", dkey, b * 16 + t)] if dkey else []), tag="ostx")

    prelude()
    pr.barrier(tag="prelude_done")
    if "attn" in phases:
        cache_weights()
    cur, ck = x_in, None
    n = len(phases)
    for i, ph in enumerate(phases):
        if i == n - 1:
            dst, dk = out_d, None
        else:
            dst, dk = (xA, "xA") if i == 0 else (xB, "xB")
        if ph == "ffnA":
            ffn("A", cur, dst, ck, dk, fuse=("attn" in phases))
        elif ph == "attn":
            attn(cur, dst, ck, dk, fused=("ffnA" in phases))
        elif ph == "ffnC":
            ffn("C", cur, dst, ck, dk)
        cur, ck = dst, dk

    pr.finalize()
    pr.emit()
    return nc

_NC_CACHE = {}


def kernel(**inputs):
    hs = np.ascontiguousarray(inputs["L0_attn_hidden_states"], dtype=np.float32)
    ncore = 8
    key = PHASES
    if key not in _NC_CACHE:
        _NC_CACHE[key] = build_program(PHASES)
    nc = _NC_CACHE[key]
    common = dict(
        wq=np.ascontiguousarray(inputs["L0_attn_Wq"], dtype=np.float32),
        wk=np.ascontiguousarray(inputs["L0_attn_Wk"], dtype=np.float32),
        wv=np.ascontiguousarray(inputs["L0_attn_Wv"], dtype=np.float32),
        wo=np.ascontiguousarray(inputs["L0_attn_Wo"], dtype=np.float32),
        aw1=np.ascontiguousarray(inputs["L0_attn_W1"], dtype=np.float32),
        ab1=np.ascontiguousarray(inputs["L0_attn_b1"], dtype=np.float32).reshape(1, 512),
        aw2=np.ascontiguousarray(inputs["L0_attn_W2"], dtype=np.float32),
        ab2=np.ascontiguousarray(inputs["L0_attn_b2"], dtype=np.float32).reshape(1, D),
        wg=np.ascontiguousarray(inputs["L0_ffn_Wg"], dtype=np.float32),
        fw1=np.ascontiguousarray(inputs["L0_ffn_W1"], dtype=np.float32),
        fb1=np.ascontiguousarray(inputs["L0_ffn_b1"], dtype=np.float32),
        fw2=np.ascontiguousarray(inputs["L0_ffn_W2"], dtype=np.float32),
        fb2=np.ascontiguousarray(inputs["L0_ffn_b2"], dtype=np.float32),
    )
    in_maps = []
    for c in range(ncore):
        sel = np.zeros((NE, 2), dtype=np.float32)
        sel[:c, 0] = 1.0
        sel[:, 1] = 1.0
        m = dict(common)
        m["x"] = np.ascontiguousarray(hs[2 * c:2 * c + 2].reshape(NTOK, D))
        m["selrank"] = sel
        in_maps.append(m)
    res = run_bass_kernel_spmd(nc, in_maps, core_ids=list(range(ncore)))
    outs = [res.results[c]["out"].reshape(2, SEQ, D) for c in range(ncore)]
    return np.concatenate(outs, axis=0)


# version 18
# speedup vs baseline: 1.1058x; 1.1058x over previous
import numpy as np
import ml_dtypes
import concourse.bass as bass
import concourse.mybir as mybir
from concourse.bass_utils import run_bass_kernel_spmd

F32 = mybir.dt.float32
BF16 = mybir.dt.bfloat16
I32 = mybir.dt.int32
U32 = mybir.dt.uint32
AF = mybir.ActivationFunctionType
ALU = mybir.AluOpType
AX = mybir.AxisListType
BF = ml_dtypes.bfloat16
IOOA = bass.IndirectOffsetOnAxis

P = 128
D = 1024
DC = 8
NB = 2
SEQ = 2048
NTOK = 4096
NT = 32
NE = 8
CAP = 2048
GCAP = 16384
FF = 4096
FC = 32
NH = 16
BIGIDX = 1.0e7

PHASES = ("ffnA", "attn", "ffnC")
DEBUG_TAPS = False

ENGINES = ("tensor", "vector", "scalar", "gpsimd", "sync")
ESEM = {"tensor": "pe", "vector": "dve", "scalar": "act", "gpsimd": "pool", "sync": "sp"}
QSEM = {"sync": "qsp", "scalar": "qact", "gpsimd": "qpool"}


class Prog:

    QROT = {"qsp": 24, "qact": 6, "qpool": 24}

    def __init__(self, nc):
        self.nc = nc
        self.ops = []
        self.eng_ops = {e: [] for e in ENGINES}
        self.last_writer = {}
        self.readers = {}
        names = set(ESEM.values())
        for q, k in self.QROT.items():
            for i in range(k):
                names.add(f"{q}{i}")
        self.sems = {s: nc.alloc_semaphore(s) for s in names}

    def op(self, eng, fn, reads=(), writes=(), dma=False, tag=""):
        o = dict(eng=eng, fn=fn, dma=dma, tag=tag, deps=[], need_inc=False,
                 seq=len(self.ops))
        for r in reads:
            w = self.last_writer.get(r)
            if w is not None:
                o["deps"].append(w)
        for w_ in writes:
            lw = self.last_writer.get(w_)
            if lw is not None:
                o["deps"].append(lw)
            for rd in self.readers.get(w_, ()):
                o["deps"].append(rd)
        for w_ in writes:
            self.last_writer[w_] = o
            self.readers[w_] = []
        for r in reads:
            self.readers.setdefault(r, []).append(o)
        self.ops.append(o)
        self.eng_ops[eng].append(o)
        return o

    def barrier(self, tag="bar"):
        tails = [ops[-1] for ops in self.eng_ops.values() if ops]
        for eng in ENGINES:
            o = self.op(eng, None, tag=tag)
            o["deps"] = list(tails)

    def finalize(self):
        for o in self.ops:
            dd, seen = [], set()
            for d in o["deps"]:
                if d is o:
                    continue
                if (d["eng"] == "tensor" and o["eng"] == "tensor"
                        and not d["dma"] and not o["dma"]):
                    continue
                if id(d) in seen:
                    continue
                seen.add(id(d))
                dd.append(d)
                d["need_inc"] = True
            o["deps"] = dd
        cnt = {s: 0 for s in self.sems}
        qidx = {q: 0 for q in self.QROT}
        for o in self.ops:
            if o["fn"] is None:
                o["sem"] = None
                o["incval"] = None
                continue
            if o["dma"]:
                q = QSEM[o["eng"]]
                i = qidx[q]
                qidx[q] += 1
                s = f"{q}{i % self.QROT[q]}"
                cnt[s] += 16
                o["sem"], o["wait_target"], o["incval"] = s, cnt[s], 16
            else:
                s = ESEM[o["eng"]]
                if o["need_inc"]:
                    cnt[s] += 1
                    o["sem"], o["wait_target"], o["incval"] = s, cnt[s], 1
                else:
                    o["sem"], o["incval"] = None, None
        self.final_counts = dict(cnt)
        for o in self.ops:
            w = {}
            stack = list(o["deps"])
            depth = 0
            while stack and depth < 100000:
                depth += 1
                d = stack.pop()
                if d.get("sem") is None:
                    stack.extend(d["deps"])
                    continue
                prev = w.get(d["sem"], 0)
                if d["wait_target"] > prev:
                    w[d["sem"]] = d["wait_target"]
            o["waits"] = w
        for eng in ENGINES:
            have = {}
            for o in self.eng_ops[eng]:
                pruned = {}
                for s, v in o.get("waits", {}).items():
                    if have.get(s, 0) < v:
                        pruned[s] = v
                        have[s] = v
                o["waits"] = pruned
        return self

    def emit(self):
        prog = self
        fc = prog.final_counts

        def mk(eng_name):
            def body(eng):
                for o in prog.eng_ops[eng_name]:
                    for s, v in sorted(o.get("waits", {}).items()):
                        eng.wait_ge(prog.sems[s], v)
                    if o["fn"] is None:
                        continue
                    inst = o["fn"](eng)
                    if o["incval"]:
                        inst.then_inc(prog.sems[o["sem"]], o["incval"])
                if eng_name == "sync":
                    for s, v in sorted(fc.items()):
                        if v and (s[0] == "q"):
                            eng.wait_ge(prog.sems[s], v)
            return body

        with self.nc.Block() as blk:
            blk.tensor(mk("tensor"))
            blk.vector(mk("vector"))
            blk.scalar(mk("scalar"))
            blk.gpsimd(mk("gpsimd"))
            blk.sync(mk("sync"))


def build_program(phases=PHASES, debug=None):
    global DEBUG_TAPS
    if debug is not None:
        DEBUG_TAPS = debug
    nc = bass.Bass("TRN2", target_bir_lowering=False)

    x_in = nc.dram_tensor("x", [NTOK, D], F32, kind="ExternalInput").ap()
    wq_d = nc.dram_tensor("wq", [D, D], F32, kind="ExternalInput").ap()
    wk_d = nc.dram_tensor("wk", [D, D], F32, kind="ExternalInput").ap()
    wv_d = nc.dram_tensor("wv", [D, D], F32, kind="ExternalInput").ap()
    wo_d = nc.dram_tensor("wo", [D, D], F32, kind="ExternalInput").ap()
    aw1_d = nc.dram_tensor("aw1", [D, 512], F32, kind="ExternalInput").ap()
    ab1_d = nc.dram_tensor("ab1", [1, 512], F32, kind="ExternalInput").ap()
    aw2_d = nc.dram_tensor("aw2", [512, D], F32, kind="ExternalInput").ap()
    ab2_d = nc.dram_tensor("ab2", [1, D], F32, kind="ExternalInput").ap()
    wg_d = nc.dram_tensor("wg", [D, NE], F32, kind="ExternalInput").ap()
    fw1_d = nc.dram_tensor("fw1", [NE, D, FF], F32, kind="ExternalInput").ap()
    fb1_d = nc.dram_tensor("fb1", [NE, FF], F32, kind="ExternalInput").ap()
    fw2_d = nc.dram_tensor("fw2", [NE, FF, D], F32, kind="ExternalInput").ap()
    fb2_d = nc.dram_tensor("fb2", [NE, D], F32, kind="ExternalInput").ap()
    selrank_d = nc.dram_tensor("selrank", [NE, 2], F32, kind="ExternalInput").ap()
    out_d = nc.dram_tensor("out", [NTOK, D], F32, kind="ExternalOutput").ap()
    dbg = {}
    if DEBUG_TAPS:
        for nm, sh, dt in [("idxf", [P, 2 * NT], F32), ("posl", [P, 2 * NT], F32),
                           ("posg", [P, 2 * NT], F32), ("keep", [P, 2 * NT], F32),
                           ("wk", [P, 2 * NT], F32), ("S8", [P, NT], F32),
                           ("m12", [P, 2 * NT], F32), ("idxi", [P, 2 * NT], I32),
                           ("masks", [P, 16 * NT], F32)]:
            dbg[nm] = nc.dram_tensor("dbg_" + nm, sh, dt, kind="ExternalOutput").ap()
        dbg["xe"] = nc.dram_tensor("dbg_xe", [NE * CAP, D], BF16, kind="ExternalOutput").ap()
        dbg["xet0"] = nc.dram_tensor("dbg_xet0", [P, DC * 1024], BF16, kind="ExternalOutput").ap()
        dbg["cp"] = nc.dram_tensor("dbg_cp", [P, 2048], BF16, kind="ExternalOutput").ap()
        dbg["kc"] = nc.dram_tensor("dbg_kc", [P, 2048], BF16, kind="ExternalOutput").ap()
        dbg["vc"] = nc.dram_tensor("dbg_vc", [P, 2048], BF16, kind="ExternalOutput").ap()
        dbg["h1c"] = nc.dram_tensor("dbg_h1c", [P, 4096], BF16, kind="ExternalOutput").ap()
        dbg["h0"] = nc.dram_tensor("dbg_h0", [P, FC * 1024], BF16, kind="ExternalOutput").ap()
        dbg["ye"] = nc.dram_tensor("dbg_ye", [NE * CAP, D], F32, kind="ExternalOutput").ap()

    il = nc.inline_tensor
    ident_b = il(np.eye(P, dtype=BF), "c_identb_d").ap()
    ident_f = il(np.eye(P, dtype=np.float32), "c_identf_d").ap()
    ut128f_d = il(np.triu(np.ones((P, P), dtype=np.float32), 1), "c_ut128_d").ap()
    ut32f_d = il(np.triu(np.ones((32, 32), dtype=np.float32), 1), "c_ut32_d").ap()
    iota8_d = il(np.broadcast_to(np.arange(NE, dtype=np.float32), (P, NE)).copy(),
                 "c_iota8_d").ap()
    onesrow_f_d = il(np.ones((1, P), dtype=np.float32), "c_onesrowf_d").ap()
    onesrow_b_d = il(np.ones((1, P), dtype=BF), "c_onesrowb_d").ap()
    onescol_f_d = il(np.ones((P, 1), dtype=np.float32), "c_onescolf_d").ap()
    onescol32_d = il(np.ones((32, 1), dtype=np.float32), "c_onescol32_d").ap()
    qq, kk = np.meshgrid(np.arange(P), np.arange(P), indexing="ij")
    diag_np = np.where(kk <= qq, 0.0, -1e30).astype(np.float32)
    diagm_d = il(diag_np, "c_diagm_d").ap()
    diagmT_d = il(np.ascontiguousarray(diag_np.T), "c_diagmT_d").ap()

    xA = nc.dram_tensor("xA", [NTOK, D], F32, kind="Internal").ap()
    xB = nc.dram_tensor("xB", [NTOK, D], F32, kind="Internal").ap()
    xe_dram = nc.dram_tensor("xe_dram", [NE * CAP, D], BF16, kind="Internal").ap()
    ye_dram = nc.dram_tensor("ye_dram", [NE * CAP, D], F32, kind="Internal").ap()
    cc_in = nc.dram_tensor("cc_in", [1, 16], F32, kind="Internal").ap()
    cc_out = nc.dram_tensor("cc_out", [NE, 16], F32, kind="Internal").ap()
    wqb = nc.dram_tensor("wqb", [D, D], BF16, kind="Internal").ap()
    wkb = nc.dram_tensor("wkb", [D, D], BF16, kind="Internal").ap()
    wvb = nc.dram_tensor("wvb", [D, D], BF16, kind="Internal").ap()
    wob = nc.dram_tensor("wob", [D, D], BF16, kind="Internal").ap()
    aw1b = nc.dram_tensor("aw1b", [D, 512], BF16, kind="Internal").ap()
    aw2b = nc.dram_tensor("aw2b", [512, D], BF16, kind="Internal").ap()
    fw1b = nc.dram_tensor("fw1b", [NE, D, FF], BF16, kind="Internal").ap()
    fw2b = nc.dram_tensor("fw2b", [NE, FF, D], BF16, kind="Internal").ap()

    pr = Prog(nc)
    sb = nc.alloc_sbuf_tensor

    c_identb = sb("c_identb", [P, P], BF16)
    c_identf = sb("c_identf", [P, P], F32)
    c_ut128f = sb("c_ut128f", [P, P], F32)
    c_ut32f = sb("c_ut32f", [32, 32], F32)
    c_iota8 = sb("c_iota8", [P, NE], F32)
    c_onesrow_f = sb("c_onesrow_f", [1, P], F32)
    c_onesrow_b = sb("c_onesrow_b", [1, P], BF16)
    c_onescol_f = sb("c_onescol_f", [P, 1], F32)
    c_onescol32_f = sb("c_onescol32_f", [32, 1], F32)
    c_diagm = sb("c_diagm", [P, P], F32)
    c_diagmT = sb("c_diagmT", [P, P], F32)
    c_selrank = sb("c_selrank", [NE, 2], F32)
    s_wg = sb("s_wg", [P, DC * NE], F32)
    s_fb1t = sb("s_fb1t", [P, NE * FC], F32)
    s_ab1t = sb("s_ab1t", [P, 4], F32)
    s_ab2t = sb("s_ab2t", [P, 8], F32)

    s_h = sb("s_h", [P, 32768], BF16)
    s_w2 = sb("s_w2", [P, 32768], BF16)
    s_xet = sb("s_xet", [P, 8192], BF16)
    s_xtok = sb("s_xtok", [P, 4096], BF16)
    s_w1w = [sb(f"s_w1w{i}", [P, 4096], BF16) for i in range(2)]

    s_scrA = sb("s_scrA", [P, 1024], F32)
    s_scrB = sb("s_scrB", [P, 1024], F32)
    s_xt = [sb(f"s_xt{i}", [P, D], F32) for i in range(2)]
    s_yest = [sb(f"s_yest{i}", [P, D], F32) for i in range(2)]

    s_rows = sb("s_rows", [1, 1024], F32)
    v_rep = s_rows[0:1, 0:512]
    v_toffrow = s_rows[0:1, 512:1024]
    v_rrow = s_rows[:].bitcast(BF16)

    s_mean = sb("s_mean", [P, NT], F32)
    s_ssq = sb("s_ssq", [P, NT], F32)
    s_rstd = sb("s_rstd", [P, NT], F32)
    s_negm1 = sb("s_negm1", [P, NT], F32)
    s_S8 = sb("s_S8", [P, NT], F32)
    s_m12 = sb("s_m12", [P, 2 * NT], F32)
    s_idxf = sb("s_idxf", [P, 2 * NT], F32)
    s_masksf = sb("s_masksf", [P, 16 * NT], F32)
    s_posl = sb("s_posl", [P, 2 * NT], F32)
    s_posg = sb("s_posg", [P, 2 * NT], F32)
    s_keep = sb("s_keep", [P, 2 * NT], F32)
    s_wk_ = sb("s_wk", [P, 2 * NT], F32)
    s_idxi = sb("s_idxi", [P, 2 * NT], I32)
    s_ta = sb("s_ta", [P, 2 * NT], F32)
    s_tb = sb("s_tb", [P, 2 * NT], F32)
    s_l8p = [sb(f"s_l8p{i}", [P, NE], F32) for i in range(2)]
    s_mx8p = [sb(f"s_mx8p{i}", [P, NE], F32) for i in range(2)]
    s_ix8p = [sb(f"s_ix8p{i}", [P, NE], U32) for i in range(2)]
    s_junk8 = sb("s_junk8", [P, NE], F32)
    s_t32 = sb("s_t32", [32, 16], F32)
    s_toff = sb("s_toff", [32, 16], F32)
    s_lt = sb("s_lt", [1, 16], F32)
    s_ag = sb("s_ag", [NE, 16], F32)
    s_s1 = sb("s_s1", [1, 16], F32)
    s_s2 = sb("s_s2", [1, 16], F32)
    s_goff = sb("s_goff", [1, 16], F32)
    s_loff = sb("s_loff", [1, 16], F32)
    s_aS = sb("s_aS", [P, NH], F32)
    s_aR = sb("s_aR", [P, NH], BF16)
    s_rb16 = sb("s_rb16", [16, P], BF16)
    s_bb = sb("s_bb", [64, P], BF16)

    pA = nc.alloc_psum_tensor("pA", [P, 1024], F32)
    pB = nc.alloc_psum_tensor("pB", [P, 1024], F32)
    pY = nc.alloc_psum_tensor("pY", [P, 1024], F32)
    pC = nc.alloc_psum_tensor("pC", [P, 1024], F32)
    pAb = pA[:].bitcast(BF16)
    pBb = pB[:].bitcast(BF16)
    pCb = pC[:].bitcast(BF16)

    v_fb1raw = s_scrA[0:32, :]
    v_abraw = s_scrA[0:8, 0:128]
    v_ab1raw = s_scrA[0:4, 128:256]
    v_fb2row = s_scrA[0:1, 0:1024]

    bcreg = {}

    def init_bcreg(g):
        r = g.alloc_register("bcreg")
        g.reg_mov(r, NE * CAP - 1)
        bcreg["r"] = r
        return g.engine_nop() if hasattr(g, "engine_nop") else g.nop()

    pr.op("gpsimd", init_bcreg, tag="bcreg")

    def dma(eng, dst, src, reads=(), writes=(), tag="dma"):
        pr.op(eng, (lambda d, s: lambda e: e.dma_start(d, s))(dst, src),
              reads=reads, writes=writes, dma=True, tag=tag)

    def mm(dst, lhsT, rhs, start, stop, reads, writes, tag="mm"):
        pr.op("tensor",
              (lambda d, l, r, st, sp:
               lambda t: t.matmul(d, l, r, start=st, stop=sp))
              (dst, lhsT, rhs, start, stop),
              reads=reads, writes=writes, tag=tag)

    def tp(dst, src, ident, reads, writes, tag="tp"):
        pr.op("tensor",
              (lambda d, s, i: lambda t: t.transpose(d, s, i))(dst, src, ident),
              reads=reads, writes=writes, tag=tag)

    def vop(fn, reads, writes, tag="v"):
        pr.op("vector", fn, reads=reads, writes=writes, tag=tag)

    def sop(fn, reads, writes, tag="s"):
        pr.op("scalar", fn, reads=reads, writes=writes, tag=tag)

    def prelude():
        for dst, src in [
            (c_identb, ident_b), (c_identf, ident_f), (c_ut128f, ut128f_d),
            (c_ut32f, ut32f_d), (c_iota8, iota8_d), (c_onesrow_f, onesrow_f_d),
            (c_onesrow_b, onesrow_b_d), (c_onescol_f, onescol_f_d),
            (c_onescol32_f, onescol32_d), (c_diagm, diagm_d),
            (c_diagmT, diagmT_d), (c_selrank, selrank_d),
        ]:
            dma("sync", dst[:], src, writes=[dst.name], tag="const")
        dma("sync", s_wg[:].rearrange("p (c e) -> p c e", c=DC),
            wg_d.rearrange("(c p) e -> p c e", p=P), writes=["s_wg"], tag="wg")
        dma("sync", v_fb1raw.rearrange("c (ee p) -> c ee p", ee=NE),
            fb1_d.rearrange("ee (c p) -> c ee p", p=P),
            writes=["scrA"], tag="fb1")
        for e in range(NE):
            tp(pC[0:P, 0:32], v_fb1raw[:, e * P:(e + 1) * P],
               c_identf[0:32, 0:32],
               reads=["scrA", "c_identf", "pC.0"], writes=["pC.0"], tag="fb1t")
            vop((lambda e: lambda v: v.tensor_copy(
                s_fb1t[:, e * FC:(e + 1) * FC], pC[0:P, 0:32]))(e),
                reads=["pC.0"], writes=["s_fb1t"], tag="fb1e")
        dma("sync", v_ab1raw, ab1_d.rearrange("o (c p) -> (o c) p", p=P),
            reads=["scrA"], writes=["scrA"], tag="ab1")
        tp(pC[0:P, 32:36], v_ab1raw, c_identf[0:4, 0:4],
           reads=["scrA", "c_identf", "pC.0"], writes=["pC.0"], tag="ab1t")
        vop(lambda v: v.tensor_copy(s_ab1t[:], pC[0:P, 32:36]),
            reads=["pC.0"], writes=["s_ab1t"], tag="ab1e")
        dma("sync", v_abraw, ab2_d.rearrange("o (c p) -> (o c) p", p=P),
            reads=["scrA"], writes=["scrA"], tag="ab2")
        tp(pC[0:P, 36:44], v_abraw, c_identf[0:8, 0:8],
           reads=["scrA", "c_identf", "pC.0"], writes=["pC.0"], tag="ab2t")
        vop(lambda v: v.tensor_copy(s_ab2t[:], pC[0:P, 36:44]),
            reads=["pC.0"], writes=["s_ab2t"], tag="ab2e")

    def cache_weights():
        for nm, dd, ss in (("wq", wqb, wq_d), ("wk", wkb, wk_d),
                           ("wv", wvb, wv_d), ("wo", wob, wo_d),
                           ("aw1", aw1b, aw1_d), ("aw2", aw2b, aw2_d)):
            dma("gpsimd", dd, ss, writes=[("wbc", nm)], tag="wcc")

    def xstage(t, deep=True):
        if not deep:
            return s_xt[t % 2][:], [("xtb", t % 2)]
        if t % 4 < 2:
            return s_xt[t % 4][:], [("xtb", t % 4)]
        j = t % 4 - 2
        xw = s_w1w[0][:].bitcast(F32)
        return (xw[:, j * 1024:(j + 1) * 1024],
                [("w1b", 0, 4 * j + i) for i in range(4)])

    def ln_pass(C, src_hbm, n_tiles, skey=None, soff=0, deep=True):
        def hk(t):
            return [("hbm", skey, soff + t)] if skey else []
        for t in range(n_tiles):
            xt, rx = xstage(t, deep)
            dma("sync" if t % 2 == 0 else "scalar",
                xt, src_hbm[t * P:(t + 1) * P, :],
                reads=rx + hk(t), writes=rx, tag="ldx")
            vop((lambda t, xt: lambda v: v.tensor_reduce(
                s_mean[:, t:t + 1], xt, AX.X, ALU.add))(t, xt),
                reads=rx, writes=[(C, "mean", t)], tag="mean")
            sop((lambda t, xt: lambda s: s.activation(
                s_scrA[:], xt, AF.Square, bias=0.0, scale=1.0 / 32.0,
                accum_out=s_ssq[:, t:t + 1]))(t, xt),
                reads=rx + ["scrA"], writes=[(C, "ssq", t), "scrA"],
                tag="ssq")
        allm = [(C, "mean", t) for t in range(n_tiles)]
        allq = [(C, "ssq", t) for t in range(n_tiles)]
        vop(lambda v: v.tensor_scalar(
            s_mean[:, 0:n_tiles], s_mean[:, 0:n_tiles], 1.0 / D, None, ALU.mult),
            reads=allm, writes=[(C, "meanN")], tag="meanN")
        vop(lambda v: v.tensor_mul(
            s_rstd[:, 0:n_tiles], s_mean[:, 0:n_tiles], s_mean[:, 0:n_tiles]),
            reads=[(C, "meanN")], writes=[(C, "m2")], tag="m2")
        vop(lambda v: v.tensor_sub(
            s_rstd[:, 0:n_tiles], s_rstd[:, 0:n_tiles], s_ssq[:, 0:n_tiles]),
            reads=[(C, "m2")] + allq, writes=[(C, "negvar0")], tag="negvar")
        vop(lambda v: v.tensor_scalar(
            s_rstd[:, 0:n_tiles], s_rstd[:, 0:n_tiles], -1e-5, None, ALU.add),
            reads=[(C, "negvar0")], writes=[(C, "negvar")], tag="nveps")
        sop(lambda s: s.activation(
            s_rstd[:, 0:n_tiles], s_rstd[:, 0:n_tiles], AF.Ln,
            bias=0.0, scale=-1.0),
            reads=[(C, "negvar")], writes=[(C, "lnvar")], tag="lnvar")
        sop(lambda s: s.activation(
            s_rstd[:, 0:n_tiles], s_rstd[:, 0:n_tiles], AF.Exp,
            bias=0.0, scale=-0.5),
            reads=[(C, "lnvar")], writes=[(C, "rstd")], tag="rstd")

    def ffn(call, src_hbm, dst_hbm, skey=None, dkey=None, fuse=False):
        C = f"f{call}"

        def hk(t):
            return [("hbm", skey, t)] if skey else []

        def pass2_tile(t, stg, rsk):
            xt, rx = stg
            vop((lambda t, xt: lambda v: v.tensor_scalar(
                s_scrA[:], xt, s_mean[:, t:t + 1], s_rstd[:, t:t + 1],
                ALU.subtract, ALU.mult))(t, xt),
                reads=rx + [rsk, "scrA"],
                writes=["scrA", (C, "zf", t)], tag="znorm")
            sop((lambda t: lambda s: s.activation(
                s_h[:, t * D:(t + 1) * D], s_scrA[:], AF.Identity,
                bias=0.0, scale=1.0))(t),
                reads=[(C, "zf", t), "scrA", ("hb", t)], writes=[("hb", t)],
                tag="zcast")
            pz = pA if t % 2 == 0 else pB
            pzk = ["pA.0", "pA.1"] if t % 2 == 0 else ["pB.0", "pB.1"]
            for c in range(DC):
                tp(pz[:, c * P:(c + 1) * P], s_scrA[:, c * P:(c + 1) * P],
                   c_identf[:],
                   reads=[(C, "zf", t), "scrA", "c_identf"] + pzk,
                   writes=pzk, tag="ztp")
            vop((lambda pz: lambda v: v.tensor_copy(s_scrB[:], pz[:]))(pz),
                reads=pzk + ["scrB"], writes=["scrB", (C, "zt", t)], tag="zte")
            for c in range(DC):
                mm(pC[:, 0:NE], s_scrB[:, c * P:(c + 1) * P],
                   s_wg[:, c * NE:(c + 1) * NE], c == 0, c == DC - 1,
                   reads=[(C, "zt", t), "scrB", "s_wg", "pC.0"],
                   writes=["pC.0"], tag="gate")
            l8 = s_l8p[t % 2]
            mx8 = s_mx8p[t % 2]
            ix8 = s_ix8p[t % 2]
            rl8, rmx, rix = ("l8b", t % 2), ("mx8b", t % 2), ("ix8b", t % 2)
            vop((lambda l8: lambda v: v.tensor_copy(l8[:], pC[:, 0:NE]))(l8),
                reads=["pC.0", rl8], writes=[rl8, (C, "l8", t)], tag="l8e")
            vop((lambda l8, mx8: lambda v: v.max(mx8[:], l8[:]))(l8, mx8),
                reads=[(C, "l8", t), rmx], writes=[rmx, (C, "mx8", t)], tag="max8")
            vop((lambda l8, mx8, ix8: lambda v: v.max_index(ix8[:], mx8[:], l8[:]))
                (l8, mx8, ix8),
                reads=[(C, "l8", t), (C, "mx8", t), rix],
                writes=[rix, (C, "ix8", t)], tag="idx8")
            vop((lambda t, mx8: lambda v: v.tensor_copy(
                s_m12[:, 2 * t:2 * t + 2], mx8[:, 0:2]))(t, mx8),
                reads=[(C, "mx8", t)], writes=[(C, "m12", t)], tag="m12")
            vop((lambda t, mx8: lambda v: v.tensor_scalar(
                s_negm1[:, t:t + 1], mx8[:, 0:1], -1.0, None, ALU.mult))(t, mx8),
                reads=[(C, "mx8", t)], writes=[(C, "negm1", t)], tag="negm1")
            sop((lambda t, l8: lambda s: s.activation(
                s_junk8[:], l8[:], AF.Exp, bias=s_negm1[:, t:t + 1], scale=1.0,
                accum_out=s_S8[:, t:t + 1]))(t, l8),
                reads=[(C, "l8", t), (C, "negm1", t)], writes=[(C, "S8", t)],
                tag="gexp")
            vop((lambda t, ix8: lambda v: v.tensor_copy(
                s_idxf[:, 2 * t:2 * t + 2], ix8[:, 0:2]))(t, ix8),
                reads=[(C, "ix8", t)], writes=[(C, "idxf", t)], tag="idxf")
            for j in range(2):
                vop((lambda t, j: lambda v: v.tensor_scalar(
                    s_masksf[:, 16 * t + 8 * j:16 * t + 8 * j + 8],
                    c_iota8[:], s_idxf[:, 2 * t + j:2 * t + j + 1], None,
                    ALU.is_equal))(t, j),
                    reads=[(C, "idxf", t), "c_iota8"],
                    writes=[(C, "mask", t, j)], tag="mask")

        staged = {}

        def grp_stats(g):
            for a in (2 * g, 2 * g + 1):
                xt, rx = xstage(a)
                staged[a] = (xt, rx)
                dma("sync" if a % 2 == 0 else "scalar",
                    xt, src_hbm[a * P:(a + 1) * P, :],
                    reads=rx + hk(a), writes=rx, tag="ldx2")
                zbj = s_xtok[:, (a % 2) * 1024:(a % 2 + 1) * 1024]
                rzb = ("xtokR", a % 2)
                sop((lambda xt, zbj, a: lambda s: s.activation(
                    zbj, xt, AF.Identity, bias=0.0, scale=1.0 / D,
                    accum_out=s_mean[:, a:a + 1]))(xt, zbj, a),
                    reads=rx + [rzb], writes=[rzb, (C, "fm", a)], tag="pmean")
                sop((lambda xt, zbj, a: lambda s: s.activation(
                    zbj, xt, AF.Square, bias=0.0, scale=1.0 / 32.0,
                    accum_out=s_ssq[:, a:a + 1]))(xt, zbj, a),
                    reads=rx + [rzb], writes=[rzb, (C, "fq", a)], tag="pssq")

        def grp_rstd(g):
            c0, c1 = 2 * g, 2 * g + 2
            vop((lambda c0, c1: lambda v: v.tensor_mul(
                s_rstd[:, c0:c1], s_mean[:, c0:c1], s_mean[:, c0:c1]))(c0, c1),
                reads=[(C, "fm", c0), (C, "fm", c0 + 1)],
                writes=[(C, "fv", g)], tag="pm2")
            vop((lambda c0, c1: lambda v: v.tensor_sub(
                s_rstd[:, c0:c1], s_rstd[:, c0:c1], s_ssq[:, c0:c1]))(c0, c1),
                reads=[(C, "fq", c0), (C, "fq", c0 + 1), (C, "fv", g)],
                writes=[(C, "fv", g)], tag="pnv")
            vop((lambda c0, c1: lambda v: v.tensor_scalar(
                s_rstd[:, c0:c1], s_rstd[:, c0:c1], -1e-5, None, ALU.add))
                (c0, c1),
                reads=[(C, "fv", g)], writes=[(C, "fv", g)], tag="pnve")
            sop((lambda c0, c1: lambda s: s.activation(
                s_rstd[:, c0:c1], s_rstd[:, c0:c1], AF.Ln, bias=0.0,
                scale=-1.0))(c0, c1),
                reads=[(C, "fv", g)], writes=[(C, "fv", g)], tag="plnv")
            sop((lambda c0, c1: lambda s: s.activation(
                s_rstd[:, c0:c1], s_rstd[:, c0:c1], AF.Exp, bias=0.0,
                scale=-0.5))(c0, c1),
                reads=[(C, "fv", g)], writes=[(C, "frs2", g)], tag="prstd")

        grp_stats(0)
        for g in range(1, NT // 2):
            grp_stats(g)
            grp_rstd(g - 1)
            for t in (2 * (g - 1), 2 * (g - 1) + 1):
                pass2_tile(t, staged[t], (C, "frs2", g - 1))
        grp_rstd(NT // 2 - 1)
        for t in (NT - 2, NT - 1):
            pass2_tile(t, staged[t], (C, "frs2", NT // 2 - 1))


        allmask = [(C, "mask", t, j) for t in range(NT) for j in range(2)]
        allidxf = [(C, "idxf", t) for t in range(NT)]
        allm12 = [(C, "m12", t) for t in range(NT)]
        allS8 = [(C, "S8", t) for t in range(NT)]

        mm(pB[:, 0:512], c_ut128f[:], s_masksf[:], True, False,
           reads=["c_ut128f", "pB.0"] + allmask, writes=["pB.0"], tag="utg")
        mm(pA[:, 0:512], c_ut128f[:], s_masksf[:], True, False,
           reads=["c_ut128f", "pA.0"] + allmask, writes=["pA.0"], tag="utl")
        mm(pY[0:1, 0:512], c_onescol_f[:], s_masksf[:], True, True,
           reads=["c_onescol_f", "pY.0"] + allmask, writes=["pY.0"], tag="tot")
        vop(lambda v: v.tensor_copy(v_rep, pY[0:1, 0:512]),
            reads=["pY.0", "rows"], writes=["rows", (C, "trow")], tag="trowe")
        dma("gpsimd", s_t32[:], v_rep,
            reads=[(C, "trow")], writes=[(C, "t32")], tag="t32")
        mm(pC[0:32, 0:16], c_ut32f[:], s_t32[:], True, True,
           reads=["c_ut32f", (C, "t32"), "pC.0"], writes=["pC.0"], tag="ut32")
        vop(lambda v: v.tensor_copy(s_toff[:], pC[0:32, 0:16]),
            reads=["pC.0"], writes=[(C, "toff")], tag="toffe")
        mm(pC[0:1, 16:32], c_onescol32_f[:], s_t32[:], True, True,
           reads=["c_onescol32_f", (C, "t32"), "pC.0"], writes=["pC.0"], tag="ltm")
        vop(lambda v: v.tensor_copy(s_lt[:], pC[0:1, 16:32]),
            reads=["pC.0"], writes=[(C, "lt")], tag="lte")
        dma("gpsimd", v_toffrow, s_toff[:],
            reads=[(C, "toff"), "rows"], writes=["rows", (C, "toffrow")],
            tag="toffrow")
        dma("gpsimd", cc_in, s_lt[:], reads=[(C, "lt"), "ccio"],
            writes=["ccio", (C, "ccin")], tag="ccw")
        pr.op("gpsimd", lambda g: g.collective_compute(
            "AllGather", ALU.bypass, replica_groups=[list(range(NE))],
            ins=[cc_in.opt()], outs=[cc_out.opt()]),
            reads=[(C, "ccin"), "ccio"], writes=["ccio", (C, "ccout")], tag="cc")
        vop(lambda v: v.memset(s_loff[:], 0.0),
            reads=[], writes=[(C, "loff0")], tag="loffz")
        vop(lambda v: v.tensor_copy(s_loff[0:1, 8:16], s_lt[0:1, 0:8]),
            reads=[(C, "loff0"), (C, "lt")], writes=[(C, "loff")], tag="loffc")
        for t in range(NT):
            vop((lambda t: lambda v: v.tensor_add(
                v_rep[0:1, 16 * t:16 * t + 16],
                v_toffrow[0:1, 16 * t:16 * t + 16], s_loff[0:1, :]))(t),
                reads=[(C, "toffrow"), (C, "loff"), "rows"],
                writes=["rows", (C, "lrow", t)], tag="lrow")
        mm(pA[:, 0:512], c_onesrow_f[:], v_rep, False, True,
           reads=["c_onesrow_f", "pA.0"] + [(C, "lrow", t) for t in range(NT)],
           writes=["pA.0"], tag="lk1")
        vop(lambda v: v.tensor_mul(s_scrA[:, 0:512], pA[:, 0:512], s_masksf[:]),
            reads=["pA.0", "scrA"] + allmask, writes=["scrA", (C, "prodl")],
            tag="prodl")
        vop(lambda v: v.tensor_reduce(
            s_posl[:], s_scrA[:, 0:512].rearrange("p (g e) -> p g e", e=8),
            AX.X, ALU.add),
            reads=[(C, "prodl"), "scrA"], writes=[(C, "posl")], tag="redl")
        vop(lambda v: v.tensor_scalar(
            s_keep[:], s_posl[:], float(CAP), None, ALU.is_lt),
            reads=[(C, "posl")], writes=[(C, "kl")], tag="kl")
        m1v = s_m12[:].rearrange("p (t j) -> p t j", j=2)[:, :, 0]
        m2v = s_m12[:].rearrange("p (t j) -> p t j", j=2)[:, :, 1]
        vop(lambda v: v.tensor_sub(s_ta[:, 0:NT], m2v, m1v),
            reads=allm12, writes=[(C, "d21")], tag="d21")
        sop(lambda s: s.activation(s_ta[:, 0:NT], s_ta[:, 0:NT], AF.Exp),
            reads=[(C, "d21")], writes=[(C, "e21")], tag="e21")
        vop(lambda v: v.reciprocal(s_tb[:, 0:NT], s_S8[:]),
            reads=allS8, writes=[(C, "g1")], tag="g1")
        vop(lambda v: v.tensor_mul(s_ta[:, NT:2 * NT], s_ta[:, 0:NT],
                                   s_tb[:, 0:NT]),
            reads=[(C, "e21"), (C, "g1")], writes=[(C, "g2")], tag="g2")
        vop(lambda v: v.tensor_add(s_ta[:, 0:NT], s_ta[:, NT:2 * NT],
                                   s_tb[:, 0:NT]),
            reads=[(C, "g1"), (C, "g2")], writes=[(C, "gsum")], tag="gsum")
        vop(lambda v: v.tensor_scalar(
            s_ta[:, 0:NT], s_ta[:, 0:NT], 1e-9, None, ALU.add),
            reads=[(C, "gsum")], writes=[(C, "den")], tag="den")
        vop(lambda v: v.reciprocal(s_ta[:, 0:NT], s_ta[:, 0:NT]),
            reads=[(C, "den")], writes=[(C, "rden")], tag="rden")
        w1v = s_wk_[:].rearrange("p (t j) -> p t j", j=2)[:, :, 0]
        w2v = s_wk_[:].rearrange("p (t j) -> p t j", j=2)[:, :, 1]
        vop(lambda v: v.tensor_mul(w1v, s_tb[:, 0:NT], s_ta[:, 0:NT]),
            reads=[(C, "g1"), (C, "rden")], writes=[(C, "w1")], tag="w1")
        vop(lambda v: v.tensor_mul(w2v, s_ta[:, NT:2 * NT], s_ta[:, 0:NT]),
            reads=[(C, "g2"), (C, "rden")], writes=[(C, "w2")], tag="w2")
        vop(lambda v: v.scalar_tensor_tensor(
            s_ta[:], s_idxf[:], float(CAP), s_posl[:], ALU.mult, ALU.add),
            reads=allidxf + [(C, "posl")], writes=[(C, "lidx")], tag="lidx")
        vop(lambda v: v.tensor_scalar(
            s_tb[:], s_keep[:], -BIGIDX, BIGIDX, ALU.mult, ALU.add),
            reads=[(C, "kl")], writes=[(C, "sentm")], tag="sentm")
        vop(lambda v: v.tensor_add(s_tb[:], s_tb[:], s_ta[:]),
            reads=[(C, "sentm"), (C, "lidx")], writes=[(C, "lidxF")],
            tag="lidxF")
        vop(lambda v: v.tensor_copy(s_idxi[:], s_tb[:]),
            reads=[(C, "lidxF")], writes=[(C, "idxi")], tag="idxi")

        for t in range(NT):
            for j in range(2):
                col = 2 * t + j
                pr.op("gpsimd", (lambda t, col: lambda g: g.indirect_dma_start(
                    xe_dram, IOOA(s_idxi[:, col:col + 1], 0),
                    s_h[:, t * D:(t + 1) * D], None,
                    bounds_check=bcreg["r"], oob_is_err=False))(t, col),
                    reads=[(C, "idxi"), ("hb", t)],
                    writes=[(C, "scat", t, j)], dma=True, tag="scat")
        pr.barrier(tag=f"{C}_scatdone")

        def w1load(e, sh, w, c):
            wb = w % 2
            dma("gpsimd", s_w1w[wb][:, c * 512:(c + 1) * 512],
                fw1_d[e, c * P:(c + 1) * P, w * 512:(w + 1) * 512],
                reads=[("w1b", wb, c)],
                writes=[("w1b", wb, c), (C, "w1", e, sh, w, c)], tag="w1l")

        for e in range(NE):
            def w2load(f):
                dma("gpsimd", s_w2[:, f * 1024:(f + 1) * 1024],
                    fw2_d[e, f * P:(f + 1) * P, :],
                    reads=[("w2B", f)], writes=[("w2B", f)], tag="w2l")
            dma("sync", v_fb2row, fb2_d[e:e + 1, :],
                reads=["scrA"], writes=["scrA", (C, "b2r", e)], tag="b2l")
            for hf in range(2):
                mm(pC[:, 0:512], c_onesrow_f[:],
                   v_fb2row[0:1, hf * 512:(hf + 1) * 512], True, True,
                   reads=["c_onesrow_f", (C, "b2r", e), "scrA", "pC.0"],
                   writes=["pC.0"], tag="b2k1")
                vop((lambda hf: lambda v: v.tensor_copy(
                    s_scrB[:, hf * 512:(hf + 1) * 512], pC[:, 0:512]))(hf),
                    reads=["pC.0", "scrB"], writes=["scrB", (C, "b2bc", e, hf)],
                    tag="b2e")
            for sh in range(2):
                for w in range(2):
                    for c in range(DC):
                        w1load(e, sh, w, c)
                    if sh == 0:
                        for f in range(4 * w, 4 * w + 4):
                            w2load(f)
                for g2 in range(4):
                    bp = (g2 % 2) * 2
                    rows0 = e * CAP + sh * 1024 + g2 * 256
                    dma("scalar",
                        s_xtok[:, bp * 1024:(bp + 2) * 1024]
                        .rearrange("p (tt d) -> p tt d", tt=2),
                        xe_dram[rows0:rows0 + 256, :]
                        .rearrange("(tt p) d -> p tt d", p=P),
                        reads=[("xtokR", bp), ("xtokR", bp + 1)],
                        writes=[("xtokR", bp), ("xtokR", bp + 1),
                                (C, "xtok", e, sh, g2)],
                        tag="xel")
                    for c in range(DC):
                        kgrp = c + g2
                        pdst = pAb if kgrp % 2 == 0 else pBb
                        pk = "pA.0" if kgrp % 2 == 0 else "pB.0"
                        for tt in range(2):
                            tp(pdst[:, tt * P:(tt + 1) * P],
                               s_xtok[:, (bp + tt) * D + c * P:(bp + tt) * D + (c + 1) * P],
                               c_identb[:],
                               reads=[(C, "xtok", e, sh, g2), ("xtokR", bp + tt),
                                      "c_identb", pk],
                               writes=[pk], tag="xtp")
                        vop((lambda c, g2, pdst: lambda v: v.tensor_copy(
                            s_xet[:, c * 1024 + g2 * 256:c * 1024 + g2 * 256 + 256],
                            pdst[:, 0:256]))(c, g2, pdst),
                            reads=[pk, ("xetb", c)],
                            writes=[("xetb", c), (C, "xet", e, sh, c, g2)],
                            tag="xte")
                xetall = [(C, "xet", e, sh, c, g2)
                          for c in range(DC) for g2 in range(4)]
                if DEBUG_TAPS and call == "A" and e == 0 and sh == 0:
                    dma("sync", dbg["xet0"], s_xet[:], reads=list(xetall),
                        writes=[("dbgw", "xet0")], tag="dbgxet")
                for w in range(8):
                    wb = w % 2
                    if w >= 2:
                        for c in range(DC):
                            w1load(e, sh, w, c)
                        if sh == 0:
                            for f in range(4 * w, 4 * w + 4):
                                w2load(f)
                    for fc4 in range(4):
                        f = w * 4 + fc4
                        ps, pk0, pk1 = ((pB, "pB.0", "pB.1") if (f % 2 == 0)
                                        else (pA, "pA.0", "pA.1"))
                        for d in range(DC):
                            lhs = s_w1w[wb][:, d * 512 + fc4 * P:
                                            d * 512 + (fc4 + 1) * P]
                            w1keys = [(C, "w1", e, sh, w, c) for c in range(DC)]
                            mm(ps[:, 0:512], lhs,
                               s_xet[:, d * 1024:d * 1024 + 512],
                               d == 0, d == DC - 1,
                               reads=w1keys + [("w1b", wb, d),
                                      ("xetb", d), pk0] + xetall,
                               writes=[pk0], tag="mm1")
                            mm(ps[:, 512:1024], lhs,
                               s_xet[:, d * 1024 + 512:d * 1024 + 1024],
                               d == 0, d == DC - 1,
                               reads=w1keys + [("w1b", wb, d),
                                      ("xetb", d), pk1] + xetall,
                               writes=[pk1], tag="mm1")
                        sop((lambda ps, e, f: lambda s: s.activation(
                            s_h[:, f * 1024:(f + 1) * 1024], ps[:], AF.Gelu,
                            bias=s_fb1t[:, e * FC + f:e * FC + f + 1],
                            scale=1.0))(ps, e, f),
                            reads=[pk0, pk1, "s_fb1t", ("hb", f)],
                            writes=[("hb", f), (C, "h", e, sh, f)], tag="gelu")
                if DEBUG_TAPS and call == "A" and e == 0 and sh == 0:
                    dma("sync", dbg["h0"], s_h[:],
                        reads=[(C, "h", e, sh, f) for f in range(FC)],
                        writes=[("dbgw", "h0")], tag="dbgh0")
                for st in range(8):
                    py, yk0, yk1 = ((pY, "pY.0", "pY.1") if st % 2 == 0
                                    else (pC, "pC.0", "pC.1"))
                    for f in range(FC):
                        lhs = s_h[:, f * 1024 + st * P:f * 1024 + (st + 1) * P]
                        mm(py[:, 0:512], lhs, s_w2[:, f * 1024:f * 1024 + 512],
                           f == 0, f == FC - 1,
                           reads=[(C, "h", e, sh, f), ("hb", f), ("w2B", f),
                                  yk0],
                           writes=[yk0], tag="mm2")
                        mm(py[:, 512:1024], lhs,
                           s_w2[:, f * 1024 + 512:f * 1024 + 1024],
                           f == 0, f == FC - 1,
                           reads=[(C, "h", e, sh, f), ("hb", f), ("w2B", f),
                                  yk1],
                           writes=[yk1], tag="mm2")
                    yst = s_yest[st % 2]
                    ry = ("yestb", st % 2)
                    vop((lambda yst, py: lambda v: v.scalar_tensor_tensor(
                        yst[:], py[:], 1.0, s_scrB[:], ALU.mult, ALU.add))(yst, py),
                        reads=[yk0, yk1, (C, "b2bc", e, 0), (C, "b2bc", e, 1),
                               "scrB", ry],
                        writes=[ry, (C, "yest", e, sh, st)], tag="yee")
                    rows0 = e * CAP + sh * 1024 + st * P
                    dma("sync", ye_dram[rows0:rows0 + P, :], yst[:],
                        reads=[(C, "yest", e, sh, st), ry], writes=[ry, (C, "yed", e, sh, st)],
                        tag="yew")
        allye = [(C, "yed", e, sh, st)
                 for e in range(NE) for sh in range(2) for st in range(8)]
        dma("gpsimd", s_ag[:], cc_out, reads=[(C, "ccout")],
            writes=[(C, "ag")], tag="ccr")
        mm(pC[0:1, 32:48], c_selrank[:, 0:1], s_ag[:], True, True,
           reads=["c_selrank", (C, "ag"), "pC.0"], writes=["pC.0"], tag="s1m")
        vop(lambda v: v.tensor_copy(s_s1[:], pC[0:1, 32:48]),
            reads=["pC.0"], writes=[(C, "s1")], tag="s1e")
        mm(pC[0:1, 48:64], c_selrank[:, 1:2], s_ag[:], True, True,
           reads=["c_selrank", (C, "ag"), "pC.0"], writes=["pC.0"], tag="s2m")
        vop(lambda v: v.tensor_copy(s_s2[:], pC[0:1, 48:64]),
            reads=["pC.0"], writes=[(C, "s2")], tag="s2e")
        vop(lambda v: v.tensor_copy(s_goff[:], s_s1[:]),
            reads=[(C, "s1")], writes=[(C, "goff0")], tag="goffc")
        vop(lambda v: v.tensor_add(
            s_goff[0:1, 8:16], s_s1[0:1, 8:16], s_s2[0:1, 0:8]),
            reads=[(C, "goff0"), (C, "s1"), (C, "s2")], writes=[(C, "goff")],
            tag="goffa")
        vop(lambda v: v.tensor_sub(s_goff[:], s_goff[:], s_loff[:]),
            reads=[(C, "goff"), (C, "loff")], writes=[(C, "gdelta")],
            tag="gdelta")
        for t in range(NT):
            vop((lambda t: lambda v: v.tensor_copy(
                v_rep[0:1, 16 * t:16 * t + 16], s_goff[0:1, :]))(t),
                reads=[(C, "gdelta"), "rows"],
                writes=["rows", (C, "grow", t)], tag="grow")
        mm(pB[:, 0:512], c_onesrow_f[:], v_rep, True, True,
           reads=["c_onesrow_f", "pB.0"] + [(C, "grow", t) for t in range(NT)],
           writes=["pB.0"], tag="gk1")
        vop(lambda v: v.tensor_mul(s_scrA[:, 0:512], pB[:, 0:512], s_masksf[:]),
            reads=["pB.0", "scrA"] + allmask,
            writes=["scrA", (C, "prodg")], tag="prodg")
        vop(lambda v: v.tensor_reduce(
            s_posg[:], s_scrA[:, 0:512].rearrange("p (g e) -> p g e", e=8),
            AX.X, ALU.add),
            reads=[(C, "prodg"), "scrA"], writes=[(C, "posg0")], tag="redg")
        vop(lambda v: v.tensor_add(s_posg[:], s_posg[:], s_posl[:]),
            reads=[(C, "posg0"), (C, "posl")], writes=[(C, "posg")],
            tag="posg")
        vop(lambda v: v.tensor_scalar(
            s_ta[:], s_posg[:], float(GCAP), None, ALU.is_lt),
            reads=[(C, "posg")], writes=[(C, "kg")], tag="kg")
        vop(lambda v: v.tensor_mul(s_keep[:], s_keep[:], s_ta[:]),
            reads=[(C, "kg"), (C, "kl")], writes=[(C, "keep")], tag="keep")
        vop(lambda v: v.tensor_mul(s_wk_[:], s_wk_[:], s_keep[:]),
            reads=[(C, "w1"), (C, "w2"), (C, "keep")], writes=[(C, "wkk")],
            tag="wkk")
        vop(lambda v: v.tensor_scalar(s_wk_[:], s_wk_[:], 0.5, None, ALU.mult),
            reads=[(C, "wkk")], writes=[(C, "wk")], tag="wkf")
        if DEBUG_TAPS and call == "A":
            for nm, buf, rk in [("idxf", s_idxf, [(C, "idxf", t) for t in range(NT)]),
                                ("posl", s_posl, [(C, "posl")]),
                                ("posg", s_posg, [(C, "posg")]),
                                ("keep", s_keep, [(C, "keep")]),
                                ("wk", s_wk_, [(C, "wk")]),
                                ("S8", s_S8, [(C, "S8", t) for t in range(NT)]),
                                ("m12", s_m12, [(C, "m12", t) for t in range(NT)]),
                                ("idxi", s_idxi, [(C, "idxi")]),
                                ("masks", s_masksf,
                                 [(C, "mask", t, j) for t in range(NT) for j in range(2)])]:
                dma("sync", dbg[nm], buf[:], reads=rk, writes=[("dbgw", nm)],
                    tag="dbg")


        if DEBUG_TAPS and call == "A":
            dma("sync", dbg["xe"], xe_dram, reads=[(C, "scat", t, j) for t in range(NT) for j in range(2)],
                writes=[("dbgw", "xe")], tag="dbgxe")
            dma("sync", dbg["ye"], ye_dram, reads=allye, writes=[("dbgw", "ye")],
                tag="dbgye")

        def rot(t):
            if t % 4 < 2:
                return s_yest[t % 4][:], ("yestb", t % 4)
            return s_xt[t % 4 - 2][:], ("xtb", t % 4 - 2)

        def combine_tile(t):
            pr.op("gpsimd", (lambda t: lambda g: g.indirect_dma_start(
                s_scrB[:], None, ye_dram, IOOA(s_idxi[:, 2 * t:2 * t + 1], 0),
                bounds_check=bcreg["r"], oob_is_err=False))(t),
                reads=[(C, "idxi"), "scrB"] + allye,
                writes=["scrB", (C, "gA", t)], dma=True, tag="gat1")
            pr.op("gpsimd", (lambda t: lambda g: g.indirect_dma_start(
                s_scrA[:], None, ye_dram, IOOA(s_idxi[:, 2 * t + 1:2 * t + 2], 0),
                bounds_check=bcreg["r"], oob_is_err=False))(t),
                reads=[(C, "idxi"), "scrA"] + allye,
                writes=["scrA", (C, "gB", t)], dma=True, tag="gat2")
            xw = s_w1w[(t // 2) % 2][:].bitcast(F32)
            xt = xw[:, (t % 2) * 1024:(t % 2 + 1) * 1024]
            rx4 = [("w1b", (t // 2) % 2, 4 * (t % 2) + i) for i in range(4)]
            dma("sync", xt, src_hbm[t * P:(t + 1) * P, :],
                reads=rx4 + hk(t), writes=rx4, tag="ldx3")
            vop((lambda t, xt: lambda v: v.scalar_tensor_tensor(
                s_scrB[:], s_scrB[:], s_wk_[:, 2 * t:2 * t + 1], xt,
                ALU.mult, ALU.add))(t, xt),
                reads=[(C, "gA", t), (C, "wk"), "scrB"] + rx4,
                writes=["scrB", (C, "acc1", t)] + rx4, tag="acc1")
            yst, ry = rot(t) if fuse else (s_yest[t % 2][:], ("yestb", t % 2))
            vop((lambda t, yst: lambda v: v.scalar_tensor_tensor(
                yst, s_scrA[:], s_wk_[:, 2 * t + 1:2 * t + 2], s_scrB[:],
                ALU.mult, ALU.add))(t, yst),
                reads=[(C, "gB", t), (C, "acc1", t), (C, "wk"), "scrB",
                       "scrA", ry],
                writes=[ry, (C, "ost", t)], tag="acc2")
            dma("sync", dst_hbm[t * P:(t + 1) * P, :], yst,
                reads=[(C, "ost", t), ry],
                writes=[ry, (C, "dst", t)] +
                ([("hbm", dkey, t)] if dkey else []), tag="stx")
            if fuse:
                zbj = s_xtok[:, (t % 2) * 1024:(t % 2 + 1) * 1024]
                rzb = ("xtokR", t % 2)
                sop((lambda yst, zbj, t: lambda s: s.activation(
                    zbj, yst, AF.Identity, bias=0.0, scale=1.0 / D,
                    accum_out=s_mean[:, t:t + 1]))(yst, zbj, t),
                    reads=[ry, (C, "ost", t), rzb],
                    writes=[rzb, ("fm", t)], tag="fmean")
                sop((lambda yst, zbj, t: lambda s: s.activation(
                    zbj, yst, AF.Square, bias=0.0, scale=1.0 / 32.0,
                    accum_out=s_ssq[:, t:t + 1]))(yst, zbj, t),
                    reads=[ry, (C, "ost", t), rzb],
                    writes=[rzb, ("fq", t)], tag="fssq")

        def fuse_rstd(g):
            c0, c1 = 2 * g, 2 * g + 2
            vop((lambda c0, c1: lambda v: v.tensor_mul(
                s_rstd[:, c0:c1], s_mean[:, c0:c1], s_mean[:, c0:c1]))(c0, c1),
                reads=[("fm", c0), ("fm", c0 + 1)], writes=[("fv", g)],
                tag="fm2")
            vop((lambda c0, c1: lambda v: v.tensor_sub(
                s_rstd[:, c0:c1], s_rstd[:, c0:c1], s_ssq[:, c0:c1]))(c0, c1),
                reads=[("fq", c0), ("fq", c0 + 1), ("fv", g)],
                writes=[("fv", g)], tag="fnv")
            vop((lambda c0, c1: lambda v: v.tensor_scalar(
                s_rstd[:, c0:c1], s_rstd[:, c0:c1], -1e-5, None, ALU.add))
                (c0, c1),
                reads=[("fv", g)], writes=[("fv", g)], tag="fnve")
            sop((lambda c0, c1: lambda s: s.activation(
                s_rstd[:, c0:c1], s_rstd[:, c0:c1], AF.Ln, bias=0.0,
                scale=-1.0))(c0, c1),
                reads=[("fv", g)], writes=[("fv", g)], tag="flnv")
            sop((lambda c0, c1: lambda s: s.activation(
                s_rstd[:, c0:c1], s_rstd[:, c0:c1], AF.Exp, bias=0.0,
                scale=-0.5))(c0, c1),
                reads=[("fv", g)], writes=[("frs", g)], tag="frstd")

        def fuse_zprep(t):
            A0 = "a0"
            yst, ry = rot(t)
            zbj = s_xtok[:, (t % 2) * 1024:(t % 2 + 1) * 1024]
            rzb = ("xtokR", t % 2)
            vop((lambda yst, zbj, t: lambda v: v.tensor_scalar(
                zbj, yst, s_mean[:, t:t + 1], s_rstd[:, t:t + 1],
                ALU.subtract, ALU.mult))(yst, zbj, t),
                reads=[ry, (C, "ost", t), ("frs", t // 2), rzb],
                writes=[rzb, (A0, "zb", t)], tag="faznorm")
            pzb = pAb if t % 2 == 0 else pBb
            pzk = "pA.0" if t % 2 == 0 else "pB.0"
            for c in range(DC):
                tp(pzb[:, c * P:(c + 1) * P], zbj[:, c * P:(c + 1) * P],
                   c_identb[:],
                   reads=[(A0, "zb", t), rzb, "c_identb", pzk],
                   writes=[pzk], tag="faztp")
            zt_view = s_h[:, 0:16384].rearrange(
                "p (c q) -> p c q", c=DC)[:, :, t * P:(t + 1) * P]
            pt_view = pzb[:, 0:1024].rearrange("p (c q) -> p c q", c=DC)
            hbks = [("hb", 2 * c + t // 8) for c in range(DC)]
            vop((lambda zv, pv: lambda v: v.tensor_copy(zv, pv))
                (zt_view, pt_view),
                reads=[pzk] + hbks,
                writes=hbks + [(A0, "zt", t)], tag="fazte")

        if fuse:
            for g in range(NT // 2):
                combine_tile(2 * g)
                combine_tile(2 * g + 1)
                fuse_rstd(g)
                if 2 * g < 16:
                    fuse_zprep(2 * g)
                    fuse_zprep(2 * g + 1)
        else:
            for t in range(NT):
                combine_tile(t)

    def attn(src_hbm, dst_hbm, skey=None, dkey=None, fused=False):
        a_zt = s_h[:, 0:16384]
        a_at = a_zt
        a_q = s_h[:, 16384:32768]
        a_ki = s_w2[:, 0:8192]
        a_kc = s_w2[:, 8192:10240]
        a_pt = [s_w2[:, 10240:11264], s_w2[:, 11264:12288]]
        a_ju = s_w2[:, 12288:13312]
        a_wq = s_w2[:, 0:8192]
        a_wk = s_w2[:, 8192:16384]
        a_wv = s_w2[:, 16384:24576]
        a_wo = s_w2[:, 24576:32768]
        a_vi = s_xet[:, 0:8192]
        a_zb = [s_xtok[:, 0:1024], s_xtok[:, 1024:2048]]
        a_cp = s_h[:, 0:2048]
        a_vc = s_xtok[:, 2048:4096]
        a_h1 = s_xtok[:, 0:4096]
        a_w1c = s_w1w[0]
        a_w2c = s_w1w[1]
        w2f = s_w2[:].bitcast(F32)
        hf_ = s_h[:].bitcast(F32)

        def projT_sl(oc, lo, hi):
            if oc < 6:
                return w2f[:, oc * SEQ + lo:oc * SEQ + hi]
            return hf_[:, 8192 + (oc - 6) * SEQ + lo:8192 + (oc - 6) * SEQ + hi]

        def projT_keys(oc):
            if oc < 6:
                return [("w2B", 4 * oc + i) for i in range(4)]
            return [("hb", 16 + 4 * (oc - 6) + i) for i in range(4)]

        wqK = [("w2B", i) for i in range(0, 8)]
        wkK = [("w2B", i) for i in range(8, 16)]
        wvK = [("w2B", i) for i in range(16, 24)]
        woK = [("w2B", i) for i in range(24, 32)]
        kiK = wqK
        kcK = [("w2B", 8), ("w2B", 9)]
        ptK = [("w2B", 10), ("w2B", 11)]

        for c in range(DC):
            dma("gpsimd", a_w1c[:, c * 512:(c + 1) * 512],
                aw1b[c * P:(c + 1) * P, :],
                reads=[("w1b", 0, c), ("wbc", "aw1")],
                writes=[("w1b", 0, c)], tag="aw1l")
        for f1 in range(4):
            dma("scalar", a_w2c[:, f1 * 1024:(f1 + 1) * 1024],
                aw2b[f1 * P:(f1 + 1) * P, :],
                reads=[("w1b", 1, 2 * f1), ("w1b", 1, 2 * f1 + 1),
                       ("wbc", "aw2")],
                writes=[("w1b", 1, 2 * f1), ("w1b", 1, 2 * f1 + 1)], tag="aw2l")
        w1cK = [("w1b", 0, c) for c in range(DC)]
        w2cK = [("w1b", 1, i) for i in range(8)]

        for b in range(NB):
            C = f"a{b}"
            bsrc = src_hbm[b * SEQ:(b + 1) * SEQ, :]
            bdst = dst_hbm[b * SEQ:(b + 1) * SEQ, :]
            NTB = 16
            for wdr, base, wnm in ((wqb, 0, "wq"), (wkb, 8, "wk"),
                                   (wvb, 16, "wv"), (wob, 24, "wo")):
                for c in range(DC):
                    dma(("sync", "scalar", "gpsimd")[c % 3],
                        s_w2[:, (base + c) * 1024:(base + c + 1) * 1024],
                        wdr[c * P:(c + 1) * P, :],
                        reads=[("w2B", base + c), ("wbc", wnm)],
                        writes=[("w2B", base + c)], tag="awl")
            if not fused:
                ln_pass(C, bsrc, NTB, skey, soff=b * 16, deep=False)

            for t in ([] if (fused and b == 0) else range(NTB)):
                g = b * 16 + t
                xt = s_xt[t % 2]
                rx = ("xtb", t % 2)
                zb = a_zb[t % 2]
                rzb = ("xtokR", t % 2)
                dma("sync", xt[:], bsrc[t * P:(t + 1) * P, :],
                    reads=[rx] + ([("hbm", skey, b * 16 + t)] if skey else []),
                    writes=[rx], tag="aldx")
                mcol = g if fused else t
                rstdkey = ("frs", g // 2) if fused else (C, "rstd")
                vop((lambda mcol, xt, zb: lambda v: v.tensor_scalar(
                    zb, xt[:], s_mean[:, mcol:mcol + 1],
                    s_rstd[:, mcol:mcol + 1],
                    ALU.subtract, ALU.mult))(mcol, xt, zb),
                    reads=[rx, rstdkey, rzb],
                    writes=[rzb, (C, "zb", t)], tag="aznorm")
                pzb = pAb if t % 2 == 0 else pBb
                pzk = "pA.0" if t % 2 == 0 else "pB.0"
                for c in range(DC):
                    tp(pzb[:, c * P:(c + 1) * P], zb[:, c * P:(c + 1) * P],
                       c_identb[:],
                       reads=[(C, "zb", t), ("xtokR", t % 2), "c_identb", pzk],
                       writes=[pzk], tag="aztp")
                zt_view = a_zt.rearrange("p (c q) -> p c q", c=DC)[:, :, t * P:(t + 1) * P]
                pt_view = pzb[:, 0:1024].rearrange("p (c q) -> p c q", c=DC)
                hbks = [("hb", 2 * c + t // 8) for c in range(DC)]
                vop((lambda zv, pv: lambda v: v.tensor_copy(zv, pv))
                    (zt_view, pt_view),
                    reads=[pzk] + hbks,
                    writes=hbks + [(C, "zt", t)], tag="azte")
            ztall = [(C, "zt", t) for t in range(NTB)]
            ztinit = [(C, "zt", t) for t in range(8)]

            for qc in range(DC):
                for hf2 in range(2):
                    ps, k0, k1 = ((pA, "pA.0", "pA.1") if hf2 == 0
                                  else (pB, "pB.0", "pB.1"))
                    for d in range(DC):
                        lhs = a_wq[:, d * D + qc * P:d * D + (qc + 1) * P]
                        mm(ps[:, 0:512], lhs,
                           a_zt[:, d * SEQ + hf2 * 1024:d * SEQ + hf2 * 1024 + 512],
                           d == 0, False,
                           reads=[("w2B", d)] + ztall + [k0],
                           writes=[k0], tag="qm")
                        mm(ps[:, 512:1024], lhs,
                           a_zt[:, d * SEQ + hf2 * 1024 + 512:d * SEQ + (hf2 + 1) * 1024],
                           d == 0, d == DC - 1,
                           reads=[("w2B", d)] + ztall + [k1],
                           writes=[k1], tag="qm")
                    sop((lambda qc, hf2, ps: lambda s: s.activation(
                        a_q[:, qc * SEQ + hf2 * 1024:qc * SEQ + (hf2 + 1) * 1024],
                        ps[:], AF.Copy, bias=0.0, scale=0.125))(qc, hf2, ps),
                        reads=[k0, k1, ("hb", 16 + 2 * qc + hf2)],
                        writes=[("hb", 16 + 2 * qc + hf2), (C, "q", qc, hf2)],
                        tag="qe")
            qall = [(C, "q", qc, hf2) for qc in range(DC) for hf2 in range(2)]

            for kc_ in range(DC):
                for d in range(DC):
                    lhs = a_wk[:, d * D + kc_ * P:d * D + (kc_ + 1) * P]
                    mm(pA[:, 0:512], lhs, a_zt[:, d * SEQ:d * SEQ + 512],
                       d == 0, False, reads=[("w2B", 8 + d)] + ztinit + ["pA.0"],
                       writes=["pA.0"], tag="kim")
                    mm(pA[:, 512:1024], lhs,
                       a_zt[:, d * SEQ + 512:d * SEQ + 1024],
                       d == 0, d == DC - 1,
                       reads=[("w2B", 8 + d)] + ztinit + ["pA.1"],
                       writes=["pA.1"], tag="kim")
                sop((lambda kc_: lambda s: s.activation(
                    a_ki[:, kc_ * 1024:(kc_ + 1) * 1024], pA[:], AF.Copy))(kc_),
                    reads=["pA.0", "pA.1", ("w2B", kc_)],
                    writes=[("w2B", kc_), (C, "ki", kc_)], tag="kie")
            kiall = [(C, "ki", kc_) for kc_ in range(DC)]

            for tt in range(8):
                for d in range(DC):
                    lhs = a_zt[:, d * SEQ + tt * P:d * SEQ + (tt + 1) * P]
                    mm(pY[:, 0:512], lhs, a_wv[:, d * D:d * D + 512],
                       d == 0, False,
                       reads=[("w2B", 16 + d), (C, "zt", tt), "pY.0"],
                       writes=["pY.0"], tag="vim")
                    mm(pY[:, 512:1024], lhs, a_wv[:, d * D + 512:d * D + 1024],
                       d == 0, d == DC - 1,
                       reads=[("w2B", 16 + d), (C, "zt", tt), "pY.1"],
                       writes=["pY.1"], tag="vim")
                sop((lambda tt: lambda s: s.activation(
                    a_vi[:, tt * 1024:(tt + 1) * 1024], pY[:], AF.Copy))(tt),
                    reads=["pY.0", "pY.1", ("xetb", tt)],
                    writes=[("xetb", tt), (C, "vi", tt)], tag="vie")
            for f1 in range(4):
                for d in range(DC):
                    lhs = a_w1c[:, d * 512 + f1 * P:d * 512 + (f1 + 1) * P]
                    mm(pA[:, 0:512], lhs, a_zt[:, d * SEQ:d * SEQ + 512],
                       d == 0, False, reads=w1cK + ztinit + ["pA.0"],
                       writes=["pA.0"], tag="cmp1")
                    mm(pA[:, 512:1024], lhs,
                       a_zt[:, d * SEQ + 512:d * SEQ + 1024],
                       d == 0, d == DC - 1, reads=w1cK + ztinit + ["pA.1"],
                       writes=["pA.1"], tag="cmp1")
                sop((lambda f1: lambda s: s.activation(
                    a_h1[:, f1 * 1024:(f1 + 1) * 1024], pA[:], AF.Relu,
                    bias=s_ab1t[:, f1:f1 + 1], scale=1.0))(f1),
                    reads=["pA.0", "pA.1", "s_ab1t"] + [("xtokR", i) for i in range(4)],
                    writes=[("xtokR", i) for i in range(4)] + [(C, "h1", f1)],
                    tag="crelu")
            h1all = [(C, "h1", f1) for f1 in range(4)]
            for dc in range(DC):
                for f1 in range(4):
                    lhs = a_w2c[:, f1 * 1024 + dc * P:f1 * 1024 + (dc + 1) * P]
                    mm(pB[:, 0:512], lhs, a_h1[:, f1 * 1024:f1 * 1024 + 512],
                       f1 == 0, False,
                       reads=w2cK + h1all + [("xtokR", i) for i in range(4)] +
                             ["pB.0"],
                       writes=["pB.0"], tag="cmp2")
                    mm(pB[:, 512:1024], lhs,
                       a_h1[:, f1 * 1024 + 512:f1 * 1024 + 1024],
                       f1 == 0, f1 == 3,
                       reads=w2cK + h1all + [("xtokR", i) for i in range(4)] +
                             ["pB.1"],
                       writes=["pB.1"], tag="cmp2")
                sop((lambda dc: lambda s: s.activation(
                    s_scrB[:], pB[:], AF.Identity, bias=s_ab2t[:, dc:dc + 1],
                    scale=1.0))(dc),
                    reads=["pB.0", "pB.1", "s_ab2t", "scrB"],
                    writes=["scrB", (C, "cst", dc)], tag="cbias")
                cv4 = s_scrB[:].rearrange("p (g four) -> p g four", four=4)
                vop((lambda cv4: lambda v: v.tensor_add(
                    s_scrA[:, 0:256], cv4[:, :, 0], cv4[:, :, 1]))(cv4),
                    reads=[(C, "cst", dc), "scrA"],
                    writes=["scrA", (C, "cp1", dc)], tag="cp1")
                vop((lambda cv4: lambda v: v.tensor_add(
                    s_scrA[:, 256:512], cv4[:, :, 2], cv4[:, :, 3]))(cv4),
                    reads=[(C, "cst", dc), "scrA", (C, "cp1", dc)],
                    writes=["scrA", (C, "cp2", dc)], tag="cp2")
                vop((lambda: lambda v: v.tensor_add(
                    s_scrA[:, 0:256], s_scrA[:, 0:256], s_scrA[:, 256:512]))(),
                    reads=[(C, "cp1", dc), (C, "cp2", dc), "scrA"],
                    writes=["scrA", (C, "cp3", dc)], tag="cp3")
                sop((lambda dc: lambda s: s.activation(
                    a_cp[:, dc * 256:(dc + 1) * 256], s_scrA[:, 0:256],
                    AF.Copy, bias=0.0, scale=0.25))(dc),
                    reads=[(C, "cp3", dc), ("hb", dc // 4)],
                    writes=[("hb", dc // 4), (C, "cp", dc)], tag="cpool")
            cpall = [(C, "cp", dc) for dc in range(DC)]
            if DEBUG_TAPS and b == 0:
                dma("sync", dbg["cp"], a_cp, reads=cpall, writes=[("dbgw", "cp")],
                    tag="dbgcp")
                dma("sync", dbg["h1c"], a_h1, reads=h1all,
                    writes=[("dbgw", "h1c")], tag="dbgh1")

            for kc_ in range(DC):
                ps, psk = (pA, "pA") if kc_ < 4 else (pY, "pY")
                off = (kc_ % 4) * 256
                pk = psk + (".0" if off < 512 else ".1")
                for d in range(DC):
                    lhs = a_wk[:, d * D + kc_ * P:d * D + (kc_ + 1) * P]
                    mm(ps[:, off:off + 256], lhs, a_cp[:, d * 256:(d + 1) * 256],
                       d == 0, d == DC - 1,
                       reads=[("w2B", 8 + d), ("hb", 0), ("hb", 1)] +
                             cpall + [pk],
                       writes=[pk], tag="kcm")
            for kc_ in range(DC):
                ps, psk = (pA, "pA") if kc_ < 4 else (pY, "pY")
                off = (kc_ % 4) * 256
                pk = psk + (".0" if off < 512 else ".1")
                sop((lambda kc_, ps, off: lambda s: s.activation(
                    a_kc[:, kc_ * 256:(kc_ + 1) * 256], ps[:, off:off + 256],
                    AF.Copy))(kc_, ps, off),
                    reads=[pk, ("w2B", 8 + kc_ // 4)],
                    writes=[("w2B", 8 + kc_ // 4), (C, "kc", kc_)], tag="kce")
            for tt in range(2):
                for d in range(DC):
                    lhs = a_cp[:, d * 256 + tt * P:d * 256 + (tt + 1) * P]
                    mm(pY[:, 0:512], lhs, a_wv[:, d * D:d * D + 512],
                       d == 0, False,
                       reads=[("w2B", 16 + d), ("hb", 0), ("hb", 1)] +
                             cpall + ["pY.0"],
                       writes=["pY.0"], tag="vcm")
                    mm(pY[:, 512:1024], lhs, a_wv[:, d * D + 512:d * D + 1024],
                       d == 0, d == DC - 1,
                       reads=[("w2B", 16 + d), ("hb", 0), ("hb", 1)] +
                             cpall + ["pY.1"],
                       writes=["pY.1"], tag="vcm")
                sop((lambda tt: lambda s: s.activation(
                    a_vc[:, tt * 1024:(tt + 1) * 1024], pY[:], AF.Copy))(tt),
                    reads=["pY.0", "pY.1", ("xtokR", 2 + tt)],
                    writes=[("xtokR", 2 + tt), (C, "vc", tt)], tag="vce")
            viall = [(C, "vi", tt) for tt in range(8)]
            if DEBUG_TAPS and b == 0:
                dma("sync", dbg["vc"], a_vc,
                    reads=[(C, "vc", 0), (C, "vc", 1)],
                    writes=[("dbgw", "vc")], tag="dbgvc")
            vcall = [(C, "vc", tt) for tt in range(2)]

            if DEBUG_TAPS and b == 0:
                dma("sync", dbg["kc"], a_kc,
                    reads=[(C, "kc", i) for i in range(DC)],
                    writes=[("dbgw", "kc")], tag="dbgkc")
            kcall = [(C, "kc", kc_) for kc_ in range(DC)]
            def _bk(k, lo, hi):
                ks = []
                if lo < 512:
                    ks.append(k + ".0")
                if hi > 512:
                    ks.append(k + ".1")
                return ks

            for qi in range(16):
                causal = qi < 8
                kts = (qi + 1) if causal else 2
                kW = kts * P
                qoff = qi * P if causal else 1024 + (qi - 8) * P
                pstat, pstk = (pA, "pA") if (qi % 2 == 0) else (pY, "pY")
                pother, potk = (pY, "pY") if (qi % 2 == 0) else (pA, "pA")
                for h in range(NH):
                    psh, pshk = (pstat, pstk) if h % 2 == 0 else (pother, potk)
                    qsl = a_q[(h % 2) * 64:(h % 2) * 64 + 64,
                              (h // 2) * SEQ + qoff:(h // 2) * SEQ + qoff + P]
                    nblk = (kW + 511) // 512
                    for blk in range(nblk):
                        w0 = blk * 512
                        w1 = min(kW, w0 + 512)
                        if causal:
                            rhs = a_ki[(h % 2) * 64:(h % 2) * 64 + 64,
                                       (h // 2) * 1024 + w0:(h // 2) * 1024 + w1]
                            kkeys = kiall
                        else:
                            rhs = a_kc[(h % 2) * 64:(h % 2) * 64 + 64,
                                       (h // 2) * 256 + w0:(h // 2) * 256 + w1]
                            kkeys = kcall
                        mm(psh[:, w0:w1], qsl, rhs, True, True,
                           reads=qall + kkeys + _bk(pshk, w0, w1),
                           writes=_bk(pshk, w0, w1), tag="qk1")
                    if causal:
                        vop((lambda psh, qi: lambda v: v.tensor_add(
                            psh[:, qi * P:(qi + 1) * P],
                            psh[:, qi * P:(qi + 1) * P], c_diagm[:]))
                            (psh, qi),
                            reads=_bk(pshk, qi * P, (qi + 1) * P) + ["c_diagm"],
                            writes=_bk(pshk, qi * P, (qi + 1) * P), tag="dmask")
                    sop((lambda psh, kW, h: lambda s: s.activation(
                        a_ju[:, 0:kW], psh[:, 0:kW], AF.Exp, bias=0.0,
                        scale=1.0, accum_out=s_aS[:, h:h + 1]))(psh, kW, h),
                        reads=_bk(pshk, 0, kW) + [("w2B", 12)],
                        writes=_bk(pshk, 0, kW) + [("w2B", 12), (C, "S", qi, h)],
                        tag="sexp")
                vop(lambda v: v.reciprocal(s_scrA[:, 0:NH], s_aS[:]),
                    reads=[(C, "S", qi, h) for h in range(NH)] + ["scrA"],
                    writes=["scrA", (C, "recS", qi)], tag="recS")
                vop(lambda v: v.tensor_copy(s_aR[:], s_scrA[:, 0:NH]),
                    reads=[(C, "recS", qi), "scrA", "s_aR"],
                    writes=["s_aR", (C, "aRb", qi)], tag="aRb")
                tp(pCb[0:16, 0:P], s_aR[:], c_identb[:],
                   reads=[(C, "aRb", qi), "c_identb", "pC.0"], writes=["pC.0"],
                   tag="rtp")
                vop(lambda v: v.tensor_copy(s_rb16[:], pCb[0:16, 0:P]),
                    reads=["pC.0", "s_rb16"], writes=["s_rb16", (C, "rb16", qi)],
                    tag="rbe")
                dma("gpsimd", v_rrow, s_rb16[:],
                    reads=[(C, "rb16", qi), "rows"],
                    writes=["rows", (C, "rrow", qi)], tag="rrow")
                def stage1(h):
                    pq, pqk = (pB, "pB") if h % 2 == 0 else (pother, potk)
                    qsl = a_q[(h % 2) * 64:(h % 2) * 64 + 64,
                              (h // 2) * SEQ + qoff:(h // 2) * SEQ + qoff + P]
                    ptb = a_pt[h % 2]
                    ptbK = ("w2B", 10 + h % 2)
                    for kt in range(kts):
                        if causal:
                            ksl = a_ki[(h % 2) * 64:(h % 2) * 64 + 64,
                                       (h // 2) * 1024 + kt * P:(h // 2) * 1024 + (kt + 1) * P]
                            kkeys = kiall
                        else:
                            ksl = a_kc[(h % 2) * 64:(h % 2) * 64 + 64,
                                       (h // 2) * 256 + kt * P:(h // 2) * 256 + (kt + 1) * P]
                            kkeys = kcall
                        bk2 = pqk + (".0" if kt < 4 else ".1")
                        mm(pq[:, kt * P:(kt + 1) * P], ksl, qsl, True, True,
                           reads=qall + kkeys + [bk2], writes=[bk2], tag="qk2")
                    if causal:
                        bk2 = pqk + (".0" if qi < 4 else ".1")
                        vop((lambda pq, qi: lambda v: v.tensor_add(
                            pq[:, qi * P:(qi + 1) * P],
                            pq[:, qi * P:(qi + 1) * P], c_diagmT[:]))(pq, qi),
                            reads=[bk2, "c_diagmT"], writes=[bk2], tag="dmaskT")
                    bkst = _bk(pqk, 0, kW)
                    sop((lambda pq, ptb, kW: lambda s: s.activation(
                        ptb[:, 0:kW], pq[:, 0:kW], AF.Exp))(pq, ptb, kW),
                        reads=bkst + [ptbK],
                        writes=bkst + [ptbK, (C, "apt", qi, h)], tag="ptexp")

                def stage2(h):
                    pvsl = pC[0:64, 512:640] if h % 2 == 0 else pC[0:64, 0:128]
                    pvk = "pC.1" if h % 2 == 0 else "pC.0"
                    bbsl = pstat[0:64, 0:P] if h % 2 == 0 \
                        else pstat[0:64, 512:512 + P]
                    bbk = pstk + (".0" if h % 2 == 0 else ".1")
                    ptb = a_pt[h % 2]
                    for kt in range(kts):
                        if causal:
                            vsl = a_vi[:, kt * 1024 + h * 64:kt * 1024 + h * 64 + 64]
                            vkeys = viall
                        else:
                            vsl = a_vc[:, kt * 1024 + h * 64:kt * 1024 + h * 64 + 64]
                            vkeys = vcall
                        mm(pvsl, vsl, ptb[:, kt * P:(kt + 1) * P],
                           kt == 0, kt == kts - 1,
                           reads=vkeys + [(C, "apt", qi, h), pvk],
                           writes=[pvk], tag="pv")
                    mm(bbsl, c_onesrow_b[0:1, 0:64],
                       v_rrow[0:1, h * P:(h + 1) * P], True, True,
                       reads=["c_onesrow_b", (C, "rrow", qi), bbk],
                       writes=[bbk], tag="bk1")
                    vop((lambda bbsl: lambda v: v.tensor_copy(
                        s_bb[:], bbsl))(bbsl),
                        reads=[bbk, "s_bb"],
                        writes=["s_bb", (C, "bb", qi, h)], tag="bbe")
                    atblk = ("hb", ((h // 2) * 2048 + qoff) // 1024)
                    osl = a_at[(h % 2) * 64:(h % 2) * 64 + 64,
                               (h // 2) * SEQ + qoff:(h // 2) * SEQ + qoff + P]
                    vop((lambda osl, pvsl: lambda v: v.tensor_mul(
                        osl, pvsl, s_bb[:]))(osl, pvsl),
                        reads=[(C, "bb", qi, h), pvk, atblk],
                        writes=[atblk, pvk, (C, "at", qi, h)], tag="ate")

                stage1(0)
                for h in range(1, NH):
                    stage1(h)
                    stage2(h - 1)
                stage2(NH - 1)
            atall = [(C, "at", qi, h) for qi in range(16) for h in range(NH)]

            for oc in range(DC):
                for hf2 in range(2):
                    for dv in range(DC):
                        lhs = a_wo[:, dv * D + oc * P:dv * D + (oc + 1) * P]
                        atbk = ("hb", 2 * dv + hf2)
                        mm(pA[:, 0:512], lhs,
                           a_at[:, dv * SEQ + hf2 * 1024:dv * SEQ + hf2 * 1024 + 512],
                           dv == 0, False,
                           reads=[("w2B", 24 + dv), atbk] + atall + ["pA.0"],
                           writes=["pA.0"], tag="proj")
                        mm(pA[:, 512:1024], lhs,
                           a_at[:, dv * SEQ + hf2 * 1024 + 512:dv * SEQ + (hf2 + 1) * 1024],
                           dv == 0, dv == DC - 1,
                           reads=[("w2B", 24 + dv), atbk] + atall + ["pA.1"],
                           writes=["pA.1"], tag="proj")
                    prj = projT_sl(oc, hf2 * 1024, (hf2 + 1) * 1024)
                    vop((lambda prj: lambda v: v.tensor_copy(prj, pA[:]))(prj),
                        reads=["pA.0", "pA.1"] + projT_keys(oc),
                        writes=projT_keys(oc) + [(C, "prj", oc, hf2)],
                        tag="prje")
            prjall = [(C, "prj", oc, hf2) for oc in range(DC) for hf2 in range(2)]

            for t in range(NTB):
                po, pok = (pB, "pB") if t % 2 == 0 else (pY, "pY")
                pokk = [pok + ".0", pok + ".1"]
                for oc in range(DC):
                    rkey = (("w2B", 4 * oc + t // 4) if oc < 6
                            else ("hb", 16 + 4 * (oc - 6) + t // 4))
                    tp(po[:, oc * P:(oc + 1) * P],
                       projT_sl(oc, t * P, (t + 1) * P), c_identf[:],
                       reads=prjall + ["c_identf", rkey] + pokk, writes=pokk,
                       tag="otp")
                xr = w2f[:, 12288 + (t % 4) * 1024:12288 + (t % 4 + 1) * 1024]
                rx4 = [("w2B", 24 + 2 * (t % 4)), ("w2B", 25 + 2 * (t % 4))]
                dma("sync", xr, bsrc[t * P:(t + 1) * P, :],
                    reads=rx4 + ([("hbm", skey, b * 16 + t)] if skey else []),
                    writes=rx4, tag="aldx2")
                yst = s_yest[t % 2]
                ry = ("yestb", t % 2)
                vop((lambda xr, yst, po: lambda v: v.tensor_add(
                    yst[:], po[:], xr))(xr, yst, po),
                    reads=[ry] + pokk + rx4,
                    writes=[ry, (C, "ost", t)] + pokk + rx4,
                    tag="ores")
                dma("sync", bdst[t * P:(t + 1) * P, :], yst[:],
                    reads=[(C, "ost", t), ry],
                    writes=[ry, (C, "dst", t)] +
                    ([("hbm", dkey, b * 16 + t)] if dkey else []), tag="ostx")

    prelude()
    pr.barrier(tag="prelude_done")
    if "attn" in phases:
        cache_weights()
    cur, ck = x_in, None
    n = len(phases)
    for i, ph in enumerate(phases):
        if i == n - 1:
            dst, dk = out_d, None
        else:
            dst, dk = (xA, "xA") if i == 0 else (xB, "xB")
        if ph == "ffnA":
            ffn("A", cur, dst, ck, dk, fuse=("attn" in phases))
        elif ph == "attn":
            attn(cur, dst, ck, dk, fused=("ffnA" in phases))
        elif ph == "ffnC":
            ffn("C", cur, dst, ck, dk)
        cur, ck = dst, dk

    pr.finalize()
    pr.emit()
    return nc

_NC_CACHE = {}


def kernel(**inputs):
    hs = np.ascontiguousarray(inputs["L0_attn_hidden_states"], dtype=np.float32)
    ncore = 8
    key = PHASES
    if key not in _NC_CACHE:
        _NC_CACHE[key] = build_program(PHASES)
    nc = _NC_CACHE[key]
    common = dict(
        wq=np.ascontiguousarray(inputs["L0_attn_Wq"], dtype=np.float32),
        wk=np.ascontiguousarray(inputs["L0_attn_Wk"], dtype=np.float32),
        wv=np.ascontiguousarray(inputs["L0_attn_Wv"], dtype=np.float32),
        wo=np.ascontiguousarray(inputs["L0_attn_Wo"], dtype=np.float32),
        aw1=np.ascontiguousarray(inputs["L0_attn_W1"], dtype=np.float32),
        ab1=np.ascontiguousarray(inputs["L0_attn_b1"], dtype=np.float32).reshape(1, 512),
        aw2=np.ascontiguousarray(inputs["L0_attn_W2"], dtype=np.float32),
        ab2=np.ascontiguousarray(inputs["L0_attn_b2"], dtype=np.float32).reshape(1, D),
        wg=np.ascontiguousarray(inputs["L0_ffn_Wg"], dtype=np.float32),
        fw1=np.ascontiguousarray(inputs["L0_ffn_W1"], dtype=np.float32),
        fb1=np.ascontiguousarray(inputs["L0_ffn_b1"], dtype=np.float32),
        fw2=np.ascontiguousarray(inputs["L0_ffn_W2"], dtype=np.float32),
        fb2=np.ascontiguousarray(inputs["L0_ffn_b2"], dtype=np.float32),
    )
    in_maps = []
    for c in range(ncore):
        sel = np.zeros((NE, 2), dtype=np.float32)
        sel[:c, 0] = 1.0
        sel[:, 1] = 1.0
        m = dict(common)
        m["x"] = np.ascontiguousarray(hs[2 * c:2 * c + 2].reshape(NTOK, D))
        m["selrank"] = sel
        in_maps.append(m)
    res = run_bass_kernel_spmd(nc, in_maps, core_ids=list(range(ncore)))
    outs = [res.results[c]["out"].reshape(2, SEQ, D) for c in range(ncore)]
    return np.concatenate(outs, axis=0)


# version 19
# speedup vs baseline: 1.1181x; 1.0111x over previous
import numpy as np
import ml_dtypes
import concourse.bass as bass
import concourse.mybir as mybir
from concourse.bass_utils import run_bass_kernel_spmd

F32 = mybir.dt.float32
BF16 = mybir.dt.bfloat16
I32 = mybir.dt.int32
U32 = mybir.dt.uint32
AF = mybir.ActivationFunctionType
ALU = mybir.AluOpType
AX = mybir.AxisListType
BF = ml_dtypes.bfloat16
IOOA = bass.IndirectOffsetOnAxis

P = 128
D = 1024
DC = 8
NB = 2
SEQ = 2048
NTOK = 4096
NT = 32
NE = 8
CAP = 2048
GCAP = 16384
FF = 4096
FC = 32
NH = 16
BIGIDX = 1.0e7

PHASES = ("ffnA", "attn", "ffnC")
DEBUG_TAPS = False

ENGINES = ("tensor", "vector", "scalar", "gpsimd", "sync")
ESEM = {"tensor": "pe", "vector": "dve", "scalar": "act", "gpsimd": "pool", "sync": "sp"}
QSEM = {"sync": "qsp", "scalar": "qact", "gpsimd": "qpool"}


class Prog:

    QROT = {"qsp": 24, "qact": 6, "qpool": 24}

    def __init__(self, nc):
        self.nc = nc
        self.ops = []
        self.eng_ops = {e: [] for e in ENGINES}
        self.last_writer = {}
        self.readers = {}
        names = set(ESEM.values())
        for q, k in self.QROT.items():
            for i in range(k):
                names.add(f"{q}{i}")
        self.sems = {s: nc.alloc_semaphore(s) for s in names}

    def op(self, eng, fn, reads=(), writes=(), dma=False, tag=""):
        o = dict(eng=eng, fn=fn, dma=dma, tag=tag, deps=[], need_inc=False,
                 seq=len(self.ops))
        for r in reads:
            w = self.last_writer.get(r)
            if w is not None:
                o["deps"].append(w)
        for w_ in writes:
            lw = self.last_writer.get(w_)
            if lw is not None:
                o["deps"].append(lw)
            for rd in self.readers.get(w_, ()):
                o["deps"].append(rd)
        for w_ in writes:
            self.last_writer[w_] = o
            self.readers[w_] = []
        for r in reads:
            self.readers.setdefault(r, []).append(o)
        self.ops.append(o)
        self.eng_ops[eng].append(o)
        return o

    def barrier(self, tag="bar"):
        tails = [ops[-1] for ops in self.eng_ops.values() if ops]
        for eng in ENGINES:
            o = self.op(eng, None, tag=tag)
            o["deps"] = list(tails)

    def finalize(self):
        for o in self.ops:
            dd, seen = [], set()
            for d in o["deps"]:
                if d is o:
                    continue
                if (d["eng"] == "tensor" and o["eng"] == "tensor"
                        and not d["dma"] and not o["dma"]):
                    continue
                if id(d) in seen:
                    continue
                seen.add(id(d))
                dd.append(d)
                d["need_inc"] = True
            o["deps"] = dd
        cnt = {s: 0 for s in self.sems}
        qidx = {q: 0 for q in self.QROT}
        for o in self.ops:
            if o["fn"] is None:
                o["sem"] = None
                o["incval"] = None
                continue
            if o["dma"]:
                q = QSEM[o["eng"]]
                i = qidx[q]
                qidx[q] += 1
                s = f"{q}{i % self.QROT[q]}"
                cnt[s] += 16
                o["sem"], o["wait_target"], o["incval"] = s, cnt[s], 16
            else:
                s = ESEM[o["eng"]]
                if o["need_inc"]:
                    cnt[s] += 1
                    o["sem"], o["wait_target"], o["incval"] = s, cnt[s], 1
                else:
                    o["sem"], o["incval"] = None, None
        self.final_counts = dict(cnt)
        for o in self.ops:
            w = {}
            stack = list(o["deps"])
            depth = 0
            while stack and depth < 100000:
                depth += 1
                d = stack.pop()
                if d.get("sem") is None:
                    stack.extend(d["deps"])
                    continue
                prev = w.get(d["sem"], 0)
                if d["wait_target"] > prev:
                    w[d["sem"]] = d["wait_target"]
            o["waits"] = w
        for eng in ENGINES:
            have = {}
            for o in self.eng_ops[eng]:
                pruned = {}
                for s, v in o.get("waits", {}).items():
                    if have.get(s, 0) < v:
                        pruned[s] = v
                        have[s] = v
                o["waits"] = pruned
        return self

    def emit(self):
        prog = self
        fc = prog.final_counts

        def mk(eng_name):
            def body(eng):
                for o in prog.eng_ops[eng_name]:
                    for s, v in sorted(o.get("waits", {}).items()):
                        eng.wait_ge(prog.sems[s], v)
                    if o["fn"] is None:
                        continue
                    inst = o["fn"](eng)
                    if o["incval"]:
                        inst.then_inc(prog.sems[o["sem"]], o["incval"])
                if eng_name == "sync":
                    for s, v in sorted(fc.items()):
                        if v and (s[0] == "q"):
                            eng.wait_ge(prog.sems[s], v)
            return body

        with self.nc.Block() as blk:
            blk.tensor(mk("tensor"))
            blk.vector(mk("vector"))
            blk.scalar(mk("scalar"))
            blk.gpsimd(mk("gpsimd"))
            blk.sync(mk("sync"))


def build_program(phases=PHASES, debug=None):
    global DEBUG_TAPS
    if debug is not None:
        DEBUG_TAPS = debug
    nc = bass.Bass("TRN2", target_bir_lowering=False)

    x_in = nc.dram_tensor("x", [NTOK, D], F32, kind="ExternalInput").ap()
    wq_d = nc.dram_tensor("wq", [D, D], F32, kind="ExternalInput").ap()
    wk_d = nc.dram_tensor("wk", [D, D], F32, kind="ExternalInput").ap()
    wv_d = nc.dram_tensor("wv", [D, D], F32, kind="ExternalInput").ap()
    wo_d = nc.dram_tensor("wo", [D, D], F32, kind="ExternalInput").ap()
    aw1_d = nc.dram_tensor("aw1", [D, 512], F32, kind="ExternalInput").ap()
    ab1_d = nc.dram_tensor("ab1", [1, 512], F32, kind="ExternalInput").ap()
    aw2_d = nc.dram_tensor("aw2", [512, D], F32, kind="ExternalInput").ap()
    ab2_d = nc.dram_tensor("ab2", [1, D], F32, kind="ExternalInput").ap()
    wg_d = nc.dram_tensor("wg", [D, NE], F32, kind="ExternalInput").ap()
    fw1_d = nc.dram_tensor("fw1", [NE, D, FF], F32, kind="ExternalInput").ap()
    fb1_d = nc.dram_tensor("fb1", [NE, FF], F32, kind="ExternalInput").ap()
    fw2_d = nc.dram_tensor("fw2", [NE, FF, D], F32, kind="ExternalInput").ap()
    fb2_d = nc.dram_tensor("fb2", [NE, D], F32, kind="ExternalInput").ap()
    selrank_d = nc.dram_tensor("selrank", [NE, 2], F32, kind="ExternalInput").ap()
    out_d = nc.dram_tensor("out", [NTOK, D], F32, kind="ExternalOutput").ap()
    dbg = {}
    if DEBUG_TAPS:
        for nm, sh, dt in [("idxf", [P, 2 * NT], F32), ("posl", [P, 2 * NT], F32),
                           ("posg", [P, 2 * NT], F32), ("keep", [P, 2 * NT], F32),
                           ("wk", [P, 2 * NT], F32), ("S8", [P, NT], F32),
                           ("m12", [P, 2 * NT], F32), ("idxi", [P, 2 * NT], I32),
                           ("masks", [P, 16 * NT], F32)]:
            dbg[nm] = nc.dram_tensor("dbg_" + nm, sh, dt, kind="ExternalOutput").ap()
        dbg["xe"] = nc.dram_tensor("dbg_xe", [NE * CAP, D], BF16, kind="ExternalOutput").ap()
        dbg["xet0"] = nc.dram_tensor("dbg_xet0", [P, DC * 1024], BF16, kind="ExternalOutput").ap()
        dbg["cp"] = nc.dram_tensor("dbg_cp", [P, 2048], BF16, kind="ExternalOutput").ap()
        dbg["kc"] = nc.dram_tensor("dbg_kc", [P, 2048], BF16, kind="ExternalOutput").ap()
        dbg["vc"] = nc.dram_tensor("dbg_vc", [P, 2048], BF16, kind="ExternalOutput").ap()
        dbg["h1c"] = nc.dram_tensor("dbg_h1c", [P, 4096], BF16, kind="ExternalOutput").ap()
        dbg["h0"] = nc.dram_tensor("dbg_h0", [P, FC * 1024], BF16, kind="ExternalOutput").ap()
        dbg["ye"] = nc.dram_tensor("dbg_ye", [NE * CAP, D], F32, kind="ExternalOutput").ap()

    il = nc.inline_tensor
    ident_b = il(np.eye(P, dtype=BF), "c_identb_d").ap()
    ident_f = il(np.eye(P, dtype=np.float32), "c_identf_d").ap()
    ut128f_d = il(np.triu(np.ones((P, P), dtype=np.float32), 1), "c_ut128_d").ap()
    ut32f_d = il(np.triu(np.ones((32, 32), dtype=np.float32), 1), "c_ut32_d").ap()
    iota8_d = il(np.broadcast_to(np.arange(NE, dtype=np.float32), (P, NE)).copy(),
                 "c_iota8_d").ap()
    onesrow_f_d = il(np.ones((1, P), dtype=np.float32), "c_onesrowf_d").ap()
    onesrow_b_d = il(np.ones((1, P), dtype=BF), "c_onesrowb_d").ap()
    onescol_f_d = il(np.ones((P, 1), dtype=np.float32), "c_onescolf_d").ap()
    onescol32_d = il(np.ones((32, 1), dtype=np.float32), "c_onescol32_d").ap()
    qq, kk = np.meshgrid(np.arange(P), np.arange(P), indexing="ij")
    diag_np = np.where(kk <= qq, 0.0, -1e30).astype(np.float32)
    diagm_d = il(diag_np, "c_diagm_d").ap()
    diagmT_d = il(np.ascontiguousarray(diag_np.T), "c_diagmT_d").ap()

    xA = nc.dram_tensor("xA", [NTOK, D], F32, kind="Internal").ap()
    xB = nc.dram_tensor("xB", [NTOK, D], F32, kind="Internal").ap()
    xe_dram = nc.dram_tensor("xe_dram", [NE * CAP, D], BF16, kind="Internal").ap()
    ye_dram = nc.dram_tensor("ye_dram", [NE * CAP, D], F32, kind="Internal").ap()
    cc_in = nc.dram_tensor("cc_in", [1, 16], F32, kind="Internal").ap()
    cc_out = nc.dram_tensor("cc_out", [NE, 16], F32, kind="Internal").ap()
    wqb = nc.dram_tensor("wqb", [D, D], BF16, kind="Internal").ap()
    wkb = nc.dram_tensor("wkb", [D, D], BF16, kind="Internal").ap()
    wvb = nc.dram_tensor("wvb", [D, D], BF16, kind="Internal").ap()
    wob = nc.dram_tensor("wob", [D, D], BF16, kind="Internal").ap()
    aw1b = nc.dram_tensor("aw1b", [D, 512], BF16, kind="Internal").ap()
    aw2b = nc.dram_tensor("aw2b", [512, D], BF16, kind="Internal").ap()
    fw1b = nc.dram_tensor("fw1b", [NE, D, FF], BF16, kind="Internal").ap()
    fw2b = nc.dram_tensor("fw2b", [NE, FF, D], BF16, kind="Internal").ap()

    pr = Prog(nc)
    sb = nc.alloc_sbuf_tensor

    c_identb = sb("c_identb", [P, P], BF16)
    c_identf = sb("c_identf", [P, P], F32)
    c_ut128f = sb("c_ut128f", [P, P], F32)
    c_ut32f = sb("c_ut32f", [32, 32], F32)
    c_iota8 = sb("c_iota8", [P, NE], F32)
    c_onesrow_f = sb("c_onesrow_f", [1, P], F32)
    c_onesrow_b = sb("c_onesrow_b", [1, P], BF16)
    c_onescol_f = sb("c_onescol_f", [P, 1], F32)
    c_onescol32_f = sb("c_onescol32_f", [32, 1], F32)
    c_diagm = sb("c_diagm", [P, P], F32)
    c_diagmT = sb("c_diagmT", [P, P], F32)
    c_selrank = sb("c_selrank", [NE, 2], F32)
    s_wg = sb("s_wg", [P, DC * NE], F32)
    s_fb1t = sb("s_fb1t", [P, NE * FC], F32)
    s_ab1t = sb("s_ab1t", [P, 4], F32)
    s_ab2t = sb("s_ab2t", [P, 8], F32)

    s_h = sb("s_h", [P, 32768], BF16)
    s_w2 = sb("s_w2", [P, 32768], BF16)
    s_xet = sb("s_xet", [P, 8192], BF16)
    s_xtok = sb("s_xtok", [P, 4096], BF16)
    s_w1w = [sb(f"s_w1w{i}", [P, 4096], BF16) for i in range(2)]

    s_scrA = sb("s_scrA", [P, 1024], F32)
    s_scrB = sb("s_scrB", [P, 1024], F32)
    s_xt = [sb(f"s_xt{i}", [P, D], F32) for i in range(2)]
    s_yest = [sb(f"s_yest{i}", [P, D], F32) for i in range(2)]

    s_rows = sb("s_rows", [1, 1024], F32)
    v_rep = s_rows[0:1, 0:512]
    v_toffrow = s_rows[0:1, 512:1024]
    v_rrow = s_rows[:].bitcast(BF16)

    s_mean = sb("s_mean", [P, NT], F32)
    s_ssq = sb("s_ssq", [P, NT], F32)
    s_rstd = sb("s_rstd", [P, NT], F32)
    s_negm1 = sb("s_negm1", [P, NT], F32)
    s_S8 = sb("s_S8", [P, NT], F32)
    s_m12 = sb("s_m12", [P, 2 * NT], F32)
    s_idxf = sb("s_idxf", [P, 2 * NT], F32)
    s_masksf = sb("s_masksf", [P, 16 * NT], F32)
    s_posl = sb("s_posl", [P, 2 * NT], F32)
    s_posg = sb("s_posg", [P, 2 * NT], F32)
    s_keep = sb("s_keep", [P, 2 * NT], F32)
    s_wk_ = sb("s_wk", [P, 2 * NT], F32)
    s_idxi = sb("s_idxi", [P, 2 * NT], I32)
    s_ta = sb("s_ta", [P, 2 * NT], F32)
    s_tb = sb("s_tb", [P, 2 * NT], F32)
    s_l8p = [sb(f"s_l8p{i}", [P, NE], F32) for i in range(2)]
    s_mx8p = [sb(f"s_mx8p{i}", [P, NE], F32) for i in range(2)]
    s_ix8p = [sb(f"s_ix8p{i}", [P, NE], U32) for i in range(2)]
    s_junk8 = sb("s_junk8", [P, NE], F32)
    s_t32 = sb("s_t32", [32, 16], F32)
    s_toff = sb("s_toff", [32, 16], F32)
    s_lt = sb("s_lt", [1, 16], F32)
    s_ag = sb("s_ag", [NE, 16], F32)
    s_s1 = sb("s_s1", [1, 16], F32)
    s_s2 = sb("s_s2", [1, 16], F32)
    s_goff = sb("s_goff", [1, 16], F32)
    s_loff = sb("s_loff", [1, 16], F32)
    s_aS = sb("s_aS", [P, NH], F32)
    s_aR = sb("s_aR", [P, NH], BF16)
    s_rb16 = sb("s_rb16", [16, P], BF16)
    s_bb = sb("s_bb", [64, P], BF16)

    pA = nc.alloc_psum_tensor("pA", [P, 1024], F32)
    pB = nc.alloc_psum_tensor("pB", [P, 1024], F32)
    pY = nc.alloc_psum_tensor("pY", [P, 1024], F32)
    pC = nc.alloc_psum_tensor("pC", [P, 1024], F32)
    pAb = pA[:].bitcast(BF16)
    pBb = pB[:].bitcast(BF16)
    pCb = pC[:].bitcast(BF16)

    v_fb1raw = s_scrA[0:32, :]
    v_abraw = s_scrA[0:8, 0:128]
    v_ab1raw = s_scrA[0:4, 128:256]
    v_fb2row = s_scrA[0:1, 0:1024]

    bcreg = {}

    def init_bcreg(g):
        r = g.alloc_register("bcreg")
        g.reg_mov(r, NE * CAP - 1)
        bcreg["r"] = r
        return g.engine_nop() if hasattr(g, "engine_nop") else g.nop()

    pr.op("gpsimd", init_bcreg, tag="bcreg")

    def dma(eng, dst, src, reads=(), writes=(), tag="dma"):
        pr.op(eng, (lambda d, s: lambda e: e.dma_start(d, s))(dst, src),
              reads=reads, writes=writes, dma=True, tag=tag)

    def mm(dst, lhsT, rhs, start, stop, reads, writes, tag="mm"):
        pr.op("tensor",
              (lambda d, l, r, st, sp:
               lambda t: t.matmul(d, l, r, start=st, stop=sp))
              (dst, lhsT, rhs, start, stop),
              reads=reads, writes=writes, tag=tag)

    def tp(dst, src, ident, reads, writes, tag="tp"):
        pr.op("tensor",
              (lambda d, s, i: lambda t: t.transpose(d, s, i))(dst, src, ident),
              reads=reads, writes=writes, tag=tag)

    def vop(fn, reads, writes, tag="v"):
        pr.op("vector", fn, reads=reads, writes=writes, tag=tag)

    def sop(fn, reads, writes, tag="s"):
        pr.op("scalar", fn, reads=reads, writes=writes, tag=tag)

    def prelude():
        for dst, src in [
            (c_identb, ident_b), (c_identf, ident_f), (c_ut128f, ut128f_d),
            (c_ut32f, ut32f_d), (c_iota8, iota8_d), (c_onesrow_f, onesrow_f_d),
            (c_onesrow_b, onesrow_b_d), (c_onescol_f, onescol_f_d),
            (c_onescol32_f, onescol32_d), (c_diagm, diagm_d),
            (c_diagmT, diagmT_d), (c_selrank, selrank_d),
        ]:
            dma("sync", dst[:], src, writes=[dst.name], tag="const")
        dma("sync", s_wg[:].rearrange("p (c e) -> p c e", c=DC),
            wg_d.rearrange("(c p) e -> p c e", p=P), writes=["s_wg"], tag="wg")
        dma("sync", v_fb1raw.rearrange("c (ee p) -> c ee p", ee=NE),
            fb1_d.rearrange("ee (c p) -> c ee p", p=P),
            writes=["scrA"], tag="fb1")
        for e in range(NE):
            tp(pC[0:P, 0:32], v_fb1raw[:, e * P:(e + 1) * P],
               c_identf[0:32, 0:32],
               reads=["scrA", "c_identf", "pC.0"], writes=["pC.0"], tag="fb1t")
            vop((lambda e: lambda v: v.tensor_copy(
                s_fb1t[:, e * FC:(e + 1) * FC], pC[0:P, 0:32]))(e),
                reads=["pC.0"], writes=["s_fb1t"], tag="fb1e")
        dma("sync", v_ab1raw, ab1_d.rearrange("o (c p) -> (o c) p", p=P),
            reads=["scrA"], writes=["scrA"], tag="ab1")
        tp(pC[0:P, 32:36], v_ab1raw, c_identf[0:4, 0:4],
           reads=["scrA", "c_identf", "pC.0"], writes=["pC.0"], tag="ab1t")
        vop(lambda v: v.tensor_copy(s_ab1t[:], pC[0:P, 32:36]),
            reads=["pC.0"], writes=["s_ab1t"], tag="ab1e")
        dma("sync", v_abraw, ab2_d.rearrange("o (c p) -> (o c) p", p=P),
            reads=["scrA"], writes=["scrA"], tag="ab2")
        tp(pC[0:P, 36:44], v_abraw, c_identf[0:8, 0:8],
           reads=["scrA", "c_identf", "pC.0"], writes=["pC.0"], tag="ab2t")
        vop(lambda v: v.tensor_copy(s_ab2t[:], pC[0:P, 36:44]),
            reads=["pC.0"], writes=["s_ab2t"], tag="ab2e")

    def cache_weights():
        for nm, dd, ss in (("wq", wqb, wq_d), ("wk", wkb, wk_d),
                           ("wv", wvb, wv_d), ("wo", wob, wo_d),
                           ("aw1", aw1b, aw1_d), ("aw2", aw2b, aw2_d)):
            dma("gpsimd", dd, ss, writes=[("wbc", nm)], tag="wcc")

    def xstage(t, deep=True):
        if not deep:
            return s_xt[t % 2][:], [("xtb", t % 2)]
        if t % 4 < 2:
            return s_xt[t % 4][:], [("xtb", t % 4)]
        j = t % 4 - 2
        xw = s_w1w[0][:].bitcast(F32)
        return (xw[:, j * 1024:(j + 1) * 1024],
                [("w1b", 0, 4 * j + i) for i in range(4)])

    def ln_pass(C, src_hbm, n_tiles, skey=None, soff=0, deep=True):
        def hk(t):
            return [("hbm", skey, soff + t)] if skey else []
        for t in range(n_tiles):
            xt, rx = xstage(t, deep)
            dma("sync" if t % 2 == 0 else "scalar",
                xt, src_hbm[t * P:(t + 1) * P, :],
                reads=rx + hk(t), writes=rx, tag="ldx")
            vop((lambda t, xt: lambda v: v.tensor_reduce(
                s_mean[:, t:t + 1], xt, AX.X, ALU.add))(t, xt),
                reads=rx, writes=[(C, "mean", t)], tag="mean")
            sop((lambda t, xt: lambda s: s.activation(
                s_scrA[:], xt, AF.Square, bias=0.0, scale=1.0 / 32.0,
                accum_out=s_ssq[:, t:t + 1]))(t, xt),
                reads=rx + ["scrA"], writes=[(C, "ssq", t), "scrA"],
                tag="ssq")
        allm = [(C, "mean", t) for t in range(n_tiles)]
        allq = [(C, "ssq", t) for t in range(n_tiles)]
        vop(lambda v: v.tensor_scalar(
            s_mean[:, 0:n_tiles], s_mean[:, 0:n_tiles], 1.0 / D, None, ALU.mult),
            reads=allm, writes=[(C, "meanN")], tag="meanN")
        vop(lambda v: v.tensor_mul(
            s_rstd[:, 0:n_tiles], s_mean[:, 0:n_tiles], s_mean[:, 0:n_tiles]),
            reads=[(C, "meanN")], writes=[(C, "m2")], tag="m2")
        vop(lambda v: v.tensor_sub(
            s_rstd[:, 0:n_tiles], s_rstd[:, 0:n_tiles], s_ssq[:, 0:n_tiles]),
            reads=[(C, "m2")] + allq, writes=[(C, "negvar0")], tag="negvar")
        vop(lambda v: v.tensor_scalar(
            s_rstd[:, 0:n_tiles], s_rstd[:, 0:n_tiles], -1e-5, None, ALU.add),
            reads=[(C, "negvar0")], writes=[(C, "negvar")], tag="nveps")
        sop(lambda s: s.activation(
            s_rstd[:, 0:n_tiles], s_rstd[:, 0:n_tiles], AF.Ln,
            bias=0.0, scale=-1.0),
            reads=[(C, "negvar")], writes=[(C, "lnvar")], tag="lnvar")
        sop(lambda s: s.activation(
            s_rstd[:, 0:n_tiles], s_rstd[:, 0:n_tiles], AF.Exp,
            bias=0.0, scale=-0.5),
            reads=[(C, "lnvar")], writes=[(C, "rstd")], tag="rstd")

    def ffn(call, src_hbm, dst_hbm, skey=None, dkey=None, fuse=False):
        C = f"f{call}"

        def hk(t):
            return [("hbm", skey, t)] if skey else []

        def pass2_tile(t, stg, rsk):
            xt, rx = stg
            vop((lambda t, xt: lambda v: v.tensor_scalar(
                s_scrA[:], xt, s_mean[:, t:t + 1], s_rstd[:, t:t + 1],
                ALU.subtract, ALU.mult))(t, xt),
                reads=rx + [rsk, "scrA"],
                writes=["scrA", (C, "zf", t)], tag="znorm")
            sop((lambda t: lambda s: s.activation(
                s_h[:, t * D:(t + 1) * D], s_scrA[:], AF.Identity,
                bias=0.0, scale=1.0))(t),
                reads=[(C, "zf", t), "scrA", ("hb", t)], writes=[("hb", t)],
                tag="zcast")
            pz = pA if t % 2 == 0 else pB
            pzk = ["pA.0", "pA.1"] if t % 2 == 0 else ["pB.0", "pB.1"]
            for c in range(DC):
                tp(pz[:, c * P:(c + 1) * P], s_scrA[:, c * P:(c + 1) * P],
                   c_identf[:],
                   reads=[(C, "zf", t), "scrA", "c_identf"] + pzk,
                   writes=pzk, tag="ztp")
            vop((lambda pz: lambda v: v.tensor_copy(s_scrB[:], pz[:]))(pz),
                reads=pzk + ["scrB"], writes=["scrB", (C, "zt", t)], tag="zte")
            for c in range(DC):
                mm(pC[:, 0:NE], s_scrB[:, c * P:(c + 1) * P],
                   s_wg[:, c * NE:(c + 1) * NE], c == 0, c == DC - 1,
                   reads=[(C, "zt", t), "scrB", "s_wg", "pC.0"],
                   writes=["pC.0"], tag="gate")
            l8 = s_l8p[t % 2]
            mx8 = s_mx8p[t % 2]
            ix8 = s_ix8p[t % 2]
            rl8, rmx, rix = ("l8b", t % 2), ("mx8b", t % 2), ("ix8b", t % 2)
            vop((lambda l8: lambda v: v.tensor_copy(l8[:], pC[:, 0:NE]))(l8),
                reads=["pC.0", rl8], writes=[rl8, (C, "l8", t)], tag="l8e")
            vop((lambda l8, mx8: lambda v: v.max(mx8[:], l8[:]))(l8, mx8),
                reads=[(C, "l8", t), rmx], writes=[rmx, (C, "mx8", t)], tag="max8")
            vop((lambda l8, mx8, ix8: lambda v: v.max_index(ix8[:], mx8[:], l8[:]))
                (l8, mx8, ix8),
                reads=[(C, "l8", t), (C, "mx8", t), rix],
                writes=[rix, (C, "ix8", t)], tag="idx8")
            vop((lambda t, mx8: lambda v: v.tensor_copy(
                s_m12[:, 2 * t:2 * t + 2], mx8[:, 0:2]))(t, mx8),
                reads=[(C, "mx8", t)], writes=[(C, "m12", t)], tag="m12")
            vop((lambda t, mx8: lambda v: v.tensor_scalar(
                s_negm1[:, t:t + 1], mx8[:, 0:1], -1.0, None, ALU.mult))(t, mx8),
                reads=[(C, "mx8", t)], writes=[(C, "negm1", t)], tag="negm1")
            sop((lambda t, l8: lambda s: s.activation(
                s_junk8[:], l8[:], AF.Exp, bias=s_negm1[:, t:t + 1], scale=1.0,
                accum_out=s_S8[:, t:t + 1]))(t, l8),
                reads=[(C, "l8", t), (C, "negm1", t), "s_junk8"],
                writes=[(C, "S8", t), "s_junk8"], tag="gexp")
            vop((lambda t, ix8: lambda v: v.tensor_copy(
                s_idxf[:, 2 * t:2 * t + 2], ix8[:, 0:2]))(t, ix8),
                reads=[(C, "ix8", t)], writes=[(C, "idxf", t)], tag="idxf")
            for j in range(2):
                vop((lambda t, j: lambda v: v.tensor_scalar(
                    s_masksf[:, 16 * t + 8 * j:16 * t + 8 * j + 8],
                    c_iota8[:], s_idxf[:, 2 * t + j:2 * t + j + 1], None,
                    ALU.is_equal))(t, j),
                    reads=[(C, "idxf", t), "c_iota8"],
                    writes=[(C, "mask", t, j)], tag="mask")

        staged = {}

        def grp_stats(g):
            for a in (2 * g, 2 * g + 1):
                xt, rx = xstage(a)
                staged[a] = (xt, rx)
                dma("sync" if a % 2 == 0 else "scalar",
                    xt, src_hbm[a * P:(a + 1) * P, :],
                    reads=rx + hk(a), writes=rx, tag="ldx2")
                zbj = s_xtok[:, (a % 2) * 1024:(a % 2 + 1) * 1024]
                rzb = ("xtokR", a % 2)
                sop((lambda xt, zbj, a: lambda s: s.activation(
                    zbj, xt, AF.Identity, bias=0.0, scale=1.0 / D,
                    accum_out=s_mean[:, a:a + 1]))(xt, zbj, a),
                    reads=rx + [rzb], writes=[rzb, (C, "fm", a)], tag="pmean")
                sop((lambda xt, zbj, a: lambda s: s.activation(
                    zbj, xt, AF.Square, bias=0.0, scale=1.0 / 32.0,
                    accum_out=s_ssq[:, a:a + 1]))(xt, zbj, a),
                    reads=rx + [rzb], writes=[rzb, (C, "fq", a)], tag="pssq")

        def grp_rstd(g):
            c0, c1 = 2 * g, 2 * g + 2
            vop((lambda c0, c1: lambda v: v.tensor_mul(
                s_rstd[:, c0:c1], s_mean[:, c0:c1], s_mean[:, c0:c1]))(c0, c1),
                reads=[(C, "fm", c0), (C, "fm", c0 + 1)],
                writes=[(C, "fv", g)], tag="pm2")
            vop((lambda c0, c1: lambda v: v.tensor_sub(
                s_rstd[:, c0:c1], s_rstd[:, c0:c1], s_ssq[:, c0:c1]))(c0, c1),
                reads=[(C, "fq", c0), (C, "fq", c0 + 1), (C, "fv", g)],
                writes=[(C, "fv", g)], tag="pnv")
            vop((lambda c0, c1: lambda v: v.tensor_scalar(
                s_rstd[:, c0:c1], s_rstd[:, c0:c1], -1e-5, None, ALU.add))
                (c0, c1),
                reads=[(C, "fv", g)], writes=[(C, "fv", g)], tag="pnve")
            sop((lambda c0, c1: lambda s: s.activation(
                s_rstd[:, c0:c1], s_rstd[:, c0:c1], AF.Ln, bias=0.0,
                scale=-1.0))(c0, c1),
                reads=[(C, "fv", g)], writes=[(C, "fv", g)], tag="plnv")
            sop((lambda c0, c1: lambda s: s.activation(
                s_rstd[:, c0:c1], s_rstd[:, c0:c1], AF.Exp, bias=0.0,
                scale=-0.5))(c0, c1),
                reads=[(C, "fv", g)], writes=[(C, "frs2", g)], tag="prstd")

        grp_stats(0)
        for g in range(1, NT // 2):
            grp_stats(g)
            grp_rstd(g - 1)
            for t in (2 * (g - 1), 2 * (g - 1) + 1):
                pass2_tile(t, staged[t], (C, "frs2", g - 1))
        grp_rstd(NT // 2 - 1)
        for t in (NT - 2, NT - 1):
            pass2_tile(t, staged[t], (C, "frs2", NT // 2 - 1))


        allmask = [(C, "mask", t, j) for t in range(NT) for j in range(2)]
        allidxf = [(C, "idxf", t) for t in range(NT)]
        allm12 = [(C, "m12", t) for t in range(NT)]
        allS8 = [(C, "S8", t) for t in range(NT)]

        mm(pB[:, 0:512], c_ut128f[:], s_masksf[:], True, False,
           reads=["c_ut128f", "pB.0"] + allmask, writes=["pB.0"], tag="utg")
        mm(pA[:, 0:512], c_ut128f[:], s_masksf[:], True, False,
           reads=["c_ut128f", "pA.0"] + allmask, writes=["pA.0"], tag="utl")
        mm(pY[0:1, 0:512], c_onescol_f[:], s_masksf[:], True, True,
           reads=["c_onescol_f", "pY.0"] + allmask, writes=["pY.0"], tag="tot")
        vop(lambda v: v.tensor_copy(v_rep, pY[0:1, 0:512]),
            reads=["pY.0", "rows"], writes=["rows", (C, "trow")], tag="trowe")
        dma("gpsimd", s_t32[:], v_rep,
            reads=[(C, "trow")], writes=[(C, "t32")], tag="t32")
        mm(pC[0:32, 0:16], c_ut32f[:], s_t32[:], True, True,
           reads=["c_ut32f", (C, "t32"), "pC.0"], writes=["pC.0"], tag="ut32")
        vop(lambda v: v.tensor_copy(s_toff[:], pC[0:32, 0:16]),
            reads=["pC.0"], writes=[(C, "toff")], tag="toffe")
        mm(pC[0:1, 16:32], c_onescol32_f[:], s_t32[:], True, True,
           reads=["c_onescol32_f", (C, "t32"), "pC.0"], writes=["pC.0"], tag="ltm")
        vop(lambda v: v.tensor_copy(s_lt[:], pC[0:1, 16:32]),
            reads=["pC.0"], writes=[(C, "lt")], tag="lte")
        dma("gpsimd", v_toffrow, s_toff[:],
            reads=[(C, "toff"), "rows"], writes=["rows", (C, "toffrow")],
            tag="toffrow")
        dma("gpsimd", cc_in, s_lt[:], reads=[(C, "lt"), "ccio"],
            writes=["ccio", (C, "ccin")], tag="ccw")
        pr.op("gpsimd", lambda g: g.collective_compute(
            "AllGather", ALU.bypass, replica_groups=[list(range(NE))],
            ins=[cc_in.opt()], outs=[cc_out.opt()]),
            reads=[(C, "ccin"), "ccio"], writes=["ccio", (C, "ccout")], tag="cc")
        vop(lambda v: v.memset(s_loff[:], 0.0),
            reads=[], writes=[(C, "loff0")], tag="loffz")
        vop(lambda v: v.tensor_copy(s_loff[0:1, 8:16], s_lt[0:1, 0:8]),
            reads=[(C, "loff0"), (C, "lt")], writes=[(C, "loff")], tag="loffc")
        for t in range(NT):
            vop((lambda t: lambda v: v.tensor_add(
                v_rep[0:1, 16 * t:16 * t + 16],
                v_toffrow[0:1, 16 * t:16 * t + 16], s_loff[0:1, :]))(t),
                reads=[(C, "toffrow"), (C, "loff"), "rows"],
                writes=["rows", (C, "lrow", t)], tag="lrow")
        mm(pA[:, 0:512], c_onesrow_f[:], v_rep, False, True,
           reads=["c_onesrow_f", "pA.0"] + [(C, "lrow", t) for t in range(NT)],
           writes=["pA.0"], tag="lk1")
        vop(lambda v: v.tensor_mul(s_scrA[:, 0:512], pA[:, 0:512], s_masksf[:]),
            reads=["pA.0", "scrA"] + allmask, writes=["scrA", (C, "prodl")],
            tag="prodl")
        vop(lambda v: v.tensor_reduce(
            s_posl[:], s_scrA[:, 0:512].rearrange("p (g e) -> p g e", e=8),
            AX.X, ALU.add),
            reads=[(C, "prodl"), "scrA"], writes=[(C, "posl")], tag="redl")
        vop(lambda v: v.tensor_scalar(
            s_keep[:], s_posl[:], float(CAP), None, ALU.is_lt),
            reads=[(C, "posl")], writes=[(C, "kl")], tag="kl")
        m1v = s_m12[:].rearrange("p (t j) -> p t j", j=2)[:, :, 0]
        m2v = s_m12[:].rearrange("p (t j) -> p t j", j=2)[:, :, 1]
        vop(lambda v: v.tensor_sub(s_ta[:, 0:NT], m2v, m1v),
            reads=allm12, writes=[(C, "d21")], tag="d21")
        sop(lambda s: s.activation(s_ta[:, 0:NT], s_ta[:, 0:NT], AF.Exp),
            reads=[(C, "d21")], writes=[(C, "e21")], tag="e21")
        vop(lambda v: v.reciprocal(s_tb[:, 0:NT], s_S8[:]),
            reads=allS8, writes=[(C, "g1")], tag="g1")
        vop(lambda v: v.tensor_mul(s_ta[:, NT:2 * NT], s_ta[:, 0:NT],
                                   s_tb[:, 0:NT]),
            reads=[(C, "e21"), (C, "g1")], writes=[(C, "g2")], tag="g2")
        vop(lambda v: v.tensor_add(s_ta[:, 0:NT], s_ta[:, NT:2 * NT],
                                   s_tb[:, 0:NT]),
            reads=[(C, "g1"), (C, "g2")], writes=[(C, "gsum")], tag="gsum")
        vop(lambda v: v.tensor_scalar(
            s_ta[:, 0:NT], s_ta[:, 0:NT], 1e-9, None, ALU.add),
            reads=[(C, "gsum")], writes=[(C, "den")], tag="den")
        vop(lambda v: v.reciprocal(s_ta[:, 0:NT], s_ta[:, 0:NT]),
            reads=[(C, "den")], writes=[(C, "rden")], tag="rden")
        w1v = s_wk_[:].rearrange("p (t j) -> p t j", j=2)[:, :, 0]
        w2v = s_wk_[:].rearrange("p (t j) -> p t j", j=2)[:, :, 1]
        vop(lambda v: v.tensor_mul(w1v, s_tb[:, 0:NT], s_ta[:, 0:NT]),
            reads=[(C, "g1"), (C, "rden")], writes=[(C, "w1")], tag="w1")
        vop(lambda v: v.tensor_mul(w2v, s_ta[:, NT:2 * NT], s_ta[:, 0:NT]),
            reads=[(C, "g2"), (C, "rden")], writes=[(C, "w2")], tag="w2")
        vop(lambda v: v.scalar_tensor_tensor(
            s_ta[:], s_idxf[:], float(CAP), s_posl[:], ALU.mult, ALU.add),
            reads=allidxf + [(C, "posl")], writes=[(C, "lidx")], tag="lidx")
        vop(lambda v: v.tensor_scalar(
            s_tb[:], s_keep[:], -BIGIDX, BIGIDX, ALU.mult, ALU.add),
            reads=[(C, "kl")], writes=[(C, "sentm")], tag="sentm")
        vop(lambda v: v.tensor_add(s_tb[:], s_tb[:], s_ta[:]),
            reads=[(C, "sentm"), (C, "lidx")], writes=[(C, "lidxF")],
            tag="lidxF")
        vop(lambda v: v.tensor_copy(s_idxi[:], s_tb[:]),
            reads=[(C, "lidxF")], writes=[(C, "idxi")], tag="idxi")

        for t in range(NT):
            for j in range(2):
                col = 2 * t + j
                pr.op("gpsimd", (lambda t, col: lambda g: g.indirect_dma_start(
                    xe_dram, IOOA(s_idxi[:, col:col + 1], 0),
                    s_h[:, t * D:(t + 1) * D], None,
                    bounds_check=bcreg["r"], oob_is_err=False))(t, col),
                    reads=[(C, "idxi"), ("hb", t)],
                    writes=[(C, "scat", t, j)], dma=True, tag="scat")
        pr.barrier(tag=f"{C}_scatdone")

        def w1load(e, sh, w, c):
            wb = w % 2
            dma("gpsimd", s_w1w[wb][:, c * 512:(c + 1) * 512],
                fw1_d[e, c * P:(c + 1) * P, w * 512:(w + 1) * 512],
                reads=[("w1b", wb, c)],
                writes=[("w1b", wb, c), (C, "w1", e, sh, w, c)], tag="w1l")

        for e in range(NE):
            def w2load(f):
                dma("gpsimd", s_w2[:, f * 1024:(f + 1) * 1024],
                    fw2_d[e, f * P:(f + 1) * P, :],
                    reads=[("w2B", f)], writes=[("w2B", f)], tag="w2l")
            dma("sync", v_fb2row, fb2_d[e:e + 1, :],
                reads=["scrA"], writes=["scrA", (C, "b2r", e)], tag="b2l")
            for hf in range(2):
                mm(pC[:, 0:512], c_onesrow_f[:],
                   v_fb2row[0:1, hf * 512:(hf + 1) * 512], True, True,
                   reads=["c_onesrow_f", (C, "b2r", e), "scrA", "pC.0"],
                   writes=["pC.0"], tag="b2k1")
                vop((lambda hf: lambda v: v.tensor_copy(
                    s_scrB[:, hf * 512:(hf + 1) * 512], pC[:, 0:512]))(hf),
                    reads=["pC.0", "scrB"], writes=["scrB", (C, "b2bc", e, hf)],
                    tag="b2e")
            for sh in range(2):
                for w in range(2):
                    for c in range(DC):
                        w1load(e, sh, w, c)
                    if sh == 0:
                        for f in range(4 * w, 4 * w + 4):
                            w2load(f)
                for g2 in range(4):
                    bp = (g2 % 2) * 2
                    rows0 = e * CAP + sh * 1024 + g2 * 256
                    dma("scalar",
                        s_xtok[:, bp * 1024:(bp + 2) * 1024]
                        .rearrange("p (tt d) -> p tt d", tt=2),
                        xe_dram[rows0:rows0 + 256, :]
                        .rearrange("(tt p) d -> p tt d", p=P),
                        reads=[("xtokR", bp), ("xtokR", bp + 1)],
                        writes=[("xtokR", bp), ("xtokR", bp + 1),
                                (C, "xtok", e, sh, g2)],
                        tag="xel")
                    for c in range(DC):
                        kgrp = c + g2
                        pdst = pAb if kgrp % 2 == 0 else pBb
                        pk = "pA.0" if kgrp % 2 == 0 else "pB.0"
                        for tt in range(2):
                            tp(pdst[:, tt * P:(tt + 1) * P],
                               s_xtok[:, (bp + tt) * D + c * P:(bp + tt) * D + (c + 1) * P],
                               c_identb[:],
                               reads=[(C, "xtok", e, sh, g2), ("xtokR", bp + tt),
                                      "c_identb", pk],
                               writes=[pk], tag="xtp")
                        vop((lambda c, g2, pdst: lambda v: v.tensor_copy(
                            s_xet[:, c * 1024 + g2 * 256:c * 1024 + g2 * 256 + 256],
                            pdst[:, 0:256]))(c, g2, pdst),
                            reads=[pk, ("xetb", c)],
                            writes=[("xetb", c), (C, "xet", e, sh, c, g2)],
                            tag="xte")
                xetall = [(C, "xet", e, sh, c, g2)
                          for c in range(DC) for g2 in range(4)]
                if DEBUG_TAPS and call == "A" and e == 0 and sh == 0:
                    dma("sync", dbg["xet0"], s_xet[:], reads=list(xetall),
                        writes=[("dbgw", "xet0")], tag="dbgxet")
                for w in range(8):
                    wb = w % 2
                    if w >= 2:
                        for c in range(DC):
                            w1load(e, sh, w, c)
                        if sh == 0:
                            for f in range(4 * w, 4 * w + 4):
                                w2load(f)
                    for fc4 in range(4):
                        f = w * 4 + fc4
                        ps, pk0, pk1 = ((pB, "pB.0", "pB.1") if (f % 2 == 0)
                                        else (pA, "pA.0", "pA.1"))
                        for d in range(DC):
                            lhs = s_w1w[wb][:, d * 512 + fc4 * P:
                                            d * 512 + (fc4 + 1) * P]
                            w1keys = [(C, "w1", e, sh, w, c) for c in range(DC)]
                            mm(ps[:, 0:512], lhs,
                               s_xet[:, d * 1024:d * 1024 + 512],
                               d == 0, d == DC - 1,
                               reads=w1keys + [("w1b", wb, d),
                                      ("xetb", d), pk0] + xetall,
                               writes=[pk0], tag="mm1")
                            mm(ps[:, 512:1024], lhs,
                               s_xet[:, d * 1024 + 512:d * 1024 + 1024],
                               d == 0, d == DC - 1,
                               reads=w1keys + [("w1b", wb, d),
                                      ("xetb", d), pk1] + xetall,
                               writes=[pk1], tag="mm1")
                        sop((lambda ps, e, f: lambda s: s.activation(
                            s_h[:, f * 1024:(f + 1) * 1024], ps[:], AF.Gelu,
                            bias=s_fb1t[:, e * FC + f:e * FC + f + 1],
                            scale=1.0))(ps, e, f),
                            reads=[pk0, pk1, "s_fb1t", ("hb", f)],
                            writes=[("hb", f), (C, "h", e, sh, f)], tag="gelu")
                if DEBUG_TAPS and call == "A" and e == 0 and sh == 0:
                    dma("sync", dbg["h0"], s_h[:],
                        reads=[(C, "h", e, sh, f) for f in range(FC)],
                        writes=[("dbgw", "h0")], tag="dbgh0")
                for st in range(8):
                    py, yk0, yk1 = ((pY, "pY.0", "pY.1") if st % 2 == 0
                                    else (pC, "pC.0", "pC.1"))
                    for f in range(FC):
                        lhs = s_h[:, f * 1024 + st * P:f * 1024 + (st + 1) * P]
                        mm(py[:, 0:512], lhs, s_w2[:, f * 1024:f * 1024 + 512],
                           f == 0, f == FC - 1,
                           reads=[(C, "h", e, sh, f), ("hb", f), ("w2B", f),
                                  yk0],
                           writes=[yk0], tag="mm2")
                        mm(py[:, 512:1024], lhs,
                           s_w2[:, f * 1024 + 512:f * 1024 + 1024],
                           f == 0, f == FC - 1,
                           reads=[(C, "h", e, sh, f), ("hb", f), ("w2B", f),
                                  yk1],
                           writes=[yk1], tag="mm2")
                    yst = s_yest[st % 2]
                    ry = ("yestb", st % 2)
                    vop((lambda yst, py: lambda v: v.scalar_tensor_tensor(
                        yst[:], py[:], 1.0, s_scrB[:], ALU.mult, ALU.add))(yst, py),
                        reads=[yk0, yk1, (C, "b2bc", e, 0), (C, "b2bc", e, 1),
                               "scrB", ry],
                        writes=[ry, (C, "yest", e, sh, st)], tag="yee")
                    rows0 = e * CAP + sh * 1024 + st * P
                    dma("sync", ye_dram[rows0:rows0 + P, :], yst[:],
                        reads=[(C, "yest", e, sh, st), ry], writes=[ry, (C, "yed", e, sh, st)],
                        tag="yew")
        allye = [(C, "yed", e, sh, st)
                 for e in range(NE) for sh in range(2) for st in range(8)]
        dma("gpsimd", s_ag[:], cc_out, reads=[(C, "ccout")],
            writes=[(C, "ag")], tag="ccr")
        mm(pC[0:1, 32:48], c_selrank[:, 0:1], s_ag[:], True, True,
           reads=["c_selrank", (C, "ag"), "pC.0"], writes=["pC.0"], tag="s1m")
        vop(lambda v: v.tensor_copy(s_s1[:], pC[0:1, 32:48]),
            reads=["pC.0"], writes=[(C, "s1")], tag="s1e")
        mm(pC[0:1, 48:64], c_selrank[:, 1:2], s_ag[:], True, True,
           reads=["c_selrank", (C, "ag"), "pC.0"], writes=["pC.0"], tag="s2m")
        vop(lambda v: v.tensor_copy(s_s2[:], pC[0:1, 48:64]),
            reads=["pC.0"], writes=[(C, "s2")], tag="s2e")
        vop(lambda v: v.tensor_copy(s_goff[:], s_s1[:]),
            reads=[(C, "s1")], writes=[(C, "goff0")], tag="goffc")
        vop(lambda v: v.tensor_add(
            s_goff[0:1, 8:16], s_s1[0:1, 8:16], s_s2[0:1, 0:8]),
            reads=[(C, "goff0"), (C, "s1"), (C, "s2")], writes=[(C, "goff")],
            tag="goffa")
        vop(lambda v: v.tensor_sub(s_goff[:], s_goff[:], s_loff[:]),
            reads=[(C, "goff"), (C, "loff")], writes=[(C, "gdelta")],
            tag="gdelta")
        for t in range(NT):
            vop((lambda t: lambda v: v.tensor_copy(
                v_rep[0:1, 16 * t:16 * t + 16], s_goff[0:1, :]))(t),
                reads=[(C, "gdelta"), "rows"],
                writes=["rows", (C, "grow", t)], tag="grow")
        mm(pB[:, 0:512], c_onesrow_f[:], v_rep, True, True,
           reads=["c_onesrow_f", "pB.0"] + [(C, "grow", t) for t in range(NT)],
           writes=["pB.0"], tag="gk1")
        vop(lambda v: v.tensor_mul(s_scrA[:, 0:512], pB[:, 0:512], s_masksf[:]),
            reads=["pB.0", "scrA"] + allmask,
            writes=["scrA", (C, "prodg")], tag="prodg")
        vop(lambda v: v.tensor_reduce(
            s_posg[:], s_scrA[:, 0:512].rearrange("p (g e) -> p g e", e=8),
            AX.X, ALU.add),
            reads=[(C, "prodg"), "scrA"], writes=[(C, "posg0")], tag="redg")
        vop(lambda v: v.tensor_add(s_posg[:], s_posg[:], s_posl[:]),
            reads=[(C, "posg0"), (C, "posl")], writes=[(C, "posg")],
            tag="posg")
        vop(lambda v: v.tensor_scalar(
            s_ta[:], s_posg[:], float(GCAP), None, ALU.is_lt),
            reads=[(C, "posg")], writes=[(C, "kg")], tag="kg")
        vop(lambda v: v.tensor_mul(s_keep[:], s_keep[:], s_ta[:]),
            reads=[(C, "kg"), (C, "kl")], writes=[(C, "keep")], tag="keep")
        vop(lambda v: v.tensor_mul(s_wk_[:], s_wk_[:], s_keep[:]),
            reads=[(C, "w1"), (C, "w2"), (C, "keep")], writes=[(C, "wkk")],
            tag="wkk")
        vop(lambda v: v.tensor_scalar(s_wk_[:], s_wk_[:], 0.5, None, ALU.mult),
            reads=[(C, "wkk")], writes=[(C, "wk")], tag="wkf")
        if DEBUG_TAPS and call == "A":
            for nm, buf, rk in [("idxf", s_idxf, [(C, "idxf", t) for t in range(NT)]),
                                ("posl", s_posl, [(C, "posl")]),
                                ("posg", s_posg, [(C, "posg")]),
                                ("keep", s_keep, [(C, "keep")]),
                                ("wk", s_wk_, [(C, "wk")]),
                                ("S8", s_S8, [(C, "S8", t) for t in range(NT)]),
                                ("m12", s_m12, [(C, "m12", t) for t in range(NT)]),
                                ("idxi", s_idxi, [(C, "idxi")]),
                                ("masks", s_masksf,
                                 [(C, "mask", t, j) for t in range(NT) for j in range(2)])]:
                dma("sync", dbg[nm], buf[:], reads=rk, writes=[("dbgw", nm)],
                    tag="dbg")


        if DEBUG_TAPS and call == "A":
            dma("sync", dbg["xe"], xe_dram, reads=[(C, "scat", t, j) for t in range(NT) for j in range(2)],
                writes=[("dbgw", "xe")], tag="dbgxe")
            dma("sync", dbg["ye"], ye_dram, reads=allye, writes=[("dbgw", "ye")],
                tag="dbgye")

        def rot(t):
            if t % 4 < 2:
                return s_yest[t % 4][:], ("yestb", t % 4)
            return s_xt[t % 4 - 2][:], ("xtb", t % 4 - 2)

        def combine_tile(t):
            pr.op("gpsimd", (lambda t: lambda g: g.indirect_dma_start(
                s_scrB[:], None, ye_dram, IOOA(s_idxi[:, 2 * t:2 * t + 1], 0),
                bounds_check=bcreg["r"], oob_is_err=False))(t),
                reads=[(C, "idxi"), "scrB"] + allye,
                writes=["scrB", (C, "gA", t)], dma=True, tag="gat1")
            pr.op("gpsimd", (lambda t: lambda g: g.indirect_dma_start(
                s_scrA[:], None, ye_dram, IOOA(s_idxi[:, 2 * t + 1:2 * t + 2], 0),
                bounds_check=bcreg["r"], oob_is_err=False))(t),
                reads=[(C, "idxi"), "scrA"] + allye,
                writes=["scrA", (C, "gB", t)], dma=True, tag="gat2")
            xw = s_w1w[(t // 2) % 2][:].bitcast(F32)
            xt = xw[:, (t % 2) * 1024:(t % 2 + 1) * 1024]
            rx4 = [("w1b", (t // 2) % 2, 4 * (t % 2) + i) for i in range(4)]
            dma("sync", xt, src_hbm[t * P:(t + 1) * P, :],
                reads=rx4 + hk(t), writes=rx4, tag="ldx3")
            vop((lambda t, xt: lambda v: v.scalar_tensor_tensor(
                s_scrB[:], s_scrB[:], s_wk_[:, 2 * t:2 * t + 1], xt,
                ALU.mult, ALU.add))(t, xt),
                reads=[(C, "gA", t), (C, "wk"), "scrB"] + rx4,
                writes=["scrB", (C, "acc1", t)] + rx4, tag="acc1")
            yst, ry = rot(t) if fuse else (s_yest[t % 2][:], ("yestb", t % 2))
            vop((lambda t, yst: lambda v: v.scalar_tensor_tensor(
                yst, s_scrA[:], s_wk_[:, 2 * t + 1:2 * t + 2], s_scrB[:],
                ALU.mult, ALU.add))(t, yst),
                reads=[(C, "gB", t), (C, "acc1", t), (C, "wk"), "scrB",
                       "scrA", ry],
                writes=[ry, (C, "ost", t)], tag="acc2")
            dma("sync", dst_hbm[t * P:(t + 1) * P, :], yst,
                reads=[(C, "ost", t), ry],
                writes=[ry, (C, "dst", t)] +
                ([("hbm", dkey, t)] if dkey else []), tag="stx")
            if fuse:
                zbj = s_xtok[:, (t % 2) * 1024:(t % 2 + 1) * 1024]
                rzb = ("xtokR", t % 2)
                sop((lambda yst, zbj, t: lambda s: s.activation(
                    zbj, yst, AF.Identity, bias=0.0, scale=1.0 / D,
                    accum_out=s_mean[:, t:t + 1]))(yst, zbj, t),
                    reads=[ry, (C, "ost", t), rzb],
                    writes=[rzb, ("fm", t)], tag="fmean")
                sop((lambda yst, zbj, t: lambda s: s.activation(
                    zbj, yst, AF.Square, bias=0.0, scale=1.0 / 32.0,
                    accum_out=s_ssq[:, t:t + 1]))(yst, zbj, t),
                    reads=[ry, (C, "ost", t), rzb],
                    writes=[rzb, ("fq", t)], tag="fssq")

        def fuse_rstd(g):
            c0, c1 = 2 * g, 2 * g + 2
            vop((lambda c0, c1: lambda v: v.tensor_mul(
                s_rstd[:, c0:c1], s_mean[:, c0:c1], s_mean[:, c0:c1]))(c0, c1),
                reads=[("fm", c0), ("fm", c0 + 1)], writes=[("fv", g)],
                tag="fm2")
            vop((lambda c0, c1: lambda v: v.tensor_sub(
                s_rstd[:, c0:c1], s_rstd[:, c0:c1], s_ssq[:, c0:c1]))(c0, c1),
                reads=[("fq", c0), ("fq", c0 + 1), ("fv", g)],
                writes=[("fv", g)], tag="fnv")
            vop((lambda c0, c1: lambda v: v.tensor_scalar(
                s_rstd[:, c0:c1], s_rstd[:, c0:c1], -1e-5, None, ALU.add))
                (c0, c1),
                reads=[("fv", g)], writes=[("fv", g)], tag="fnve")
            sop((lambda c0, c1: lambda s: s.activation(
                s_rstd[:, c0:c1], s_rstd[:, c0:c1], AF.Ln, bias=0.0,
                scale=-1.0))(c0, c1),
                reads=[("fv", g)], writes=[("fv", g)], tag="flnv")
            sop((lambda c0, c1: lambda s: s.activation(
                s_rstd[:, c0:c1], s_rstd[:, c0:c1], AF.Exp, bias=0.0,
                scale=-0.5))(c0, c1),
                reads=[("fv", g)], writes=[("frs", g)], tag="frstd")

        def fuse_zprep(t):
            A0 = "a0"
            yst, ry = rot(t)
            zbj = s_xtok[:, (t % 2) * 1024:(t % 2 + 1) * 1024]
            rzb = ("xtokR", t % 2)
            vop((lambda yst, zbj, t: lambda v: v.tensor_scalar(
                zbj, yst, s_mean[:, t:t + 1], s_rstd[:, t:t + 1],
                ALU.subtract, ALU.mult))(yst, zbj, t),
                reads=[ry, (C, "ost", t), ("frs", t // 2), rzb],
                writes=[rzb, (A0, "zb", t)], tag="faznorm")
            pzb = pAb if t % 2 == 0 else pBb
            pzk = "pA.0" if t % 2 == 0 else "pB.0"
            for c in range(DC):
                tp(pzb[:, c * P:(c + 1) * P], zbj[:, c * P:(c + 1) * P],
                   c_identb[:],
                   reads=[(A0, "zb", t), rzb, "c_identb", pzk],
                   writes=[pzk], tag="faztp")
            zt_view = s_h[:, 0:16384].rearrange(
                "p (c q) -> p c q", c=DC)[:, :, t * P:(t + 1) * P]
            pt_view = pzb[:, 0:1024].rearrange("p (c q) -> p c q", c=DC)
            hbks = [("hb", 2 * c + t // 8) for c in range(DC)]
            vop((lambda zv, pv: lambda v: v.tensor_copy(zv, pv))
                (zt_view, pt_view),
                reads=[pzk] + hbks,
                writes=hbks + [(A0, "zt", t)], tag="fazte")

        if fuse:
            for g in range(NT // 2):
                combine_tile(2 * g)
                combine_tile(2 * g + 1)
                fuse_rstd(g)
                if 2 * g < 16:
                    fuse_zprep(2 * g)
                    fuse_zprep(2 * g + 1)
        else:
            for t in range(NT):
                combine_tile(t)

    def attn(src_hbm, dst_hbm, skey=None, dkey=None, fused=False):
        a_zt = s_h[:, 0:16384]
        a_at = a_zt
        a_q = s_h[:, 16384:32768]
        a_ki = s_w2[:, 0:8192]
        a_kc = s_w2[:, 8192:10240]
        a_pt = [s_w2[:, 10240:11264], s_w2[:, 11264:12288]]
        a_ju = s_w2[:, 12288:13312]
        a_wq = s_w2[:, 0:8192]
        a_wk = s_w2[:, 8192:16384]
        a_wv = s_w2[:, 16384:24576]
        a_wo = s_w2[:, 24576:32768]
        a_vi = s_xet[:, 0:8192]
        a_zb = [s_xtok[:, 0:1024], s_xtok[:, 1024:2048]]
        a_cp = s_h[:, 0:2048]
        a_vc = s_xtok[:, 2048:4096]
        a_h1 = s_xtok[:, 0:4096]
        a_w1c = s_w1w[0]
        a_w2c = s_w1w[1]
        w2f = s_w2[:].bitcast(F32)
        hf_ = s_h[:].bitcast(F32)

        def projT_sl(oc, lo, hi):
            if oc < 6:
                return w2f[:, oc * SEQ + lo:oc * SEQ + hi]
            return hf_[:, 8192 + (oc - 6) * SEQ + lo:8192 + (oc - 6) * SEQ + hi]

        def projT_keys(oc):
            if oc < 6:
                return [("w2B", 4 * oc + i) for i in range(4)]
            return [("hb", 16 + 4 * (oc - 6) + i) for i in range(4)]

        wqK = [("w2B", i) for i in range(0, 8)]
        wkK = [("w2B", i) for i in range(8, 16)]
        wvK = [("w2B", i) for i in range(16, 24)]
        woK = [("w2B", i) for i in range(24, 32)]
        kiK = wqK
        kcK = [("w2B", 8), ("w2B", 9)]
        ptK = [("w2B", 10), ("w2B", 11)]

        for c in range(DC):
            dma("gpsimd", a_w1c[:, c * 512:(c + 1) * 512],
                aw1b[c * P:(c + 1) * P, :],
                reads=[("w1b", 0, c), ("wbc", "aw1")],
                writes=[("w1b", 0, c)], tag="aw1l")
        for f1 in range(4):
            dma("scalar", a_w2c[:, f1 * 1024:(f1 + 1) * 1024],
                aw2b[f1 * P:(f1 + 1) * P, :],
                reads=[("w1b", 1, 2 * f1), ("w1b", 1, 2 * f1 + 1),
                       ("wbc", "aw2")],
                writes=[("w1b", 1, 2 * f1), ("w1b", 1, 2 * f1 + 1)], tag="aw2l")
        w1cK = [("w1b", 0, c) for c in range(DC)]
        w2cK = [("w1b", 1, i) for i in range(8)]

        for b in range(NB):
            C = f"a{b}"
            bsrc = src_hbm[b * SEQ:(b + 1) * SEQ, :]
            bdst = dst_hbm[b * SEQ:(b + 1) * SEQ, :]
            NTB = 16
            for wdr, base, wnm in ((wqb, 0, "wq"), (wkb, 8, "wk"),
                                   (wvb, 16, "wv"), (wob, 24, "wo")):
                for c in range(DC):
                    dma(("sync", "scalar", "gpsimd")[c % 3],
                        s_w2[:, (base + c) * 1024:(base + c + 1) * 1024],
                        wdr[c * P:(c + 1) * P, :],
                        reads=[("w2B", base + c), ("wbc", wnm)],
                        writes=[("w2B", base + c)], tag="awl")
            if not fused:
                ln_pass(C, bsrc, NTB, skey, soff=b * 16, deep=False)

            for t in ([] if (fused and b == 0) else range(NTB)):
                g = b * 16 + t
                xt = s_xt[t % 2]
                rx = ("xtb", t % 2)
                zb = a_zb[t % 2]
                rzb = ("xtokR", t % 2)
                dma("sync", xt[:], bsrc[t * P:(t + 1) * P, :],
                    reads=[rx] + ([("hbm", skey, b * 16 + t)] if skey else []),
                    writes=[rx], tag="aldx")
                mcol = g if fused else t
                rstdkey = ("frs", g // 2) if fused else (C, "rstd")
                vop((lambda mcol, xt, zb: lambda v: v.tensor_scalar(
                    zb, xt[:], s_mean[:, mcol:mcol + 1],
                    s_rstd[:, mcol:mcol + 1],
                    ALU.subtract, ALU.mult))(mcol, xt, zb),
                    reads=[rx, rstdkey, rzb],
                    writes=[rzb, (C, "zb", t)], tag="aznorm")
                pzb = pAb if t % 2 == 0 else pBb
                pzk = "pA.0" if t % 2 == 0 else "pB.0"
                for c in range(DC):
                    tp(pzb[:, c * P:(c + 1) * P], zb[:, c * P:(c + 1) * P],
                       c_identb[:],
                       reads=[(C, "zb", t), ("xtokR", t % 2), "c_identb", pzk],
                       writes=[pzk], tag="aztp")
                zt_view = a_zt.rearrange("p (c q) -> p c q", c=DC)[:, :, t * P:(t + 1) * P]
                pt_view = pzb[:, 0:1024].rearrange("p (c q) -> p c q", c=DC)
                hbks = [("hb", 2 * c + t // 8) for c in range(DC)]
                vop((lambda zv, pv: lambda v: v.tensor_copy(zv, pv))
                    (zt_view, pt_view),
                    reads=[pzk] + hbks,
                    writes=hbks + [(C, "zt", t)], tag="azte")
            ztall = [(C, "zt", t) for t in range(NTB)]
            ztinit = [(C, "zt", t) for t in range(8)]

            for qc in range(DC):
                for hf2 in range(2):
                    ps, k0, k1 = ((pA, "pA.0", "pA.1") if hf2 == 0
                                  else (pB, "pB.0", "pB.1"))
                    for d in range(DC):
                        lhs = a_wq[:, d * D + qc * P:d * D + (qc + 1) * P]
                        mm(ps[:, 0:512], lhs,
                           a_zt[:, d * SEQ + hf2 * 1024:d * SEQ + hf2 * 1024 + 512],
                           d == 0, False,
                           reads=[("w2B", d)] + ztall + [k0],
                           writes=[k0], tag="qm")
                        mm(ps[:, 512:1024], lhs,
                           a_zt[:, d * SEQ + hf2 * 1024 + 512:d * SEQ + (hf2 + 1) * 1024],
                           d == 0, d == DC - 1,
                           reads=[("w2B", d)] + ztall + [k1],
                           writes=[k1], tag="qm")
                    sop((lambda qc, hf2, ps: lambda s: s.activation(
                        a_q[:, qc * SEQ + hf2 * 1024:qc * SEQ + (hf2 + 1) * 1024],
                        ps[:], AF.Copy, bias=0.0, scale=0.125))(qc, hf2, ps),
                        reads=[k0, k1, ("hb", 16 + 2 * qc + hf2)],
                        writes=[("hb", 16 + 2 * qc + hf2), (C, "q", qc, hf2)],
                        tag="qe")
            qall = [(C, "q", qc, hf2) for qc in range(DC) for hf2 in range(2)]

            for kc_ in range(DC):
                for d in range(DC):
                    lhs = a_wk[:, d * D + kc_ * P:d * D + (kc_ + 1) * P]
                    mm(pA[:, 0:512], lhs, a_zt[:, d * SEQ:d * SEQ + 512],
                       d == 0, False, reads=[("w2B", 8 + d)] + ztinit + ["pA.0"],
                       writes=["pA.0"], tag="kim")
                    mm(pA[:, 512:1024], lhs,
                       a_zt[:, d * SEQ + 512:d * SEQ + 1024],
                       d == 0, d == DC - 1,
                       reads=[("w2B", 8 + d)] + ztinit + ["pA.1"],
                       writes=["pA.1"], tag="kim")
                sop((lambda kc_: lambda s: s.activation(
                    a_ki[:, kc_ * 1024:(kc_ + 1) * 1024], pA[:], AF.Copy))(kc_),
                    reads=["pA.0", "pA.1", ("w2B", kc_)],
                    writes=[("w2B", kc_), (C, "ki", kc_)], tag="kie")
            kiall = [(C, "ki", kc_) for kc_ in range(DC)]

            for tt in range(8):
                for d in range(DC):
                    lhs = a_zt[:, d * SEQ + tt * P:d * SEQ + (tt + 1) * P]
                    mm(pY[:, 0:512], lhs, a_wv[:, d * D:d * D + 512],
                       d == 0, False,
                       reads=[("w2B", 16 + d), (C, "zt", tt), "pY.0"],
                       writes=["pY.0"], tag="vim")
                    mm(pY[:, 512:1024], lhs, a_wv[:, d * D + 512:d * D + 1024],
                       d == 0, d == DC - 1,
                       reads=[("w2B", 16 + d), (C, "zt", tt), "pY.1"],
                       writes=["pY.1"], tag="vim")
                sop((lambda tt: lambda s: s.activation(
                    a_vi[:, tt * 1024:(tt + 1) * 1024], pY[:], AF.Copy))(tt),
                    reads=["pY.0", "pY.1", ("xetb", tt)],
                    writes=[("xetb", tt), (C, "vi", tt)], tag="vie")
            for f1 in range(4):
                for d in range(DC):
                    lhs = a_w1c[:, d * 512 + f1 * P:d * 512 + (f1 + 1) * P]
                    mm(pA[:, 0:512], lhs, a_zt[:, d * SEQ:d * SEQ + 512],
                       d == 0, False, reads=w1cK + ztinit + ["pA.0"],
                       writes=["pA.0"], tag="cmp1")
                    mm(pA[:, 512:1024], lhs,
                       a_zt[:, d * SEQ + 512:d * SEQ + 1024],
                       d == 0, d == DC - 1, reads=w1cK + ztinit + ["pA.1"],
                       writes=["pA.1"], tag="cmp1")
                sop((lambda f1: lambda s: s.activation(
                    a_h1[:, f1 * 1024:(f1 + 1) * 1024], pA[:], AF.Relu,
                    bias=s_ab1t[:, f1:f1 + 1], scale=1.0))(f1),
                    reads=["pA.0", "pA.1", "s_ab1t"] + [("xtokR", i) for i in range(4)],
                    writes=[("xtokR", i) for i in range(4)] + [(C, "h1", f1)],
                    tag="crelu")
            h1all = [(C, "h1", f1) for f1 in range(4)]
            for dc in range(DC):
                for f1 in range(4):
                    lhs = a_w2c[:, f1 * 1024 + dc * P:f1 * 1024 + (dc + 1) * P]
                    mm(pB[:, 0:512], lhs, a_h1[:, f1 * 1024:f1 * 1024 + 512],
                       f1 == 0, False,
                       reads=w2cK + h1all + [("xtokR", i) for i in range(4)] +
                             ["pB.0"],
                       writes=["pB.0"], tag="cmp2")
                    mm(pB[:, 512:1024], lhs,
                       a_h1[:, f1 * 1024 + 512:f1 * 1024 + 1024],
                       f1 == 0, f1 == 3,
                       reads=w2cK + h1all + [("xtokR", i) for i in range(4)] +
                             ["pB.1"],
                       writes=["pB.1"], tag="cmp2")
                sop((lambda dc: lambda s: s.activation(
                    s_scrB[:], pB[:], AF.Identity, bias=s_ab2t[:, dc:dc + 1],
                    scale=1.0))(dc),
                    reads=["pB.0", "pB.1", "s_ab2t", "scrB"],
                    writes=["scrB", (C, "cst", dc)], tag="cbias")
                cv4 = s_scrB[:].rearrange("p (g four) -> p g four", four=4)
                vop((lambda cv4: lambda v: v.tensor_add(
                    s_scrA[:, 0:256], cv4[:, :, 0], cv4[:, :, 1]))(cv4),
                    reads=[(C, "cst", dc), "scrA"],
                    writes=["scrA", (C, "cp1", dc)], tag="cp1")
                vop((lambda cv4: lambda v: v.tensor_add(
                    s_scrA[:, 256:512], cv4[:, :, 2], cv4[:, :, 3]))(cv4),
                    reads=[(C, "cst", dc), "scrA", (C, "cp1", dc)],
                    writes=["scrA", (C, "cp2", dc)], tag="cp2")
                vop((lambda: lambda v: v.tensor_add(
                    s_scrA[:, 0:256], s_scrA[:, 0:256], s_scrA[:, 256:512]))(),
                    reads=[(C, "cp1", dc), (C, "cp2", dc), "scrA"],
                    writes=["scrA", (C, "cp3", dc)], tag="cp3")
                sop((lambda dc: lambda s: s.activation(
                    a_cp[:, dc * 256:(dc + 1) * 256], s_scrA[:, 0:256],
                    AF.Copy, bias=0.0, scale=0.25))(dc),
                    reads=[(C, "cp3", dc), ("hb", dc // 4)],
                    writes=[("hb", dc // 4), (C, "cp", dc)], tag="cpool")
            cpall = [(C, "cp", dc) for dc in range(DC)]
            if DEBUG_TAPS and b == 0:
                dma("sync", dbg["cp"], a_cp, reads=cpall, writes=[("dbgw", "cp")],
                    tag="dbgcp")
                dma("sync", dbg["h1c"], a_h1, reads=h1all,
                    writes=[("dbgw", "h1c")], tag="dbgh1")

            for kc_ in range(DC):
                ps, psk = (pA, "pA") if kc_ < 4 else (pY, "pY")
                off = (kc_ % 4) * 256
                pk = psk + (".0" if off < 512 else ".1")
                for d in range(DC):
                    lhs = a_wk[:, d * D + kc_ * P:d * D + (kc_ + 1) * P]
                    mm(ps[:, off:off + 256], lhs, a_cp[:, d * 256:(d + 1) * 256],
                       d == 0, d == DC - 1,
                       reads=[("w2B", 8 + d), ("hb", 0), ("hb", 1)] +
                             cpall + [pk],
                       writes=[pk], tag="kcm")
            for kc_ in range(DC):
                ps, psk = (pA, "pA") if kc_ < 4 else (pY, "pY")
                off = (kc_ % 4) * 256
                pk = psk + (".0" if off < 512 else ".1")
                sop((lambda kc_, ps, off: lambda s: s.activation(
                    a_kc[:, kc_ * 256:(kc_ + 1) * 256], ps[:, off:off + 256],
                    AF.Copy))(kc_, ps, off),
                    reads=[pk, ("w2B", 8 + kc_ // 4)],
                    writes=[("w2B", 8 + kc_ // 4), (C, "kc", kc_)], tag="kce")
            for tt in range(2):
                for d in range(DC):
                    lhs = a_cp[:, d * 256 + tt * P:d * 256 + (tt + 1) * P]
                    mm(pY[:, 0:512], lhs, a_wv[:, d * D:d * D + 512],
                       d == 0, False,
                       reads=[("w2B", 16 + d), ("hb", 0), ("hb", 1)] +
                             cpall + ["pY.0"],
                       writes=["pY.0"], tag="vcm")
                    mm(pY[:, 512:1024], lhs, a_wv[:, d * D + 512:d * D + 1024],
                       d == 0, d == DC - 1,
                       reads=[("w2B", 16 + d), ("hb", 0), ("hb", 1)] +
                             cpall + ["pY.1"],
                       writes=["pY.1"], tag="vcm")
                sop((lambda tt: lambda s: s.activation(
                    a_vc[:, tt * 1024:(tt + 1) * 1024], pY[:], AF.Copy))(tt),
                    reads=["pY.0", "pY.1", ("xtokR", 2 + tt)],
                    writes=[("xtokR", 2 + tt), (C, "vc", tt)], tag="vce")
            viall = [(C, "vi", tt) for tt in range(8)]
            if DEBUG_TAPS and b == 0:
                dma("sync", dbg["vc"], a_vc,
                    reads=[(C, "vc", 0), (C, "vc", 1)],
                    writes=[("dbgw", "vc")], tag="dbgvc")
            vcall = [(C, "vc", tt) for tt in range(2)]

            if DEBUG_TAPS and b == 0:
                dma("sync", dbg["kc"], a_kc,
                    reads=[(C, "kc", i) for i in range(DC)],
                    writes=[("dbgw", "kc")], tag="dbgkc")
            kcall = [(C, "kc", kc_) for kc_ in range(DC)]
            def _bk(k, lo, hi):
                ks = []
                if lo < 512:
                    ks.append(k + ".0")
                if hi > 512:
                    ks.append(k + ".1")
                return ks

            for qi in range(16):
                causal = qi < 8
                kts = (qi + 1) if causal else 2
                kW = kts * P
                qoff = qi * P if causal else 1024 + (qi - 8) * P
                pstat, pstk = (pA, "pA") if (qi % 2 == 0) else (pY, "pY")
                pother, potk = (pY, "pY") if (qi % 2 == 0) else (pA, "pA")
                for h in range(NH):
                    psh, pshk = (pstat, pstk) if h % 2 == 0 else (pother, potk)
                    qsl = a_q[(h % 2) * 64:(h % 2) * 64 + 64,
                              (h // 2) * SEQ + qoff:(h // 2) * SEQ + qoff + P]
                    nblk = (kW + 511) // 512
                    for blk in range(nblk):
                        w0 = blk * 512
                        w1 = min(kW, w0 + 512)
                        if causal:
                            rhs = a_ki[(h % 2) * 64:(h % 2) * 64 + 64,
                                       (h // 2) * 1024 + w0:(h // 2) * 1024 + w1]
                            kkeys = kiall
                        else:
                            rhs = a_kc[(h % 2) * 64:(h % 2) * 64 + 64,
                                       (h // 2) * 256 + w0:(h // 2) * 256 + w1]
                            kkeys = kcall
                        mm(psh[:, w0:w1], qsl, rhs, True, True,
                           reads=qall + kkeys + _bk(pshk, w0, w1),
                           writes=_bk(pshk, w0, w1), tag="qk1")
                    if causal:
                        vop((lambda psh, qi: lambda v: v.tensor_add(
                            psh[:, qi * P:(qi + 1) * P],
                            psh[:, qi * P:(qi + 1) * P], c_diagm[:]))
                            (psh, qi),
                            reads=_bk(pshk, qi * P, (qi + 1) * P) + ["c_diagm"],
                            writes=_bk(pshk, qi * P, (qi + 1) * P), tag="dmask")
                    sop((lambda psh, kW, h: lambda s: s.activation(
                        a_ju[:, 0:kW], psh[:, 0:kW], AF.Exp, bias=0.0,
                        scale=1.0, accum_out=s_aS[:, h:h + 1]))(psh, kW, h),
                        reads=_bk(pshk, 0, kW) + [("w2B", 12)],
                        writes=_bk(pshk, 0, kW) + [("w2B", 12), (C, "S", qi, h)],
                        tag="sexp")
                vop(lambda v: v.reciprocal(s_scrA[:, 0:NH], s_aS[:]),
                    reads=[(C, "S", qi, h) for h in range(NH)] + ["scrA"],
                    writes=["scrA", (C, "recS", qi)], tag="recS")
                vop(lambda v: v.tensor_copy(s_aR[:], s_scrA[:, 0:NH]),
                    reads=[(C, "recS", qi), "scrA", "s_aR"],
                    writes=["s_aR", (C, "aRb", qi)], tag="aRb")
                tp(pCb[0:16, 0:P], s_aR[:], c_identb[:],
                   reads=[(C, "aRb", qi), "c_identb", "pC.0"], writes=["pC.0"],
                   tag="rtp")
                vop(lambda v: v.tensor_copy(s_rb16[:], pCb[0:16, 0:P]),
                    reads=["pC.0", "s_rb16"], writes=["s_rb16", (C, "rb16", qi)],
                    tag="rbe")
                dma("gpsimd", v_rrow, s_rb16[:],
                    reads=[(C, "rb16", qi), "rows"],
                    writes=["rows", (C, "rrow", qi)], tag="rrow")
                def stage1(h):
                    pq, pqk = (pB, "pB") if h % 2 == 0 else (pother, potk)
                    qsl = a_q[(h % 2) * 64:(h % 2) * 64 + 64,
                              (h // 2) * SEQ + qoff:(h // 2) * SEQ + qoff + P]
                    ptb = a_pt[h % 2]
                    ptbK = ("w2B", 10 + h % 2)
                    for kt in range(kts):
                        if causal:
                            ksl = a_ki[(h % 2) * 64:(h % 2) * 64 + 64,
                                       (h // 2) * 1024 + kt * P:(h // 2) * 1024 + (kt + 1) * P]
                            kkeys = kiall
                        else:
                            ksl = a_kc[(h % 2) * 64:(h % 2) * 64 + 64,
                                       (h // 2) * 256 + kt * P:(h // 2) * 256 + (kt + 1) * P]
                            kkeys = kcall
                        bk2 = pqk + (".0" if kt < 4 else ".1")
                        mm(pq[:, kt * P:(kt + 1) * P], ksl, qsl, True, True,
                           reads=qall + kkeys + [bk2], writes=[bk2], tag="qk2")
                    if causal:
                        bk2 = pqk + (".0" if qi < 4 else ".1")
                        vop((lambda pq, qi: lambda v: v.tensor_add(
                            pq[:, qi * P:(qi + 1) * P],
                            pq[:, qi * P:(qi + 1) * P], c_diagmT[:]))(pq, qi),
                            reads=[bk2, "c_diagmT"], writes=[bk2], tag="dmaskT")
                    bkst = _bk(pqk, 0, kW)
                    sop((lambda pq, ptb, kW: lambda s: s.activation(
                        ptb[:, 0:kW], pq[:, 0:kW], AF.Exp))(pq, ptb, kW),
                        reads=bkst + [ptbK],
                        writes=bkst + [ptbK, (C, "apt", qi, h)], tag="ptexp")

                def stage2(h):
                    pvsl = pC[0:64, 512:640] if h % 2 == 0 else pC[0:64, 0:128]
                    pvk = "pC.1" if h % 2 == 0 else "pC.0"
                    bbsl = pstat[0:64, 0:P] if h % 2 == 0 \
                        else pstat[0:64, 512:512 + P]
                    bbk = pstk + (".0" if h % 2 == 0 else ".1")
                    ptb = a_pt[h % 2]
                    for kt in range(kts):
                        if causal:
                            vsl = a_vi[:, kt * 1024 + h * 64:kt * 1024 + h * 64 + 64]
                            vkeys = viall
                        else:
                            vsl = a_vc[:, kt * 1024 + h * 64:kt * 1024 + h * 64 + 64]
                            vkeys = vcall
                        mm(pvsl, vsl, ptb[:, kt * P:(kt + 1) * P],
                           kt == 0, kt == kts - 1,
                           reads=vkeys + [(C, "apt", qi, h), pvk],
                           writes=[pvk], tag="pv")
                    mm(bbsl, c_onesrow_b[0:1, 0:64],
                       v_rrow[0:1, h * P:(h + 1) * P], True, True,
                       reads=["c_onesrow_b", (C, "rrow", qi), bbk],
                       writes=[bbk], tag="bk1")
                    vop((lambda bbsl: lambda v: v.tensor_copy(
                        s_bb[:], bbsl))(bbsl),
                        reads=[bbk, "s_bb"],
                        writes=["s_bb", (C, "bb", qi, h)], tag="bbe")
                    atblk = ("hb", ((h // 2) * 2048 + qoff) // 1024)
                    osl = a_at[(h % 2) * 64:(h % 2) * 64 + 64,
                               (h // 2) * SEQ + qoff:(h // 2) * SEQ + qoff + P]
                    vop((lambda osl, pvsl: lambda v: v.tensor_mul(
                        osl, pvsl, s_bb[:]))(osl, pvsl),
                        reads=[(C, "bb", qi, h), pvk, atblk],
                        writes=[atblk, pvk, (C, "at", qi, h)], tag="ate")

                stage1(0)
                for h in range(1, NH):
                    stage1(h)
                    stage2(h - 1)
                stage2(NH - 1)
            atall = [(C, "at", qi, h) for qi in range(16) for h in range(NH)]

            for oc in range(DC):
                for hf2 in range(2):
                    for dv in range(DC):
                        lhs = a_wo[:, dv * D + oc * P:dv * D + (oc + 1) * P]
                        atbk = ("hb", 2 * dv + hf2)
                        mm(pA[:, 0:512], lhs,
                           a_at[:, dv * SEQ + hf2 * 1024:dv * SEQ + hf2 * 1024 + 512],
                           dv == 0, False,
                           reads=[("w2B", 24 + dv), atbk] + atall + ["pA.0"],
                           writes=["pA.0"], tag="proj")
                        mm(pA[:, 512:1024], lhs,
                           a_at[:, dv * SEQ + hf2 * 1024 + 512:dv * SEQ + (hf2 + 1) * 1024],
                           dv == 0, dv == DC - 1,
                           reads=[("w2B", 24 + dv), atbk] + atall + ["pA.1"],
                           writes=["pA.1"], tag="proj")
                    prj = projT_sl(oc, hf2 * 1024, (hf2 + 1) * 1024)
                    vop((lambda prj: lambda v: v.tensor_copy(prj, pA[:]))(prj),
                        reads=["pA.0", "pA.1"] + projT_keys(oc),
                        writes=projT_keys(oc) + [(C, "prj", oc, hf2)],
                        tag="prje")
            prjall = [(C, "prj", oc, hf2) for oc in range(DC) for hf2 in range(2)]

            for t in range(NTB):
                po, pok = (pB, "pB") if t % 2 == 0 else (pY, "pY")
                pokk = [pok + ".0", pok + ".1"]
                for oc in range(DC):
                    rkey = (("w2B", 4 * oc + t // 4) if oc < 6
                            else ("hb", 16 + 4 * (oc - 6) + t // 4))
                    tp(po[:, oc * P:(oc + 1) * P],
                       projT_sl(oc, t * P, (t + 1) * P), c_identf[:],
                       reads=prjall + ["c_identf", rkey] + pokk, writes=pokk,
                       tag="otp")
                xr = w2f[:, 12288 + (t % 4) * 1024:12288 + (t % 4 + 1) * 1024]
                rx4 = [("w2B", 24 + 2 * (t % 4)), ("w2B", 25 + 2 * (t % 4))]
                dma("sync", xr, bsrc[t * P:(t + 1) * P, :],
                    reads=rx4 + ([("hbm", skey, b * 16 + t)] if skey else []),
                    writes=rx4, tag="aldx2")
                yst = s_yest[t % 2]
                ry = ("yestb", t % 2)
                vop((lambda xr, yst, po: lambda v: v.tensor_add(
                    yst[:], po[:], xr))(xr, yst, po),
                    reads=[ry] + pokk + rx4,
                    writes=[ry, (C, "ost", t)] + pokk + rx4,
                    tag="ores")
                dma("sync", bdst[t * P:(t + 1) * P, :], yst[:],
                    reads=[(C, "ost", t), ry],
                    writes=[ry, (C, "dst", t)] +
                    ([("hbm", dkey, b * 16 + t)] if dkey else []), tag="ostx")

    prelude()
    pr.barrier(tag="prelude_done")
    if "attn" in phases:
        cache_weights()
    cur, ck = x_in, None
    n = len(phases)
    for i, ph in enumerate(phases):
        if i == n - 1:
            dst, dk = out_d, None
        else:
            dst, dk = (xA, "xA") if i == 0 else (xB, "xB")
        if ph == "ffnA":
            ffn("A", cur, dst, ck, dk, fuse=("attn" in phases))
        elif ph == "attn":
            attn(cur, dst, ck, dk, fused=("ffnA" in phases))
        elif ph == "ffnC":
            ffn("C", cur, dst, ck, dk)
        cur, ck = dst, dk

    pr.finalize()
    pr.emit()
    return nc

_NC_CACHE = {}


def kernel(**inputs):
    hs = np.ascontiguousarray(inputs["L0_attn_hidden_states"], dtype=np.float32)
    ncore = 8
    key = PHASES
    if key not in _NC_CACHE:
        _NC_CACHE[key] = build_program(PHASES)
    nc = _NC_CACHE[key]
    common = dict(
        wq=np.ascontiguousarray(inputs["L0_attn_Wq"], dtype=np.float32),
        wk=np.ascontiguousarray(inputs["L0_attn_Wk"], dtype=np.float32),
        wv=np.ascontiguousarray(inputs["L0_attn_Wv"], dtype=np.float32),
        wo=np.ascontiguousarray(inputs["L0_attn_Wo"], dtype=np.float32),
        aw1=np.ascontiguousarray(inputs["L0_attn_W1"], dtype=np.float32),
        ab1=np.ascontiguousarray(inputs["L0_attn_b1"], dtype=np.float32).reshape(1, 512),
        aw2=np.ascontiguousarray(inputs["L0_attn_W2"], dtype=np.float32),
        ab2=np.ascontiguousarray(inputs["L0_attn_b2"], dtype=np.float32).reshape(1, D),
        wg=np.ascontiguousarray(inputs["L0_ffn_Wg"], dtype=np.float32),
        fw1=np.ascontiguousarray(inputs["L0_ffn_W1"], dtype=np.float32),
        fb1=np.ascontiguousarray(inputs["L0_ffn_b1"], dtype=np.float32),
        fw2=np.ascontiguousarray(inputs["L0_ffn_W2"], dtype=np.float32),
        fb2=np.ascontiguousarray(inputs["L0_ffn_b2"], dtype=np.float32),
    )
    in_maps = []
    for c in range(ncore):
        sel = np.zeros((NE, 2), dtype=np.float32)
        sel[:c, 0] = 1.0
        sel[:, 1] = 1.0
        m = dict(common)
        m["x"] = np.ascontiguousarray(hs[2 * c:2 * c + 2].reshape(NTOK, D))
        m["selrank"] = sel
        in_maps.append(m)
    res = run_bass_kernel_spmd(nc, in_maps, core_ids=list(range(ncore)))
    outs = [res.results[c]["out"].reshape(2, SEQ, D) for c in range(ncore)]
    return np.concatenate(outs, axis=0)
